# version 10
# baseline (speedup 1.0000x reference)
import contextlib
import numpy as np
import ml_dtypes
import concourse.bass as bass
import concourse.mybir as mybir
from concourse.bass_utils import run_bass_kernel_spmd

F32 = mybir.dt.float32
BF16 = mybir.dt.bfloat16
I32 = mybir.dt.int32
U8 = mybir.dt.uint8
AF = mybir.ActivationFunctionType
ALU = mybir.AluOpType
AX = mybir.AxisListType

D = 2048
T = 1024
NT = T // 128
ND = D // 128
EPS = 1e-5
NBF = np.dtype(ml_dtypes.bfloat16)


class Ctr:
    def __init__(self, sem):
        self.sem = sem
        self.n = 0

    def bump(self, k):
        self.n += k
        return self.n


class SharedCell(dict):
    def __getitem__(self, k):
        return dict.get(self, "x", 0)

    def __setitem__(self, k, v):
        dict.__setitem__(self, "x", v)


class Prog:

    ENGINES = ("sync", "gpsimd", "vector", "scalar", "tensor")

    def __init__(self, nc):
        self.nc = nc
        self.streams = {e: [] for e in self.ENGINES}
        self.waited = {e: {} for e in self.ENGINES}
        self.ctrs = {}
        self.max_waited = {}
        self._es = contextlib.ExitStack()

    def ctr(self, name):
        if name not in self.ctrs:
            sem = self._es.enter_context(self.nc.semaphore(name))
            self.ctrs[name] = Ctr(sem)
        return self.ctrs[name]

    def task(self, eng, f, waits=(), inc=None, inc_k=1):
        eff = []
        waits = list(waits) + list(getattr(self, "default_waits", ()))
        for (c, n) in waits:
            if n is None or n <= 0:
                continue
            key = id(c)
            if self.waited[eng].get(key, -1) >= n:
                continue
            self.waited[eng][key] = n
            eff.append((c.sem, n))
        tgt = inc.bump(inc_k) if inc is not None else None
        sem = inc.sem if inc is not None else None
        for (c, n) in waits:
            if n:
                ent = self.max_waited.setdefault(id(c), [c, 0])
                ent[1] = max(ent[1], n)

        def g(e, f=f, eff=tuple(eff), sem=sem, inc_k=inc_k):
            for (s, n) in eff:
                e.wait_ge(s, n)
            last = f(e)
            if sem is not None:
                assert last is not None, "task emit fn must return its last instruction"
                last.then_inc(sem, inc_k)
        self.streams[eng].append(g)
        return tgt

    def finish(self):
        for key, (c, mw) in self.max_waited.items():
            name = next((n for n, cc in self.ctrs.items() if cc is c), "?")
            assert mw <= c.n, f"wait {mw} exceeds total {c.n} on ctr '{name}'"
        with self.nc.Block() as blk:
            for eng in self.ENGINES:
                fs = self.streams[eng]
                if not fs:
                    continue

                def body(e, fs=fs):
                    for f in fs:
                        f(e)
                getattr(blk, eng)(body)
        self._es.close()


_UID = [0]


def _uid():
    _UID[0] += 1
    return f"t{_UID[0]}_"


def bfc(x):
    return np.ascontiguousarray(np.asarray(x), dtype=NBF)


def f32c(x):
    return np.ascontiguousarray(np.asarray(x), dtype=np.float32)


def u8c(x):
    x = np.ascontiguousarray(np.asarray(x))
    assert x.dtype == np.int32
    return np.ascontiguousarray(x.view(np.uint8)[..., ::4])



def emit_identity(p, ident, one1):
    cQ = p.ctr("q")
    p.task("gpsimd", lambda e: e.memset(ident[:], 0.0), inc=cQ)
    t0 = cQ.n
    p.task("gpsimd", lambda e: e.affine_select(
        out=ident[:], in_=ident[:], compare_op=ALU.not_equal,
        fill=1.0, base=0, pattern=[[-1, 128]], channel_multiplier=1),
        waits=[(cQ, t0)], inc=cQ)
    p.task("gpsimd", lambda e: e.memset(one1[:], 1.0), inc=cQ)
    return (cQ, cQ.n)


def emit_ln_chunk(p, x_ap, xn_ap, dump, cols, width=D, waits=(), prev_v=0,
                  out_ap=None, resid_ap=None):
    cV, cC = p.ctr("v"), p.ctr("c")
    hw = width // 2
    inv = 1.0 / width

    def f_sq(e):
        e.activation(out=dump[:, 0:hw], in_=x_ap[:, 0:hw], func=AF.Square,
                     accum_out=cols[:, 1:2])
        return e.activation(out=dump[:, hw:width], in_=x_ap[:, hw:width],
                            func=AF.Square, accum_out=cols[:, 7:8])
    p.task("scalar", f_sq, waits=list(waits) + [(cV, prev_v)], inc=cC)
    t_sq = cC.n

    def f_v(e):
        e.reduce_sum(out=cols[:, 0:1], in_=x_ap[:], axis=AX.X)
        e.tensor_tensor(out=cols[:, 1:2], in0=cols[:, 1:2], in1=cols[:, 7:8],
                        op=ALU.add)
        e.tensor_scalar(out=cols[:, 2:3], in0=cols[:, 0:1], scalar1=inv,
                        scalar2=None, op0=ALU.mult)
        e.tensor_scalar(out=cols[:, 3:4], in0=cols[:, 1:2], scalar1=inv,
                        scalar2=None, op0=ALU.mult)
        e.tensor_tensor(out=cols[:, 5:6], in0=cols[:, 2:3], in1=cols[:, 2:3],
                        op=ALU.mult)
        e.tensor_tensor(out=cols[:, 3:4], in0=cols[:, 3:4],
                        in1=cols[:, 5:6], op=ALU.subtract)
        return e.tensor_scalar(out=cols[:, 3:4], in0=cols[:, 3:4], scalar1=EPS,
                               scalar2=None, op0=ALU.add)
    p.task("vector", f_v, waits=list(waits) + [(cC, t_sq)], inc=cV)
    t_var = cV.n
    p.task("scalar", lambda e: e.activation(
        out=cols[:, 6:7], in_=cols[:, 3:4], func=AF.Sqrt),
        waits=[(cV, t_var)], inc=cC)
    t_std = cC.n

    if xn_ap is not None:
        def f_n(e):
            e.reciprocal(out=cols[:, 4:5], in_=cols[:, 6:7])
            return e.tensor_scalar(out=xn_ap[:], in0=x_ap[:],
                                   scalar1=cols[:, 2:3], scalar2=cols[:, 4:5],
                                   op0=ALU.subtract, op1=ALU.mult)
    else:
        def f_n(e):
            e.reciprocal(out=cols[:, 4:5], in_=cols[:, 6:7])
            e.tensor_scalar(out=x_ap[:], in0=x_ap[:],
                            scalar1=cols[:, 2:3], scalar2=cols[:, 4:5],
                            op0=ALU.subtract, op1=ALU.mult)
            return e.tensor_tensor(out=out_ap[:], in0=resid_ap[:], in1=x_ap[:],
                                   op=ALU.add)
    p.task("vector", f_n, waits=[(cC, t_std)], inc=cV)
    return cV.n


def emit_tm_to_fm(p, tiles, ident, psum, waits=(), t_ident=None, psum2=None):
    cP, cC, cV = p.ctr("pe"), p.ctr("c"), p.ctr("v")
    if not hasattr(p, "_tr_ev"):
        p._tr_ev = {}
    psums = [psum, psum2 if psum2 is not None else psum]
    last_c, last_v = 0, 0
    nb = (len(tiles) + 3) // 4
    for bi in range(nb):
        grp = tiles[bi * 4:bi * 4 + 4]
        pi = bi % 2
        ps_i = psums[pi]
        key = id(ps_i)

        def f_t(e, grp=grp, ps_i=ps_i):
            last = None
            for j, (src, dst) in enumerate(grp):
                last = e.matmul(out=ps_i[:, j * 128:(j + 1) * 128],
                                lhsT=src, rhs=ident[:], start=True, stop=True)
            return last
        wl = list(waits)
        prev = p._tr_ev.get(key)
        if prev is not None:
            wl.append(prev)
        if t_ident is not None:
            wl.append(t_ident)
        p.task("tensor", f_t, waits=wl, inc=cP)
        t_tr = cP.n

        def f_e(e, grp=grp, ps_i=ps_i):
            last = None
            for j, (src, dst) in enumerate(grp):
                last = e.copy(out=dst, in_=ps_i[:, j * 128:(j + 1) * 128])
            return last
        p.task("scalar", f_e, waits=[(cP, t_tr)], inc=cC)
        p._tr_ev[key] = (cC, cC.n)
        last_c = cC.n
    return last_c



H = 8192
NH = H // 128
GRP = 8
NG = NH // GRP


def build_l0f(nc, stage=99, p=None, io=None, gate=()):
    x_in = io["x_in"] if io is not None else nc.dram_tensor("x_in", [T, D], F32, kind="ExternalInput").ap()
    wup_q = io["wup_q"] if io is not None else nc.dram_tensor("wup_q", [H, D], U8, kind="ExternalInput").ap()
    wup_s = io["wup_s"] if io is not None else nc.dram_tensor("wup_s", [128, NH * 32], F32, kind="ExternalInput").ap()
    bup_c = io["bup_c"] if io is not None else nc.dram_tensor("bup_c", [128, NH], F32, kind="ExternalInput").ap()
    la_t = io["la_t"] if io is not None else nc.dram_tensor("la_t", [128, ND * 16], F32, kind="ExternalInput").ap()
    lora_ub = io["lora_ub"] if io is not None else nc.dram_tensor("lora_ub", [16, H], F32, kind="ExternalInput").ap()
    wdn_q = io["wdn_q"] if io is not None else nc.dram_tensor("wdn_q", [D, H], U8, kind="ExternalInput").ap()
    wdn_s = io["wdn_s"] if io is not None else nc.dram_tensor("wdn_s", [128, ND * 128], F32, kind="ExternalInput").ap()
    b_dn = io["b_dn"] if io is not None else nc.dram_tensor("b_dn", [1, D], F32, kind="ExternalInput").ap()
    da_t = io["da_t"] if io is not None else nc.dram_tensor("da_t", [128, NH * 16], F32, kind="ExternalInput").ap()
    lora_db = io["lora_db"] if io is not None else nc.dram_tensor("lora_db", [16, D], F32, kind="ExternalInput").ap()
    x_out = io["x_out"] if io is not None else nc.dram_tensor("x_out", [T, D], F32, kind="ExternalOutput").ap()

    es = contextlib.ExitStack()
    with es:
        U = _uid()
        sb = lambda name, shape, dt=BF16: es.enter_context(nc.sbuf_tensor(U + name, shape, dt))
        ps = lambda name, shape: es.enter_context(nc.psum_tensor(U + name, shape, F32))
        XR = [sb(f"XR{b}", [128, D], F32) for b in range(2)]
        xf = [sb(f"xf{c}", [128, T]) for c in range(ND)]
        y1 = [sb(f"y1{m}", [128, T]) for m in range(GRP)]
        wub = [sb(f"wub{b}", [128, ND * 128]) for b in range(2)]
        wuraw = [sb(f"wuraw{b}", [128, D], U8) for b in range(2)]
        wdraw = [sb(f"wdraw{b}", [128, GRP * 128], U8) for b in range(2)]
        wdb = [sb(f"wdb{b}", [128, GRP * 512]) for b in range(2)]
        wus = sb("wus", [128, NH * 32], F32)
        wds = sb("wds", [128, ND * 128], F32)
        la = sb("la", [128, ND * 16])
        lbg = [sb(f"lbg{b}", [16, GRP * 128]) for b in range(2)]
        da = sb("da", [128, NH * 16])
        db_t = sb("db", [16, D])
        bup = sb("bup", [128, NH], F32)
        bdn_r = sb("bdn_r", [1, D])
        l1 = sb("l1", [16, T])
        l1d_b = sb("l1d_b", [16, T])
        ident = sb("ident", [128, 128])
        one1 = sb("one1", [1, 128])
        dump = sb("dump", [128, D])
        cols = sb("cols", [128, 8], F32)
        wq_nat = sb("wq_nat", [128, 4 * 1024])
        p_tr = ps("p_tr", [128, 512])
        p_tr2 = ps("p_tr2", [128, 512])
        p_up = [ps(f"p_up{b}", [128, 512]) for b in range(2)]
        p_l1 = ps("p_l1", [16, T])
        p_dn = [ps(f"p_dn{b}", [128, 512]) for b in range(2)]
        xn_es = contextlib.ExitStack()
        xn = [xn_es.enter_context(nc.sbuf_tensor(U + f"xn{i}", [128, D], BF16))
              for i in range(NT)]

        own_prog = p is None
        if own_prog:
            p = Prog(nc)
        p.default_waits = list(gate)
        cV, cC, cP = (p.ctr(n) for n in ("v", "c", "pe"))
        cSc = p.ctr("sc")
        cG = p.ctr("g")
        cXR = [p.ctr("xr0"), p.ctr("xr1")]
        cWU = [p.ctr("wu0"), p.ctr("wu1")]
        cWD = [p.ctr("wd0"), p.ctr("wd1")]
        cLB = [p.ctr("lb0"), p.ctr("lb1")]
        cO = [p.ctr("o0"), p.ctr("o1")]

        t_id = emit_identity(p, ident, one1)
        p.task("sync", lambda e: e.dma_start(out=wus[:], in_=wup_s), inc=cSc, inc_k=16)
        p.task("sync", lambda e: e.dma_start(out=wds[:], in_=wdn_s), inc=cSc, inc_k=16)
        p.task("sync", lambda e: e.dma_start(out=bup[:], in_=bup_c), inc=cSc, inc_k=16)
        t_scales = cSc.n
        for (dst, src) in ((la, la_t), (da, da_t), (db_t, lora_db), (bdn_r, b_dn)):
            p.task("gpsimd", lambda e, dst=dst, src=src: e.dma_start(
                out=dst[:], in_=src), inc=cG, inc_k=16)
        t_const = cG.n

        prev_v = 0
        xr_free = {0: 0, 1: 0}
        for i in range(NT):
            b = i % 2
            p.task("sync", lambda e, i=i, b=b: e.dma_start(
                out=XR[b][:], in_=x_in[i * 128:(i + 1) * 128, :]),
                waits=[(cV, xr_free[b])], inc=cXR[b], inc_k=16)
            t_ld = cXR[b].n
            prev_v = emit_ln_chunk(p, XR[b], xn[i], dump, cols,
                                   waits=[(cXR[b], t_ld)], prev_v=prev_v)
            xr_free[b] = prev_v

        tiles = [(xn[i][:, c * 128:(c + 1) * 128], xf[c][:, i * 128:(i + 1) * 128])
                 for c in range(ND) for i in range(NT)]
        t_xf = emit_tm_to_fm(p, tiles, ident, p_tr,
                             waits=[(cV, prev_v)], t_ident=t_id, psum2=p_tr2)

        xn_es.close()
        Y2 = [sb(f"Y2{i}", [128, D], F32) for i in range(NT)]

        def f_l1(e):
            last = None
            for tb in range(2):
                for k in range(ND):
                    last = e.matmul(out=p_l1[:, tb * 512:(tb + 1) * 512],
                                    lhsT=la[:, k * 16:(k + 1) * 16],
                                    rhs=xf[k][:, tb * 512:(tb + 1) * 512],
                                    start=(k == 0), stop=(k == ND - 1))
            return last
        import os as _os
        if _os.environ.get("SKIP_L1") == "1":
            t_l1 = 0
        else:
            p.task("tensor", f_l1, waits=[(cC, t_xf), (cG, t_const)], inc=cP)
            p.task("scalar", lambda e: e.copy(out=l1[:], in_=p_l1[:]),
                   waits=[(cP, cP.n)], inc=cC)
            t_l1 = cC.n

        up_tiles_done = {}
        wuraw_free = {0: 0, 1: 0}
        wdraw_free = {0: 0, 1: 0}
        wqnat_free = {0: 0, 1: 0, "dn": 0}
        relu_done = {}
        y1_consumed = {}
        y2_evac = {}
        dn_psum_free = {0: None, 1: None}
        lbg_loaded = {}
        lbg_used = {0: 0, 1: 0}
        wub_used = {0: 0, 1: 0}
        wdb_used = {0: 0, 1: 0}

        def sched_up_weight(m):
            UPW_PART = int(os.environ.get("UPW_PART", "3"))
            b = m % 2
            p.task("sync", lambda e, m=m, b=b: e.dma_start(
                out=wuraw[b][:], in_=wup_q[m * 128:(m + 1) * 128, :]),
                waits=[(cV, wuraw_free[b])], inc=cWU[b], inc_k=16)
            t_dma = cWU[b].n
            base = (m % 2) * 2048
            for s4 in range(4):
                def f_dq(e, m=m, b=b, s4=s4, base=base):
                    sc = wus[:, m * 32 + s4 * 8: m * 32 + (s4 + 1) * 8]
                    return e.scalar_tensor_tensor(
                        out=wq_nat[:, base + s4 * 512:base + (s4 + 1) * 512].rearrange(
                            "p (blk f) -> p blk f", blk=8),
                        in0=wuraw[b][:, s4 * 512:(s4 + 1) * 512].rearrange(
                            "p (blk f) -> p blk f", blk=8),
                        scalar=7.5,
                        in1=sc.rearrange("p blk -> p blk ()").broadcast_to([128, 8, 64]),
                        op0=ALU.subtract, op1=ALU.mult)
                p.task("vector", f_dq,
                       waits=[(cWU[b], t_dma), (cSc, t_scales),
                              (cC, wqnat_free[m % 2])], inc=cV)
            wuraw_free[b] = cV.n
            t_dq = cV.n
            tiles = [(wq_nat[:, base + k * 128:base + (k + 1) * 128],
                      wub[b][:, k * 128:(k + 1) * 128]) for k in range(ND)]
            up_tiles_done[m] = emit_tm_to_fm(p, tiles, ident, p_tr,
                                             waits=[(cV, t_dq),
                                                    (cP, wub_used[b])],
                                             t_ident=t_id, psum2=p_tr2)
            wqnat_free[m % 2] = up_tiles_done[m]

        import os
        NO_LORA_MM = os.environ.get("NO_LORA_MM") == "1"
        NO_RELU_BIAS = os.environ.get("NO_RELU_BIAS") == "1"
        UP_MIN = os.environ.get("UP_MIN") == "1"
        RELU_COPY = os.environ.get("RELU_COPY") == "1"

        def sched_up_compute(m, g_idx, gi):
            b = m % 2
            for tb in range(2):
                MMVAR = os.environ.get("MMVAR", "")

                def f_mm(e, m=m, b=b, tb=tb, gi=gi):
                    nk = 1 if UP_MIN else ND
                    if MMVAR == "T1":
                        return e.matmul(out=p_up[b][:, 0:128], lhsT=ident[:],
                                        rhs=ident[:], start=True, stop=True)
                    if MMVAR == "T2":
                        return e.matmul(out=p_up[b][:, 0:128], lhsT=wub[b][:, 0:128],
                                        rhs=ident[:], start=True, stop=True)
                    if MMVAR == "T3":
                        return e.matmul(out=p_up[b][:, 0:512], lhsT=ident[:],
                                        rhs=xf[0][:, 0:512], start=True, stop=True)
                    if MMVAR == "T4":
                        return e.matmul(out=p_up[b][:, 0:256], lhsT=ident[:],
                                        rhs=xf[0][:, 0:256], start=True, stop=True)
                    if MMVAR == "T5":
                        return e.matmul(out=p_up[b][:, 0:512], lhsT=ident[:],
                                        rhs=wub[b][:, 0:512], start=True, stop=True)
                    if MMVAR == "T6":
                        return e.matmul(out=p_up[b][:, 0:384], lhsT=ident[:],
                                        rhs=xf[0][:, 0:384], start=True, stop=True)
                    if MMVAR == "T7":
                        return e.matmul(out=p_up[b][:, 0:500], lhsT=ident[:],
                                        rhs=xf[0][:, 0:500], start=True, stop=True)
                    if MMVAR == "T8":
                        return e.matmul(out=p_tr[:, 0:512], lhsT=ident[:],
                                        rhs=xf[0][:, 0:512], start=True, stop=True)
                    if MMVAR == "T10":
                        return e.matmul(out=p_up[1][:, 0:512], lhsT=ident[:],
                                        rhs=xf[0][:, 0:512], start=True, stop=True)
                    if MMVAR == "T11":
                        return e.matmul(out=p_dn[0][:, 0:512], lhsT=ident[:],
                                        rhs=xf[0][:, 0:512], start=True, stop=True)
                    for k in range(nk):
                        last = e.matmul(out=p_up[tb][:, 0:512],
                                 lhsT=wub[b][:, k * 128:(k + 1) * 128],
                                 rhs=xf[k][:, tb * 512:(tb + 1) * 512],
                                 start=(k == 0), stop=(NO_LORA_MM and k == nk - 1))
                    if NO_LORA_MM:
                        return last
                    mj = m - gi * GRP
                    return e.matmul(out=p_up[tb][:, 0:512],
                                    lhsT=lbg[gi % 2][:, mj * 128:(mj + 1) * 128],
                                    rhs=l1[:, tb * 512:(tb + 1) * 512],
                                    start=False, stop=True)
                p.task("tensor", f_mm,
                       waits=[(cC, max(up_tiles_done[m], t_l1,
                                       relu_done.get((m - 1, tb), 0))),
                              (cLB[gi % 2], lbg_loaded[gi])], inc=cP)
                t_mm = cP.n

                def f_relu(e, m=m, b=b, tb=tb, g_idx=g_idx):
                    if RELU_COPY:
                        return e.copy(out=y1[g_idx][:, tb * 512:(tb + 1) * 512],
                                      in_=p_up[tb][:, 0:512])
                    if NO_RELU_BIAS:
                        return e.activation(out=y1[g_idx][:, tb * 512:(tb + 1) * 512],
                                            in_=p_up[tb][:, 0:512], func=AF.Relu)
                    return e.activation(out=y1[g_idx][:, tb * 512:(tb + 1) * 512],
                                        in_=p_up[tb][:, 0:512], func=AF.Relu,
                                        bias=bup[:, m:m + 1])
                p.task("scalar", f_relu,
                       waits=[(cP, max(t_mm, y1_consumed.get(gi - 1, 0))),
                              (cSc, t_scales)], inc=cC)
                relu_done[(m, tb)] = cC.n
            wub_used[b] = cP.n

        def sched_l1d_mms(gi):
            def f(e, gi=gi):
                last = None
                for mj in range(GRP):
                    m = gi * GRP + mj
                    for tb in range(2):
                        last = e.matmul(out=p_l1[:, tb * 512:(tb + 1) * 512],
                                        lhsT=da[:, m * 16:(m + 1) * 16],
                                        rhs=y1[mj][:, tb * 512:(tb + 1) * 512],
                                        start=(gi == 0 and mj == 0),
                                        stop=(gi == NG - 1 and mj == GRP - 1),
                                        skip_group_check=True)
                return last
            p.task("tensor", f, waits=[(cC, max(relu_done[(gi * GRP + GRP - 1, 1)],
                                                t_l1)), (cG, t_const)], inc=cP)
            return cP.n

        def sched_dn_weight(gi, ds, buf):
            for dci in range(4):
                dc = ds * 4 + dci
                p.task("sync", lambda e, dc=dc, gi=gi, buf=buf: e.dma_start(
                    out=wdraw[buf][:],
                    in_=wdn_q[dc * 128:(dc + 1) * 128,
                              gi * GRP * 128:(gi + 1) * GRP * 128]),
                    waits=[(cV, wdraw_free[buf])], inc=cWD[buf], inc_k=16)
                t_dma = cWD[buf].n

                def f_dq(e, dc=dc, gi=gi, buf=buf, dci=dci):
                    sc = wds[:, dc * 128 + gi * 16: dc * 128 + (gi + 1) * 16]
                    return e.scalar_tensor_tensor(
                        out=wq_nat[:, dci * 1024:(dci + 1) * 1024].rearrange(
                            "p (blk f) -> p blk f", blk=16),
                        in0=wdraw[buf][:].rearrange("p (blk f) -> p blk f", blk=16),
                        scalar=7.5,
                        in1=sc.rearrange("p blk -> p blk ()").broadcast_to([128, 16, 64]),
                        op0=ALU.subtract, op1=ALU.mult)
                p.task("vector", f_dq, waits=[(cWD[buf], t_dma), (cSc, t_scales),
                                              (cC, max(wqnat_free[0], wqnat_free[1],
                                                       wqnat_free["dn"]))], inc=cV)
                wdraw_free[buf] = cV.n
            t_dq = cV.n
            tiles = [(wq_nat[:, dci * 1024 + mj * 128: dci * 1024 + (mj + 1) * 128],
                      wdb[buf][:, mj * 512 + dci * 128: mj * 512 + (dci + 1) * 128])
                     for mj in range(GRP) for dci in range(4)]
            t_w = emit_tm_to_fm(p, tiles, ident, p_tr,
                                waits=[(cV, t_dq), (cP, wdb_used[buf])],
                                t_ident=t_id, psum2=p_tr2)
            wqnat_free[0] = wqnat_free[1] = wqnat_free["dn"] = t_w
            return t_w

        def sched_dn_compute(gi, ds, buf, t_w, t_l1db):
            for tt in range(NT):
                pb = tt % 2

                def f_mm(e, gi=gi, ds=ds, buf=buf, tt=tt, pb=pb):
                    last = None
                    for mj in range(GRP):
                        last = e.matmul(out=p_dn[pb][:, 0:512],
                                        lhsT=y1[mj][:, tt * 128:(tt + 1) * 128],
                                        rhs=wdb[buf][:, mj * 512:(mj + 1) * 512],
                                        start=(mj == 0),
                                        stop=(mj == GRP - 1 and gi != NG - 1))
                    if gi == NG - 1:
                        e.matmul(out=p_dn[pb][:, 0:512],
                                 lhsT=l1d_b[:, tt * 128:(tt + 1) * 128],
                                 rhs=db_t[:, ds * 512:(ds + 1) * 512],
                                 start=False, stop=False)
                        last = e.matmul(out=p_dn[pb][:, 0:512],
                                        lhsT=one1[:],
                                        rhs=bdn_r[:, ds * 512:(ds + 1) * 512],
                                        start=False, stop=True)
                    return last
                wl = [(cC, max(t_w, relu_done[(gi * GRP + GRP - 1, 1)], t_l1db))]
                if dn_psum_free[pb] is not None:
                    wl.append(dn_psum_free[pb])
                prev = y2_evac.get((tt, ds))
                if prev is not None:
                    wl.append(prev)
                p.task("tensor", f_mm, waits=wl, inc=cP)
                t_mm = cP.n

                if gi == 0:
                    p.task("scalar", lambda e, tt=tt, ds=ds, pb=pb: e.copy(
                        out=Y2[tt][:, ds * 512:(ds + 1) * 512], in_=p_dn[pb][:, 0:512]),
                        waits=[(cP, t_mm)], inc=cC)
                    y2_evac[(tt, ds)] = (cC, cC.n)
                    dn_psum_free[pb] = (cC, cC.n)
                else:
                    def f_add(e, tt=tt, ds=ds, pb=pb):
                        return e.tensor_tensor(
                            out=Y2[tt][:, ds * 512:(ds + 1) * 512],
                            in0=Y2[tt][:, ds * 512:(ds + 1) * 512],
                            in1=p_dn[pb][:, 0:512], op=ALU.add)
                    wl2 = [(cP, t_mm)]
                    prev2 = y2_evac.get((tt, ds))
                    if prev2 is not None and prev2[0] is cC:
                        wl2.append(prev2)
                    p.task("vector", f_add, waits=wl2, inc=cV)
                    y2_evac[(tt, ds)] = (cV, cV.n)
                    dn_psum_free[pb] = (cV, cV.n)

        def dbg_store(bufs, n):
            cOg = p.ctr("og")
            for i, bp in enumerate(bufs[:n]):
                sh = bp.ap().shape if hasattr(bp, 'ap') else bp.shape
                p.task("gpsimd", lambda e, i=i, bp=bp, sh=sh: e.dma_start(
                    out=x_out[i * 128:i * 128 + sh[0], 0:sh[1]], in_=bp[:]),
                    waits=[(cV, cV.n), (cC, cC.n), (cP, cP.n)], inc=cOg, inc_k=16)
            p.task("sync", lambda e: e.nop(), waits=[(cOg, cOg.n)])
            if own_prog:
                p.finish()

        if stage == 3:
            dbg_store([l1], 1); return
        t_l1db = 0
        for gi in range(NG):
            bg = gi % 2
            p.task("gpsimd", lambda e, gi=gi, bg=bg: e.dma_start(
                out=lbg[bg][:], in_=lora_ub[:, gi * GRP * 128:(gi + 1) * GRP * 128]),
                waits=[(cP, lbg_used[bg])], inc=cLB[bg], inc_k=16)
            lbg_loaded[gi] = cLB[bg].n
            for mj in range(GRP):
                m = gi * GRP + mj
                if os.environ.get("SKIP_UPW") == "1":
                    up_tiles_done[m] = 0
                else:
                    sched_up_weight(m)
                if stage == 35 and m == 0:
                    dbg_store([wub[0]], 1); return
                sched_up_compute(m, mj, gi)
                if stage == 36 and m == 0:
                    import os as _os
                    dbg_store([wub[0] if _os.environ.get("DBG_WUB") == "1" else y1[0]], 1)
                    return
            lbg_used[bg] = cP.n
            t_lora_acc = sched_l1d_mms(gi)
            if gi == NG - 1:
                p.task("scalar", lambda e: e.copy(out=l1d_b[:], in_=p_l1[:]),
                       waits=[(cP, t_lora_acc)], inc=cC)
                t_l1db = cC.n
            for ds in range(4):
                buf = (gi * 4 + ds) % 2
                t_w = sched_dn_weight(gi, ds, buf)
                sched_dn_compute(gi, ds, buf, t_w, t_l1db)
            y1_consumed[gi] = cP.n
            if stage == 4 and gi == 0:
                dbg_store(y1, GRP); return
            if stage == 5 and gi == 0:
                dbg_store(Y2, NT); return

        if stage == 6:
            dbg_store(Y2, NT); return

        t_y2_all_c = cC.n
        t_y2_all_v = cV.n
        prev_v2 = 0
        xr_store = {0: 0, 1: 0}
        for i in range(NT):
            b = i % 2
            p.task("sync", lambda e, i=i, b=b: e.dma_start(
                out=XR[b][:], in_=x_in[i * 128:(i + 1) * 128, :]),
                waits=[(cO[b], xr_store[b]), (cV, t_y2_all_v if i < 2 else 0)],
                inc=cXR[b], inc_k=16)
            t_ld = cXR[b].n
            tv = emit_ln_chunk(p, Y2[i], None, dump, cols,
                               waits=[(cXR[b], t_ld), (cV, t_y2_all_v),
                                      (cC, t_y2_all_c)],
                               prev_v=prev_v2, out_ap=XR[b], resid_ap=XR[b])
            prev_v2 = tv
            p.task("sync", lambda e, i=i, b=b: e.dma_start(
                out=x_out[i * 128:(i + 1) * 128, :], in_=XR[b][:]),
                waits=[(cV, tv)], inc=cO[b], inc_k=16)
            xr_store[b] = cO[b].n
        p.task("sync", lambda e: e.nop(), waits=[(cO[0], cO[0].n), (cO[1], cO[1].n)])
        if own_prog:
            p.finish()
    return nc



def build_l1f(nc, stage=99, p=None, io=None, gate=()):
    x_in = io["x_in"] if io is not None else nc.dram_tensor("x_in", [T, D], F32, kind="ExternalInput").ap()
    gw_q = io["gw_q"] if io is not None else nc.dram_tensor("gw_q", [H, D], U8, kind="ExternalInput").ap()
    uw_q = io["uw_q"] if io is not None else nc.dram_tensor("uw_q", [H, D], U8, kind="ExternalInput").ap()
    dw_q = io["dw_q"] if io is not None else nc.dram_tensor("dw_q", [D, H], U8, kind="ExternalInput").ap()
    gs_c = io["gs_c"] if io is not None else nc.dram_tensor("gs_c", [128, NH], F32, kind="ExternalInput").ap()
    us_c = io["us_c"] if io is not None else nc.dram_tensor("us_c", [128, NH], F32, kind="ExternalInput").ap()
    ds_c = io["ds_c"] if io is not None else nc.dram_tensor("ds_c", [128, ND], F32, kind="ExternalInput").ap()
    x_out = io["x_out"] if io is not None else nc.dram_tensor("x_out", [T, D], F32, kind="ExternalOutput").ap()

    es = contextlib.ExitStack()
    with es:
        U = _uid()
        sb = lambda name, shape, dt=BF16: es.enter_context(nc.sbuf_tensor(U + name, shape, dt))
        ps = lambda name, shape: es.enter_context(nc.psum_tensor(U + name, shape, F32))
        XR = [sb(f"XR{b}", [128, D], F32) for b in range(2)]
        xf = [sb(f"xf{c}", [128, T]) for c in range(ND)]
        y1 = [sb(f"y1{m}", [128, T]) for m in range(GRP)]
        wgb = [sb(f"wgb{b}", [128, ND * 128]) for b in range(2)]
        wub = [sb(f"wub{b}", [128, ND * 128]) for b in range(2)]
        wgraw = [sb(f"wgraw{b}", [128, D], U8) for b in range(2)]
        wuraw = [sb(f"wuraw{b}", [128, D], U8) for b in range(2)]
        wdraw = [sb(f"wdraw{b}", [128, GRP * 128], U8) for b in range(2)]
        wdb = [sb(f"wdb{b}", [128, GRP * 512]) for b in range(2)]
        gsc = sb("gsc", [128, NH], F32)
        gzc = sb("gzc", [128, NH], F32)
        usc = sb("usc", [128, NH], F32)
        uzc = sb("uzc", [128, NH], F32)
        dsc = sb("dsc", [128, ND], F32)
        dzc = sb("dzc", [128, ND], F32)
        g_sb = [sb(f"g_sb{b}", [128, 512]) for b in range(2)]
        ident = sb("ident", [128, 128])
        one1 = sb("one1", [1, 128])
        dump = sb("dump", [128, D])
        cols = sb("cols", [128, 8], F32)
        wq_nat = sb("wq_nat", [128, 4 * 1024])
        p_tr = ps("p_tr", [128, 512])
        p_tr2 = ps("p_tr2", [128, 512])
        p_g = [ps(f"p_g{b}", [128, 512]) for b in range(2)]
        p_u = [ps(f"p_u{b}", [128, 512]) for b in range(2)]
        p_dn = [ps(f"p_dn{b}", [128, 512]) for b in range(2)]
        xn_es = contextlib.ExitStack()
        xn = [xn_es.enter_context(nc.sbuf_tensor(U + f"xn{i}", [128, D], BF16))
              for i in range(NT)]

        own_prog = p is None
        if own_prog:
            p = Prog(nc)
        p.default_waits = list(gate)
        cV, cC, cP = (p.ctr(n) for n in ("v", "c", "pe"))
        cSc = p.ctr("sc")
        cXR = [p.ctr("xr0"), p.ctr("xr1")]
        cWG = [p.ctr("wg0"), p.ctr("wg1")]
        cWU = [p.ctr("wu0"), p.ctr("wu1")]
        cWD = [p.ctr("wd0"), p.ctr("wd1")]
        cO = [p.ctr("o0"), p.ctr("o1")]

        t_id = emit_identity(p, ident, one1)
        p.task("sync", lambda e: e.dma_start(out=gsc[:], in_=gs_c), inc=cSc, inc_k=16)
        p.task("sync", lambda e: e.dma_start(out=usc[:], in_=us_c), inc=cSc, inc_k=16)
        p.task("sync", lambda e: e.dma_start(out=dsc[:], in_=ds_c), inc=cSc, inc_k=16)
        t_sc_dma = cSc.n

        def f_z(e):
            e.tensor_scalar(out=gzc[:], in0=gsc[:], scalar1=128.0, scalar2=None,
                            op0=ALU.mult)
            e.tensor_scalar(out=uzc[:], in0=usc[:], scalar1=128.0, scalar2=None,
                            op0=ALU.mult)
            return e.tensor_scalar(out=dzc[:], in0=dsc[:], scalar1=128.0, scalar2=None,
                                   op0=ALU.mult)
        p.task("vector", f_z, waits=[(cSc, t_sc_dma)], inc=cV)
        t_scales = cV.n

        prev_v = t_scales
        xr_free = {0: 0, 1: 0}
        for i in range(NT):
            b = i % 2
            p.task("sync", lambda e, i=i, b=b: e.dma_start(
                out=XR[b][:], in_=x_in[i * 128:(i + 1) * 128, :]),
                waits=[(cV, xr_free[b])], inc=cXR[b], inc_k=16)
            t_ld = cXR[b].n
            prev_v = emit_ln_chunk(p, XR[b], xn[i], dump, cols,
                                   waits=[(cXR[b], t_ld)], prev_v=prev_v)
            xr_free[b] = prev_v
        tiles = [(xn[i][:, c * 128:(c + 1) * 128], xf[c][:, i * 128:(i + 1) * 128])
                 for c in range(ND) for i in range(NT)]
        t_xf = emit_tm_to_fm(p, tiles, ident, p_tr, waits=[(cV, prev_v)], t_ident=t_id, psum2=p_tr2)
        xn_es.close()
        Y2 = [sb(f"Y2{i}", [128, D], F32) for i in range(NT)]

        up_tiles_done = {}
        raw_free = {id(wgraw): {0: 0, 1: 0}, id(wuraw): {0: 0, 1: 0}}
        wdraw_free = {0: 0, 1: 0}
        wqnat_free = {0: 0, 1: 0, "dn": 0}
        h_done = {}
        y1_consumed = {}
        y2_evac = {}
        dn_psum_free = {0: None, 1: None}
        wb_used = {0: 0, 1: 0}
        wdb_used = {0: 0, 1: 0}

        def sched_gu_weight(m, raw, wb, q_ap, s_sb, z_sb, ctr, hb):
            b = m % 2
            base = hb * 2048
            p.task("sync", lambda e, m=m, b=b, raw=raw, q_ap=q_ap: e.dma_start(
                out=raw[b][:], in_=q_ap[m * 128:(m + 1) * 128, :]),
                waits=[(cV, raw_free[id(raw)][b])], inc=ctr[b], inc_k=16)
            t_dma = ctr[b].n
            for s4 in range(4):
                def f_dq(e, m=m, b=b, s4=s4, raw=raw, s_sb=s_sb, z_sb=z_sb, base=base):
                    return e.tensor_scalar(
                        out=wq_nat[:, base + s4 * 512:base + (s4 + 1) * 512],
                        in0=raw[b][:, s4 * 512:(s4 + 1) * 512],
                        scalar1=s_sb[:, m:m + 1], scalar2=z_sb[:, m:m + 1],
                        op0=ALU.mult, op1=ALU.subtract)
                p.task("vector", f_dq,
                       waits=[(ctr[b], t_dma), (cV, t_scales),
                              (cC, wqnat_free[hb])], inc=cV)
            raw_free[id(raw)][b] = cV.n
            t_dq = cV.n
            tiles = [(wq_nat[:, base + k * 128:base + (k + 1) * 128],
                      wb[b][:, k * 128:(k + 1) * 128]) for k in range(ND)]
            t = emit_tm_to_fm(p, tiles, ident, p_tr,
                              waits=[(cV, t_dq), (cP, wb_used[b])], t_ident=t_id, psum2=p_tr2)
            wqnat_free[hb] = t
            return t

        def sched_gu_compute(m, g_idx, gi):
            b = m % 2
            for tb in range(2):
                def f_g(e, m=m, b=b, tb=tb):
                    last = None
                    for k in range(ND):
                        last = e.matmul(out=p_g[tb][:, 0:512],
                                        lhsT=wgb[b][:, k * 128:(k + 1) * 128],
                                        rhs=xf[k][:, tb * 512:(tb + 1) * 512],
                                        start=(k == 0), stop=(k == ND - 1))
                    return last
                p.task("tensor", f_g,
                       waits=[(cC, up_tiles_done[m]),
                              (cV, h_done.get((m - 1, tb), 0))],
                       inc=cP)
                t_gmm = cP.n

                def f_silu(e, tb=tb, b=b):
                    return e.activation(out=g_sb[tb][:], in_=p_g[tb][:, 0:512],
                                        func=AF.Silu)
                p.task("scalar", f_silu, waits=[(cP, t_gmm)], inc=cC)
                t_silu = cC.n

                def f_u(e, m=m, b=b, tb=tb):
                    last = None
                    for k in range(ND):
                        last = e.matmul(out=p_u[tb][:, 0:512],
                                        lhsT=wub[b][:, k * 128:(k + 1) * 128],
                                        rhs=xf[k][:, tb * 512:(tb + 1) * 512],
                                        start=(k == 0), stop=(k == ND - 1))
                    return last
                p.task("tensor", f_u,
                       waits=[(cC, up_tiles_done[m]),
                              (cV, h_done.get((m - 1, tb), 0))],
                       inc=cP)
                t_umm = cP.n

                def f_h(e, m=m, tb=tb, g_idx=g_idx):
                    return e.tensor_tensor(out=y1[g_idx][:, tb * 512:(tb + 1) * 512],
                                           in0=g_sb[tb][:], in1=p_u[tb][:, 0:512],
                                           op=ALU.mult)
                p.task("vector", f_h, waits=[(cP, t_umm), (cC, t_silu),
                                             (cP, y1_consumed.get(gi - 1, 0))], inc=cV)
                h_done[(m, tb)] = cV.n
            wb_used[b] = cP.n

        def sched_dn_weight(gi, ds, buf):
            for dci in range(4):
                dc = ds * 4 + dci
                p.task("sync", lambda e, dc=dc, gi=gi, buf=buf: e.dma_start(
                    out=wdraw[buf][:],
                    in_=dw_q[dc * 128:(dc + 1) * 128,
                             gi * GRP * 128:(gi + 1) * GRP * 128]),
                    waits=[(cV, wdraw_free[buf])], inc=cWD[buf], inc_k=16)
                t_dma = cWD[buf].n

                def f_dq(e, dc=dc, buf=buf, dci=dci):
                    return e.tensor_scalar(
                        out=wq_nat[:, dci * 1024:(dci + 1) * 1024],
                        in0=wdraw[buf][:],
                        scalar1=dsc[:, dc:dc + 1], scalar2=dzc[:, dc:dc + 1],
                        op0=ALU.mult, op1=ALU.subtract)
                p.task("vector", f_dq, waits=[(cWD[buf], t_dma), (cV, t_scales),
                                              (cC, max(wqnat_free[0], wqnat_free[1],
                                                       wqnat_free["dn"]))], inc=cV)
                wdraw_free[buf] = cV.n
            t_dq = cV.n
            tiles = [(wq_nat[:, dci * 1024 + mj * 128: dci * 1024 + (mj + 1) * 128],
                      wdb[buf][:, mj * 512 + dci * 128: mj * 512 + (dci + 1) * 128])
                     for mj in range(GRP) for dci in range(4)]
            t_w = emit_tm_to_fm(p, tiles, ident, p_tr,
                                waits=[(cV, t_dq), (cP, wdb_used[buf])], t_ident=t_id, psum2=p_tr2)
            wqnat_free[0] = wqnat_free[1] = wqnat_free["dn"] = t_w
            return t_w

        def sched_dn_compute(gi, ds, buf, t_w):
            for tt in range(NT):
                pb = tt % 2

                def f_mm(e, gi=gi, ds=ds, buf=buf, tt=tt, pb=pb):
                    last = None
                    for mj in range(GRP):
                        last = e.matmul(out=p_dn[pb][:, 0:512],
                                        lhsT=y1[mj][:, tt * 128:(tt + 1) * 128],
                                        rhs=wdb[buf][:, mj * 512:(mj + 1) * 512],
                                        start=(mj == 0), stop=(mj == GRP - 1))
                    return last
                wl = [(cC, t_w), (cV, h_done[(gi * GRP + GRP - 1, 1)])]
                if dn_psum_free[pb] is not None:
                    wl.append(dn_psum_free[pb])
                prev = y2_evac.get((tt, ds))
                if prev is not None:
                    wl.append(prev)
                p.task("tensor", f_mm, waits=wl, inc=cP)
                t_mm = cP.n

                if gi == 0:
                    p.task("scalar", lambda e, tt=tt, ds=ds, pb=pb: e.copy(
                        out=Y2[tt][:, ds * 512:(ds + 1) * 512], in_=p_dn[pb][:, 0:512]),
                        waits=[(cP, t_mm)], inc=cC)
                    y2_evac[(tt, ds)] = (cC, cC.n)
                    dn_psum_free[pb] = (cC, cC.n)
                else:
                    def f_add(e, tt=tt, ds=ds, pb=pb):
                        return e.tensor_tensor(
                            out=Y2[tt][:, ds * 512:(ds + 1) * 512],
                            in0=Y2[tt][:, ds * 512:(ds + 1) * 512],
                            in1=p_dn[pb][:, 0:512], op=ALU.add)
                    wl2 = [(cP, t_mm)]
                    prev2 = y2_evac.get((tt, ds))
                    if prev2 is not None and prev2[0] is cC:
                        wl2.append(prev2)
                    p.task("vector", f_add, waits=wl2, inc=cV)
                    y2_evac[(tt, ds)] = (cV, cV.n)
                    dn_psum_free[pb] = (cV, cV.n)
            wdb_used[buf] = cP.n

        import os as _os

        def dbg_store(bufs, n):
            cOg = p.ctr("og")
            for i, bp in enumerate(bufs[:n]):
                sh = bp.ap().shape if hasattr(bp, 'ap') else bp.shape
                p.task("gpsimd", lambda e, i=i, bp=bp, sh=sh: e.dma_start(
                    out=x_out[i * 128:i * 128 + sh[0], 0:sh[1]], in_=bp[:]),
                    waits=[(cV, cV.n), (cC, cC.n), (cP, cP.n)], inc=cOg, inc_k=16)
            p.task("sync", lambda e: e.nop(), waits=[(cOg, cOg.n)])
            if own_prog:
                p.finish()

        def gu_w(m):
            tg = sched_gu_weight(m, wgraw, wgb, gw_q, gsc, gzc, cWG, 0)
            tu = sched_gu_weight(m, wuraw, wub, uw_q, usc, uzc, cWU, 1)
            up_tiles_done[m] = max(tg, tu)

        gu_w(0)
        dn_w = {}
        for gi in range(NG):
            for mj in range(GRP):
                m = gi * GRP + mj
                if m + 1 < NH:
                    gu_w(m + 1)
                sched_gu_compute(m, mj, gi)
            dn_w[(gi, 0)] = sched_dn_weight(gi, 0, (gi * 4) % 2)
            for ds in range(4):
                buf = (gi * 4 + ds) % 2
                if ds + 1 < 4:
                    dn_w[(gi, ds + 1)] = sched_dn_weight(gi, ds + 1, (gi * 4 + ds + 1) % 2)
                sched_dn_compute(gi, ds, buf, dn_w[(gi, ds)])
            y1_consumed[gi] = cP.n

        t_y2_all_c = cC.n
        t_y2_all_v = cV.n
        prev_v2 = 0
        xr_store = {0: 0, 1: 0}
        for i in range(NT):
            b = i % 2
            p.task("sync", lambda e, i=i, b=b: e.dma_start(
                out=XR[b][:], in_=x_in[i * 128:(i + 1) * 128, :]),
                waits=[(cO[b], xr_store[b]), (cV, t_y2_all_v if i < 2 else 0)],
                inc=cXR[b], inc_k=16)
            t_ld = cXR[b].n
            tv = emit_ln_chunk(p, Y2[i], None, dump, cols,
                               waits=[(cXR[b], t_ld), (cV, t_y2_all_v),
                                      (cC, t_y2_all_c)],
                               prev_v=prev_v2, out_ap=XR[b], resid_ap=XR[b])
            prev_v2 = tv
            p.task("sync", lambda e, i=i, b=b: e.dma_start(
                out=x_out[i * 128:(i + 1) * 128, :], in_=XR[b][:]),
                waits=[(cV, tv)], inc=cO[b], inc_k=16)
            xr_store[b] = cO[b].n
        p.task("sync", lambda e: e.nop(), waits=[(cO[0], cO[0].n), (cO[1], cO[1].n)])
        if own_prog:
            p.finish()
    return nc


NHEAD = 16
HSLAB = 4
NSLAB = NHEAD // HSLAB


def build_l0a(nc, stage=99, p=None, io=None, gate=()):
    x_own = io["x_own"] if io is not None else nc.dram_tensor("x_own", [T, D], F32, kind="ExternalInput").ap()
    x_pre = io["x_pre"] if io is not None else nc.dram_tensor("x_pre", [T, D], F32, kind="ExternalInput").ap()
    wq = io["wq"] if io is not None else nc.dram_tensor("wq", [D, D], F32, kind="ExternalInput").ap()
    wk = io["wk"] if io is not None else nc.dram_tensor("wk", [D, D], F32, kind="ExternalInput").ap()
    wv = io["wv"] if io is not None else nc.dram_tensor("wv", [D, D], F32, kind="ExternalInput").ap()
    wo = io["wo"] if io is not None else nc.dram_tensor("wo", [D, D], F32, kind="ExternalInput").ap()
    wg1 = io["wg1"] if io is not None else nc.dram_tensor("wg1", [D, 128], F32, kind="ExternalInput").ap()
    wg2 = io["wg2"] if io is not None else nc.dram_tensor("wg2", [128, D], F32, kind="ExternalInput").ap()
    nw_c = io["nw_c"] if io is not None else nc.dram_tensor("nw_c", [128, NHEAD], F32, kind="ExternalInput").ap()
    ld_r = io["ld_r"] if io is not None else nc.dram_tensor("ld_r", [1, NHEAD], F32, kind="ExternalInput").ap()
    x_out = io["x_out"] if io is not None else nc.dram_tensor("x_out", [T, D], F32, kind="ExternalOutput").ap()

    es = contextlib.ExitStack()
    with es:
        U = _uid()
        sb = lambda name, shape, dt=BF16: es.enter_context(nc.sbuf_tensor(U + name, shape, dt))
        ps = lambda name, shape, dt=F32: es.enter_context(nc.psum_tensor(U + name, shape, dt))
        XR = [sb(f"XR{b}", [128, D], F32) for b in range(2)]
        xf = [sb(f"xf{c}", [128, T]) for c in range(ND)]
        O_tm = [sb(f"O{c}", [128, D]) for c in range(NT)]
        S_sb = sb("S_sb", [128, NHEAD * 128], F32)
        S_bf = sb("S_bf", [128, NHEAD * 128])
        DTm = sb("DTm", [128, NHEAD * 128])
        ldc = sb("ldc", [128, NHEAD], F32)
        qdec = sb("qdec", [128, NHEAD], F32)
        kdec = sb("kdec", [128, NHEAD], F32)
        lam = sb("lam", [128, NHEAD], F32)
        idxc = sb("idxc", [128, 2], F32)
        diff = sb("diff", [128, 128], F32)
        nwc = sb("nwc", [128, NHEAD], F32)
        wg2_sb = sb("wg2_sb", [128, D])
        wg1_sb = sb("wg1_sb", [128, ND * 128])
        g1 = sb("g1", [128, T])
        ident = sb("ident", [128, 128])
        one1 = sb("one1", [1, 128])
        ld_row = sb("ld_row", [1, NHEAD], F32)
        one1f = sb("one1f", [1, 128], F32)
        dump = sb("dump", [128, D])
        cols = sb("cols", [128, 8], F32)
        gate_sb = sb("gate_sb", [128, T])
        oc_f32 = sb("oc_f32", [128, 128], F32)
        kfm = sb("kfm", [128, 128])
        ksc = sb("ksc", [128, 128])
        scb = sb("scb", [128, 128])
        xnc = sb("xnc", [128, D])
        means = sb("means", [128, NHEAD], F32)
        rst = sb("rst", [128, NHEAD], F32)
        rst2 = sb("rst2", [128, NHEAD], F32)
        p_tr = ps("p_tr", [128, 512])
        p_tr2 = ps("p_tr2", [128, 512])
        p_pj = [ps(f"p_pj{b}", [128, 512]) for b in range(2)]
        p_sc = ps("p_sc", [128, 512])
        p_oi = ps("p_oi", [128, 512])
        p_oe = ps("p_oe", [128, 512])
        p_st = ps("p_st", [128, 512])

        own_prog = p is None
        if own_prog:
            p = Prog(nc)
        p.default_waits = list(gate)
        cV, cC, cP = (p.ctr(n) for n in ("v", "c", "pe"))
        cQ = p.ctr("q")
        cG = p.ctr("g")
        cXR = [p.ctr("xr0"), p.ctr("xr1")]
        cW = [p.ctr("w0"), p.ctr("w1")]
        cO = [p.ctr("o0"), p.ctr("o1")]

        t_id = emit_identity(p, ident, one1)
        p.task("gpsimd", lambda e: e.memset(one1f[:], 1.0), inc=p.ctr("q"))
        t_id = (t_id[0], t_id[0].n)

        cSc = p.ctr("sc")
        cLd = p.ctr("ldr")
        p.task("sync", lambda e: e.dma_start(out=ld_row[:], in_=ld_r), inc=cLd, inc_k=16)
        t_ld = cLd.n

        def f_iota(e):
            e.iota(idxc[:, 0:1], [[1, 1]], base=0, channel_multiplier=1,
                   allow_small_or_imprecise_dtypes=True)
            return e.iota(diff[:], [[1, 128]], base=0, channel_multiplier=-1,
                          allow_small_or_imprecise_dtypes=True)
        p.task("gpsimd", f_iota, inc=cQ)
        t_iota0 = cQ.n
        p.task("gpsimd", lambda e: e.affine_select(
            out=diff[:], in_=diff[:], compare_op=ALU.is_ge, fill=0.0, base=0,
            pattern=[[1, 128]], channel_multiplier=-1),
            waits=[(cQ, t_iota0)], inc=cQ)
        t_iota = cQ.n

        def f_ldc(e):
            return e.matmul(out=p_tr[:, 0:NHEAD], lhsT=one1f[:], rhs=ld_row[:],
                            start=True, stop=True)
        p.task("tensor", f_ldc, waits=[(cLd, t_ld), (cQ, t_id[1])], inc=cP)
        p.task("scalar", lambda e: e.copy(out=ldc[:], in_=p_tr[:, 0:NHEAD]),
               waits=[(cP, cP.n)], inc=cC)
        t_ldc = cC.n

        def f_idx(e):
            return e.tensor_scalar(out=idxc[:, 1:2], in0=idxc[:, 0:1], scalar1=1.0,
                                   scalar2=None, op0=ALU.add)
        p.task("vector", f_idx, waits=[(cQ, t_iota)], inc=cV)

        def f_idx2(e):
            return e.tensor_scalar(out=idxc[:, 0:1], in0=idxc[:, 0:1], scalar1=127.0,
                                   scalar2=-1.0, op0=ALU.subtract, op1=ALU.mult)
        p.task("vector", f_idx2, waits=[(cV, cV.n)], inc=cV)
        t_idx = cV.n

        def f_dec(e):
            e.activation(out=qdec[:], in_=ldc[:], func=AF.Exp, scale=idxc[:, 1:2])
            e.activation(out=kdec[:], in_=ldc[:], func=AF.Exp, scale=idxc[:, 0:1])
            return e.activation(out=lam[:], in_=ldc[:], func=AF.Exp, scale=128.0)
        p.task("scalar", f_dec, waits=[(cC, t_ldc), (cV, t_idx)], inc=cC)
        t_dec = cC.n

        for h in range(NHEAD):
            p.task("scalar", lambda e, h=h: e.activation(
                out=DTm[:, h * 128:(h + 1) * 128], in_=diff[:], func=AF.Exp,
                scale=ldc[:, h:h + 1]), waits=[(cC, t_ldc), (cQ, t_iota)], inc=cC)
            t_e = cC.n
            p.task("gpsimd", lambda e, h=h: e.affine_select(
                out=DTm[:, h * 128:(h + 1) * 128], in_=DTm[:, h * 128:(h + 1) * 128],
                compare_op=ALU.is_ge, fill=0.0, base=0, pattern=[[1, 128]],
                channel_multiplier=-1), waits=[(cC, t_e)], inc=cQ)
        t_dtm = cQ.n

        p.task("gpsimd", lambda e: e.dma_start(out=wg2_sb[:], in_=wg2), inc=cG, inc_k=16)
        p.task("gpsimd", lambda e: e.dma_start(
            out=wg1_sb[:].rearrange("p (k r) -> p k r", k=ND),
            in_=wg1.rearrange("(k p) r -> p k r", p=128)), inc=cG, inc_k=16)
        t_wg = cG.n
        p.task("sync", lambda e: e.dma_start(out=nwc[:], in_=nw_c), inc=cSc, inc_k=16)
        t_nwb = 0
        t_nwc = cSc.n

        def f_s0(e):
            e.memset(S_sb[:], 0.0)
            return e.memset(S_bf[:], 0.0)
        p.task("gpsimd", lambda e: e.memset(S_sb[:], 0.0), inc=cQ)
        p.task("gpsimd", lambda e: e.memset(S_bf[:], 0.0), inc=cQ)
        t_s0 = cQ.n

        wbuf = [sb(f"wbuf{b}", [128, 16 * 512]) for b in range(2)]
        kq_sl = [sb(f"kq{b}", [128, HSLAB * 1024]) for b in range(2)]
        K_sl = [sb(f"Ksl{b}", [128, NT * 512]) for b in range(2)]
        V_sl = [sb(f"Vsl{b}", [128, NT * 512]) for b in range(2)]

        wbuf_used = {0: 0, 1: 0}
        kv_used = {0: (cP, 0), 1: (cP, 0)}
        s_ready = {}
        ln_prev = [0]

        def emit_half_ln(x_ap_dram, out_xf=True):
            prev_v = ln_prev[0]
            for i in range(NT):
                b = i % 2
                p.task("sync", lambda e, i=i, b=b: e.dma_start(
                    out=XR[b][:], in_=x_ap_dram[i * 128:(i + 1) * 128, :]),
                    waits=[(cV, xr_free_ln[b])], inc=cXR[b], inc_k=16)
                t_ldx = cXR[b].n
                prev_v = emit_ln_chunk(p, XR[b], xnc, dump, cols,
                                       waits=[(cXR[b], t_ldx), xnc_free[0]],
                                       prev_v=prev_v)
                xr_free_ln[b] = prev_v
                tiles = [(xnc[:, c * 128:(c + 1) * 128], xf[c][:, i * 128:(i + 1) * 128])
                         for c in range(ND)]
                t_tr = emit_tm_to_fm(p, tiles, ident, p_tr,
                                     waits=[(cV, prev_v)], t_ident=t_id, psum2=p_tr2)
                xnc_free[0] = (cP, p.ctrs["pe"].n)
                prev_v = prev_v
                ln_gate[0] = t_tr
            ln_prev[0] = prev_v
            return ln_gate[0]

        xr_free_ln = {0: 0, 1: 0}
        xnc_free = [(cP, 0)]
        ln_gate = [0]


        def sched_proj_slab(ms, xf_src, own):
            b = ms % 2
            col0 = ms * 512
            projs = [(wk, 0), (wv, 1)] + ([(wq, 2)] if own else [])
            for (w_ap, wi) in projs:
                p.task("gpsimd", lambda e, w_ap=w_ap, wi=wi, col0=col0, b=b: e.dma_start(
                    out=wbuf[b][:, wi * 512 * 0:16 * 512].rearrange(
                        "p (k f) -> p k f", k=16) if False else
                    wbuf[b][:].rearrange("p (k f) -> p k f", k=16),
                    in_=w_ap[:, col0:col0 + 512].rearrange("(k p) f -> p k f", p=128)),
                    waits=[(cP, wbuf_used[b])], inc=cW[b], inc_k=16)
                t_w = cW[b].n
                if wi == 0 or wi == 1:
                    dst = K_sl[b] if wi == 0 else V_sl[b]
                    for c in range(NT):
                        def f_mm(e, c=c, b=b, t_w=t_w):
                            last = None
                            for k in range(ND):
                                last = e.matmul(out=p_pj[c % 2][:, 0:512],
                                                lhsT=xf_src[k][:, c * 128:(c + 1) * 128],
                                                rhs=wbuf[b][:, k * 512:(k + 1) * 512],
                                                start=(k == 0), stop=(k == ND - 1))
                            return last
                        p.task("tensor", f_mm,
                               waits=[(cW[b], t_w),
                                      (cC, max(pj_evac[c % 2], ln_gate[0]))],
                               inc=cP)
                        t_mm = cP.n
                        p.task("scalar", lambda e, c=c, b=b, dst=dst: e.activation(
                            out=dst[:, c * 512:(c + 1) * 512], in_=p_pj[c % 2][:, 0:512],
                            func=AF.Silu), waits=[(cP, t_mm)], inc=cC)
                        pj_evac[c % 2] = cC.n
                else:
                    for hh in range(HSLAB):
                        for tb in range(2):
                            def f_mm(e, hh=hh, tb=tb, b=b, col0=col0, t_w=t_w):
                                last = None
                                for k in range(ND):
                                    last = e.matmul(
                                        out=p_pj[tb % 2][:, 0:512],
                                        lhsT=wbuf[b][:, k * 512 + hh * 128:
                                                     k * 512 + (hh + 1) * 128],
                                        rhs=xf_src[k][:, tb * 512:(tb + 1) * 512],
                                        start=(k == 0), stop=(k == ND - 1))
                                return last
                            p.task("tensor", f_mm,
                                   waits=[(cW[b], t_w),
                                          (cC, max(pj_evac[tb % 2], ln_gate[0]))],
                                   inc=cP)
                            t_mm = cP.n
                            p.task("scalar", lambda e, hh=hh, tb=tb, b=b: e.activation(
                                out=kq_sl[b][:, hh * 1024 + tb * 512:
                                             hh * 1024 + (tb + 1) * 512],
                                in_=p_pj[tb % 2][:, 0:512], func=AF.Silu),
                                waits=[(cP, t_mm)], inc=cC)
                            pj_evac[tb % 2] = cC.n
                wbuf_used[b] = cP.n
            return cC.n

        pj_evac = {0: 0, 1: 0}

        def sched_state_chunk(ms, b, c, t_kv, own):
            for hh in range(HSLAB):
                h = ms * HSLAB + hh
                if own:
                    def f_tr(e, b=b, c=c, hh=hh):
                        return e.matmul(out=p_tr[:, 0:128],
                                        lhsT=K_sl[b][:, c * 512 + hh * 128:
                                                     c * 512 + (hh + 1) * 128],
                                        rhs=ident[:], start=True, stop=True)
                    p.task("tensor", f_tr, waits=[(cC, max(t_kv, kfm_free[0]))], inc=cP)
                    t_trm = cP.n
                    p.task("scalar", lambda e: e.copy(out=kfm[:], in_=p_tr[:, 0:128]),
                           waits=[(cP, t_trm)], inc=cC)
                    t_kfm = cC.n
                    def f_sc(e, b=b, c=c, hh=hh):
                        return e.matmul(out=p_sc[:, 0:128], lhsT=kfm[:],
                                        rhs=kq_sl[b][:, hh * 1024 + c * 128:
                                                     hh * 1024 + (c + 1) * 128],
                                        start=True, stop=True)
                    p.task("tensor", f_sc, waits=[(cC, t_kfm), (cV, scb_free[0])],
                           inc=cP)
                    t_scm = cP.n
                    def f_msk(e, h=h):
                        return e.tensor_tensor(out=scb[:], in0=p_sc[:, 0:128],
                                               in1=DTm[:, h * 128:(h + 1) * 128],
                                               op=ALU.mult)
                    p.task("vector", f_msk, waits=[(cP, t_scm), (cQ, t_dtm)], inc=cV)
                    t_msk = cV.n
                    scb_free[0] = t_msk
                    def f_oi(e, b=b, c=c, hh=hh):
                        return e.matmul(out=p_oi[:, 0:128], lhsT=scb[:],
                                        rhs=V_sl[b][:, c * 512 + hh * 128:
                                                    c * 512 + (hh + 1) * 128],
                                        start=True, stop=True)
                    p.task("tensor", f_oi, waits=[(cV, t_msk), (cV, oi_free[0])], inc=cP)
                    t_oim = cP.n
                    def f_oe(e, b=b, c=c, hh=hh, h=h):
                        return e.matmul(out=p_oe[:, 0:128],
                                        lhsT=kq_sl[b][:, hh * 1024 + c * 128:
                                                      hh * 1024 + (c + 1) * 128],
                                        rhs=S_bf[:, h * 128:(h + 1) * 128],
                                        start=True, stop=True)
                    p.task("tensor", f_oe,
                           waits=[(cV, max(s_ready.get(h, 0), oi_free[0]))], inc=cP)
                    t_oem = cP.n
                    def f_cmb1(e, h=h):
                        return e.tensor_scalar(out=oc_f32[:], in0=p_oe[:, 0:128],
                                               scalar1=qdec[:, h:h + 1], scalar2=None,
                                               op0=ALU.mult)
                    p.task("vector", f_cmb1, waits=[(cP, max(t_oem, t_oim)),
                                                    (cC, t_dec)], inc=cV)

                    def f_cmb2(e, c=c, h=h):
                        return e.tensor_tensor(out=O_tm[c][:, h * 128:(h + 1) * 128],
                                               in0=oc_f32[:], in1=p_oi[:, 0:128],
                                               op=ALU.add)
                    p.task("vector", f_cmb2, waits=[(cV, cV.n)], inc=cV)
                    oi_free[0] = cV.n
                def f_ks(e, b=b, c=c, hh=hh, h=h):
                    return e.tensor_scalar(out=ksc[:],
                                           in0=K_sl[b][:, c * 512 + hh * 128:
                                                       c * 512 + (hh + 1) * 128],
                                           scalar1=kdec[:, h:h + 1], scalar2=None,
                                           op0=ALU.mult)
                p.task("vector", f_ks, waits=[(cC, max(t_kv, t_dec)),
                                              (cP, ksc_free[0])], inc=cV)
                t_ks = cV.n

                def f_st(e, b=b, c=c, hh=hh):
                    return e.matmul(out=p_st[:, 0:128], lhsT=ksc[:],
                                    rhs=V_sl[b][:, c * 512 + hh * 128:
                                                c * 512 + (hh + 1) * 128],
                                    start=True, stop=True)
                p.task("tensor", f_st, waits=[(cV, max(t_ks, st_free[0]))], inc=cP)
                t_stm = cP.n
                ksc_free[0] = cP.n

                def f_su(e, h=h):
                    e.tensor_scalar(out=S_sb[:, h * 128:(h + 1) * 128],
                                    in0=S_sb[:, h * 128:(h + 1) * 128],
                                    scalar1=lam[:, h:h + 1], scalar2=None,
                                    op0=ALU.mult)
                    e.tensor_tensor(out=S_sb[:, h * 128:(h + 1) * 128],
                                    in0=S_sb[:, h * 128:(h + 1) * 128],
                                    in1=p_st[:, 0:128], op=ALU.add)
                    return e.tensor_copy(out=S_bf[:, h * 128:(h + 1) * 128],
                                         in_=S_sb[:, h * 128:(h + 1) * 128])
                wl = [(cP, t_stm), (cC, t_dec)]
                p.task("vector", f_su, waits=wl, inc=cV)
                s_ready[h] = cV.n
                st_free[0] = cV.n

        kfm_free = [0]
        scb_free = [0]
        oi_free = [0]
        ksc_free = [0]
        st_free = [0]

        def dbg_store(bufs, n):
            cOg = p.ctr("og")
            for i, bp in enumerate(bufs[:n]):
                sh = bp.ap().shape if hasattr(bp, 'ap') else bp.shape
                p.task("gpsimd", lambda e, i=i, bp=bp, sh=sh: e.dma_start(
                    out=x_out[i * 128:i * 128 + sh[0], 0:sh[1]], in_=bp[:]),
                    waits=[(cV, cV.n), (cC, cC.n), (cP, cP.n), (cQ, cQ.n)],
                    inc=cOg, inc_k=16)
            p.task("sync", lambda e: e.nop(), waits=[(cOg, cOg.n)])
            if own_prog:
                p.finish()

        emit_half_ln(x_pre)
        for ms in range(NSLAB):
            b = ms % 2
            t_kv = sched_proj_slab(ms, xf, own=False)
            for c in range(NT):
                sched_state_chunk(ms, b, c, t_kv, own=False)
            kv_used[b] = (cP, p.ctrs["pe"].n)
            if stage == 1 and ms == 0:
                dbg_store([S_sb], 1); return
        if stage == 2:
            dbg_store([S_sb], 1); return

        emit_half_ln(x_own)
        def f_g1(e):
            last = None
            for tb in range(2):
                for k in range(ND):
                    last = e.matmul(out=p_pj[0][:, 0:512],
                                    lhsT=wg1_sb[:, k * 128:(k + 1) * 128],
                                    rhs=xf[k][:, tb * 512:(tb + 1) * 512],
                                    start=(k == 0), stop=(k == ND - 1))
            return last
        for tb in range(2):
            def f_g1t(e, tb=tb):
                last = None
                for k in range(ND):
                    last = e.matmul(out=p_pj[0][:, 0:512],
                                    lhsT=wg1_sb[:, k * 128:(k + 1) * 128],
                                    rhs=xf[k][:, tb * 512:(tb + 1) * 512],
                                    start=(k == 0), stop=(k == ND - 1))
                return last
            p.task("tensor", f_g1t, waits=[(cC, max(ln_gate[0], pj_evac[0])),
                                           (cG, t_wg)], inc=cP)
            t_g1 = cP.n
            p.task("scalar", lambda e, tb=tb: e.copy(
                out=g1[:, tb * 512:(tb + 1) * 512], in_=p_pj[0][:, 0:512]),
                waits=[(cP, t_g1)], inc=cC)
            pj_evac[0] = cC.n

        for ms in range(NSLAB):
            b = ms % 2
            t_kv = sched_proj_slab(ms, xf, own=True)
            for c in range(NT):
                sched_state_chunk(ms, b, c, t_kv, own=True)
            kv_used[b] = (cP, p.ctrs["pe"].n)
            if stage == 3 and ms == 0:
                dbg_store(O_tm, 1); return
        if stage == 4:
            dbg_store(O_tm, NT); return

        gn_done = []
        gsb_free = {}
        for c in range(NT):
            t_og_l = 0
            for s4 in range(4):
                def f_gt(e, c=c, s4=s4):
                    return e.matmul(out=p_pj[s4 % 2][:, 0:512],
                                    lhsT=g1[:, c * 128:(c + 1) * 128],
                                    rhs=wg2_sb[:, s4 * 512:(s4 + 1) * 512],
                                    start=True, stop=True)
                p.task("tensor", f_gt, waits=[(cC, max(pj_evac[s4 % 2], 0)),
                                              (cG, t_wg)], inc=cP)
                t_gt = cP.n
                p.task("scalar", lambda e, s4=s4: e.activation(
                    out=gate_sb[:, (s4 % 2) * 512:(s4 % 2) * 512 + 512],
                    in_=p_pj[s4 % 2][:, 0:512], func=AF.Sigmoid),
                    waits=[(cP, t_gt), (cV, gsb_free.get(s4 % 2, 0))], inc=cC)
                pj_evac[s4 % 2] = cC.n
                t_gate = cC.n

                def f_og(e, c=c, s4=s4):
                    return e.tensor_tensor(
                        out=O_tm[c][:, s4 * 512:(s4 + 1) * 512],
                        in0=O_tm[c][:, s4 * 512:(s4 + 1) * 512],
                        in1=gate_sb[:, (s4 % 2) * 512:(s4 % 2) * 512 + 512],
                        op=ALU.mult)
                p.task("vector", f_og, waits=[(cC, t_gate),
                                              (cV, gn_done[-1] if gn_done else 0)],
                       inc=cV)
                gsb_free[s4 % 2] = cV.n
                t_og_l = cV.n
            t_og = t_og_l
            for h in range(NHEAD):
                p.task("scalar", lambda e, c=c, h=h: e.activation(
                    out=dump[:, h * 128:(h + 1) * 128],
                    in_=O_tm[c][:, h * 128:(h + 1) * 128], func=AF.Square,
                    accum_out=means[:, h:h + 1]),
                    waits=[(cV, t_og)], inc=cC)
            t_mean = cC.n

            def f_rst(e):
                return e.tensor_scalar(out=rst2[:], in0=means[:], scalar1=1.0 / 128.0,
                                       scalar2=EPS, op0=ALU.mult, op1=ALU.add)
            p.task("vector", f_rst, waits=[(cC, t_mean)], inc=cV)
            t_v1 = cV.n
            p.task("scalar", lambda e: e.activation(out=means[:], in_=rst2[:],
                                                    func=AF.Sqrt),
                   waits=[(cV, t_v1)], inc=cC)
            t_sq = cC.n
            p.task("vector", lambda e: e.reciprocal(out=rst[:], in_=means[:]),
                   waits=[(cC, t_sq)], inc=cV)
            t_rc = cV.n
            for h in range(NHEAD):
                p.task("vector", lambda e, c=c, h=h: e.tensor_scalar(
                    out=O_tm[c][:, h * 128:(h + 1) * 128],
                    in0=O_tm[c][:, h * 128:(h + 1) * 128],
                    scalar1=rst[:, h:h + 1], scalar2=None, op0=ALU.mult),
                    waits=[(cV, t_rc)], inc=cV)
            gn_done.append(cV.n)

        if stage == 5:
            dbg_store(O_tm, NT); return
        tiles = [(O_tm[i][:, c * 128:(c + 1) * 128], xf[c][:, i * 128:(i + 1) * 128])
                 for c in range(ND) for i in range(NT)]
        t_o4f = emit_tm_to_fm(p, tiles, ident, p_tr,
                              waits=[(cV, gn_done[-1])], t_ident=t_id, psum2=p_tr2)

        delta = O_tm
        wo_evac = {0: 0, 1: 0}
        wo_scaled = {}
        for ms in range(4):
            b = ms % 2
            p.task("gpsimd", lambda e, ms=ms, b=b: e.dma_start(
                out=wbuf[b][:].rearrange("p (k f) -> p k f", k=16),
                in_=wo[:, ms * 512:(ms + 1) * 512].rearrange("(k p) f -> p k f", p=128)),
                waits=[(cP, wbuf_used[b]), (cV, wo_scaled.get(b, 0))],
                inc=cW[b], inc_k=16)
            t_wdma = cW[b].n

            def f_wsc(e, b=b):
                last = None
                for k in range(ND):
                    last = e.tensor_scalar(
                        out=wbuf[b][:, k * 512:(k + 1) * 512],
                        in0=wbuf[b][:, k * 512:(k + 1) * 512],
                        scalar1=nwc[:, k:k + 1], scalar2=None, op0=ALU.mult)
                return last
            p.task("vector", f_wsc, waits=[(cW[b], t_wdma), (cSc, t_nwc)], inc=cV)
            wo_scaled[b] = cV.n
            t_w = None
            for tt in range(NT):
                def f_mm(e, ms=ms, b=b, tt=tt):
                    last = None
                    for k in range(ND):
                        last = e.matmul(out=p_pj[tt % 2][:, 0:512],
                                        lhsT=xf[k][:, tt * 128:(tt + 1) * 128],
                                        rhs=wbuf[b][:, k * 512:(k + 1) * 512],
                                        start=(k == 0), stop=(k == ND - 1))
                    return last
                p.task("tensor", f_mm, waits=[(cV, wo_scaled[b]), (cC, t_o4f),
                                              (cC, wo_evac[tt % 2]),
                                              (cV, gn_done[-1])], inc=cP)
                t_mm = cP.n
                p.task("scalar", lambda e, ms=ms, tt=tt: e.copy(
                    out=delta[tt][:, ms * 512:(ms + 1) * 512],
                    in_=p_pj[tt % 2][:, 0:512]),
                    waits=[(cP, t_mm)], inc=cC)
                wo_evac[tt % 2] = cC.n
            wbuf_used[b] = cP.n
        t_delta = cC.n

        prev_v2 = 0
        xr_store = {0: 0, 1: 0}
        for i in range(NT):
            b = i % 2
            p.task("sync", lambda e, i=i, b=b: e.dma_start(
                out=XR[b][:], in_=x_own[i * 128:(i + 1) * 128, :]),
                waits=[(cO[b], xr_store[b]), (cC, t_delta if i < 2 else 0)],
                inc=cXR[b], inc_k=16)
            t_ldx = cXR[b].n
            tv = emit_ln_postadd_bf16(p, delta[i], XR[b], dump, cols,
                                      waits=[(cXR[b], t_ldx), (cC, t_delta)],
                                      prev_v=prev_v2)
            prev_v2 = tv
            p.task("sync", lambda e, i=i, b=b: e.dma_start(
                out=x_out[i * 128:(i + 1) * 128, :], in_=XR[b][:]),
                waits=[(cV, tv)], inc=cO[b], inc_k=16)
            xr_store[b] = cO[b].n
        p.task("sync", lambda e: e.nop(), waits=[(cO[0], cO[0].n), (cO[1], cO[1].n)])
        if own_prog:
            p.finish()
    return nc


def emit_ln_postadd_bf16(p, y_bf, xr, dump, cols, waits=(), prev_v=0, width=D):
    cV, cC = p.ctr("v"), p.ctr("c")
    hw = width // 2
    inv = 1.0 / width

    def f_sq(e):
        e.activation(out=dump[:, 0:hw], in_=y_bf[:, 0:hw], func=AF.Square,
                     accum_out=cols[:, 1:2])
        return e.activation(out=dump[:, hw:width], in_=y_bf[:, hw:width],
                            func=AF.Square, accum_out=cols[:, 7:8])
    p.task("scalar", f_sq, waits=list(waits) + [(cV, prev_v)], inc=cC)
    t_sq = cC.n

    def f_v(e):
        e.reduce_sum(out=cols[:, 0:1], in_=y_bf[:], axis=AX.X)
        e.tensor_tensor(out=cols[:, 1:2], in0=cols[:, 1:2], in1=cols[:, 7:8],
                        op=ALU.add)
        e.tensor_scalar(out=cols[:, 2:3], in0=cols[:, 0:1], scalar1=inv,
                        scalar2=None, op0=ALU.mult)
        e.tensor_scalar(out=cols[:, 3:4], in0=cols[:, 1:2], scalar1=inv,
                        scalar2=None, op0=ALU.mult)
        e.tensor_tensor(out=cols[:, 5:6], in0=cols[:, 2:3], in1=cols[:, 2:3],
                        op=ALU.mult)
        e.tensor_tensor(out=cols[:, 3:4], in0=cols[:, 3:4], in1=cols[:, 5:6],
                        op=ALU.subtract)
        return e.tensor_scalar(out=cols[:, 3:4], in0=cols[:, 3:4], scalar1=EPS,
                               scalar2=None, op0=ALU.add)
    p.task("vector", f_v, waits=list(waits) + [(cC, t_sq)], inc=cV)
    t_var = cV.n
    p.task("scalar", lambda e: e.activation(out=cols[:, 6:7], in_=cols[:, 3:4],
                                            func=AF.Sqrt),
           waits=[(cV, t_var)], inc=cC)
    t_std = cC.n

    def f_n(e):
        e.reciprocal(out=cols[:, 4:5], in_=cols[:, 6:7])
        e.tensor_scalar(out=dump[:], in0=y_bf[:],
                        scalar1=cols[:, 2:3], scalar2=cols[:, 4:5],
                        op0=ALU.subtract, op1=ALU.mult)
        return e.tensor_tensor(out=xr[:], in0=xr[:], in1=dump[:], op=ALU.add)
    p.task("vector", f_n, waits=[(cC, t_std)], inc=cV)
    return cV.n


NQG = 4
NHD = 8
DH = 64
DKV = 512
SEQ = 2048
NTT = SEQ // 128
HGRP = 4


def build_l1a(nc, stage=99, p=None, io=None, gate=()):
    x_own = io["x_own"] if io is not None else nc.dram_tensor("x_own", [T, D], F32, kind="ExternalInput").ap()
    x_oth = io["x_oth"] if io is not None else nc.dram_tensor("x_oth", [T, D], F32, kind="ExternalInput").ap()
    wqt = io["wqt"] if io is not None else nc.dram_tensor("wqt", [D, D], F32, kind="ExternalInput").ap()
    wkt = io["wkt"] if io is not None else nc.dram_tensor("wkt", [D, DKV], F32, kind="ExternalInput").ap()
    wvt = io["wvt"] if io is not None else nc.dram_tensor("wvt", [D, DKV], F32, kind="ExternalInput").ap()
    mem = io["mem"] if io is not None else nc.dram_tensor("mem", [DKV, DKV], F32, kind="ExternalInput").ap()
    mn_c = io["mn_c"] if io is not None else nc.dram_tensor("mn_c", [128, 4], F32, kind="ExternalInput").ap()
    mw = io["mw"] if io is not None else nc.dram_tensor("mw", [1, 1], F32, kind="ExternalInput").ap()
    x_out = io["x_out"] if io is not None else nc.dram_tensor("x_out", [T, D], F32, kind="ExternalOutput").ap()

    es = contextlib.ExitStack()
    with es:
        U = _uid()
        sb = lambda name, shape, dt=BF16: es.enter_context(nc.sbuf_tensor(U + name, shape, dt))
        ps = lambda name, shape: es.enter_context(nc.psum_tensor(U + name, shape, F32))
        XR = [sb(f"XR{b}", [128, D], F32) for b in range(2)]
        q_fm = [sb(f"qfm{c}", [128, T]) for c in range(ND)]
        K_tm = sb("K_tm", [128, NTT * 512])
        V_bf = sb("V_bf", [128, NTT * 520])
        q_s = [sb(f"q_s{b}", [64, T]) for b in range(2)]
        expT = [sb(f"expT{b}", [128, 512]) for b in range(2)]
        amem_sb = [sb(f"amem{c}", [128, T]) for c in range(2)]
        atmp = sb("atmp", [128, 128], F32)
        mem_sb = sb("mem_sb", [128, 4 * 512])
        mnc_sb = sb("mnc_sb", [128, 4], F32)
        mnc_bf = sb("mnc_bf", [128, 4])
        wcols = sb("wcols", [128, 4], F32)
        mwrow = sb("mwrow", [1, 2], F32)
        dnr = sb("dnr", [1, T], F32)
        rcol = sb("rcol", [128, 4], F32)
        ident = sb("ident", [128, 128])
        one1 = sb("one1", [1, 128])
        one1f = sb("one1f", [1, 128], F32)
        onef = sb("onef", [1, 1], F32)
        dump = sb("dump", [128, D])
        dump2 = sb("dump2", [128, D])
        cols = sb("cols", [128, 8], F32)
        wbuf = [sb(f"wbuf{b}", [128, 16 * 512]) for b in range(2)]
        p_tr = ps("p_tr", [128, 512])
        p_pj = [ps(f"p_pj{b}", [128, 512]) for b in range(2)]
        p_sc = [ps(f"p_sc{b}", [128, 512]) for b in range(2)]
        p_dn = ps("p_dn", [1, T])
        p_pv = ps("p_pv", [128, 260])
        xn_es = contextlib.ExitStack()
        xf = [xn_es.enter_context(nc.sbuf_tensor(U + f"xf{c}", [128, 2 * T], BF16))
              for c in range(ND)]

        own_prog = p is None
        if own_prog:
            p = Prog(nc)
        p.default_waits = list(gate)
        cV, cC, cP = (p.ctr(n) for n in ("v", "c", "pe"))
        cQ = p.ctr("q")
        cG = p.ctr("g")
        cSc = p.ctr("sc")
        cXR = [p.ctr("xr0"), p.ctr("xr1")]
        cW = [p.ctr("w0"), p.ctr("w1")]
        cO = [p.ctr("o0"), p.ctr("o1")]

        t_id = emit_identity(p, ident, one1)
        p.task("gpsimd", lambda e: e.memset(one1f[:], 1.0), inc=cQ)
        p.task("gpsimd", lambda e: e.memset(onef[:], 1.0), inc=cQ)
        p.task("gpsimd", lambda e: e.memset(
            V_bf[:].rearrange("p (tt h s) -> p (tt h) s", tt=NTT, h=8)[:, :, 64:65],
            1.0), inc=cQ)
        t_setup = (cQ, cQ.n)

        p.task("gpsimd", lambda e: e.dma_start(
            out=mem_sb[:].rearrange("p (k f) -> p k f", k=4),
            in_=mem.rearrange("(k p) f -> p k f", p=128)), inc=cG, inc_k=16)
        t_mem = cG.n
        p.task("sync", lambda e: e.dma_start(out=mnc_sb[:], in_=mn_c), inc=cSc, inc_k=16)
        p.task("sync", lambda e: e.dma_start(out=mwrow[:, 0:1], in_=mw), inc=cSc, inc_k=16)
        t_mw = cSc.n
        p.task("vector", lambda e: e.tensor_copy(out=mnc_bf[:], in_=mnc_sb[:]),
               waits=[(cSc, t_mw)], inc=cV)
        t_mnb = cV.n
        p.task("scalar", lambda e: e.activation(out=mwrow[:, 1:2], in_=mwrow[:, 0:1],
                                                func=AF.Sigmoid),
               waits=[(cSc, t_mw)], inc=cC)
        t_sig = cC.n
        p.task("tensor", lambda e: e.matmul(out=p_tr[:, 0:1], lhsT=one1f[:],
                                            rhs=mwrow[:, 1:2], start=True, stop=True),
               waits=[(cC, t_sig), t_setup], inc=cP)
        p.task("scalar", lambda e: e.copy(out=wcols[:, 0:1], in_=p_tr[:, 0:1]),
               waits=[(cP, cP.n)], inc=cC)
        t_w1 = cC.n
        p._tr_evac = t_w1
        p.task("vector", lambda e: e.tensor_scalar(
            out=wcols[:, 1:2], in0=wcols[:, 0:1], scalar1=1.0, scalar2=-1.0,
            op0=ALU.subtract, op1=ALU.mult), waits=[(cC, t_w1)], inc=cV)
        t_wc = cV.n

        prev_v = [0]
        xr_free = {0: 0, 1: 0}
        t_lastpe = [0]

        def half_ln(x_dram, half):
            for i in range(NT):
                b = i % 2
                p.task("sync", lambda e, i=i, b=b, x_dram=x_dram: e.dma_start(
                    out=XR[b][:], in_=x_dram[i * 128:(i + 1) * 128, :]),
                    waits=[(cV, xr_free[b])], inc=cXR[b], inc_k=16)
                t_ld = cXR[b].n
                pv = emit_ln_chunk(p, XR[b], dump, dump2, cols,
                                   waits=[(cXR[b], t_ld), (cP, t_lastpe[0])],
                                   prev_v=prev_v[0])
                prev_v[0] = pv
                xr_free[b] = pv
                tiles = [(dump[:, c * 128:(c + 1) * 128],
                          xf[c][:, half * T + i * 128: half * T + (i + 1) * 128])
                         for c in range(ND)]
                emit_tm_to_fm(p, tiles, ident, p_tr, waits=[(cV, pv)], t_ident=t_id)
                t_lastpe[0] = p.ctrs["pe"].n

        half_ln(x_own, 0)
        half_ln(x_oth, 1)
        t_xf = p.ctrs["c"].n

        wbuf_used = SharedCell()
        pj_evac = {0: 0, 1: 0}

        def stream_w(w_ap, col0, width, b):
            p.task("gpsimd", lambda e, w_ap=w_ap, col0=col0, width=width, b=b:
                   e.dma_start(
                       out=wbuf[b][:, 0:16 * width].rearrange("p (k f) -> p k f", k=16),
                       in_=w_ap[:, col0:col0 + width].rearrange("(k p) f -> p k f", p=128)),
                   waits=[(cP, wbuf_used[b])], inc=cW[b], inc_k=16)
            return cW[b].n

        t_wk = stream_w(wkt, 0, 512, 0)
        for tt in range(NTT):
            def f_mm(e, tt=tt, t_wk=t_wk):
                last = None
                for k in range(ND):
                    last = e.matmul(out=p_pj[tt % 2][:, 0:512],
                                    lhsT=xf[k][:, tt * 128:(tt + 1) * 128],
                                    rhs=wbuf[0][:, k * 512:(k + 1) * 512],
                                    start=(k == 0), stop=(k == ND - 1))
                return last
            p.task("tensor", f_mm, waits=[(cW[0], t_wk),
                                          (cC, max(pj_evac[tt % 2], t_xf))], inc=cP)
            t_mm = cP.n
            p.task("scalar", lambda e, tt=tt: e.copy(
                out=K_tm[:, tt * 512:(tt + 1) * 512], in_=p_pj[tt % 2][:, 0:512]),
                waits=[(cP, t_mm)], inc=cC)
            pj_evac[tt % 2] = cC.n
        wbuf_used[0] = cP.n

        t_wv = stream_w(wvt, 0, 512, 1)
        for tt in range(NTT):
            def f_mm(e, tt=tt, t_wv=t_wv):
                last = None
                for k in range(ND):
                    last = e.matmul(out=p_pj[tt % 2][:, 0:512],
                                    lhsT=xf[k][:, tt * 128:(tt + 1) * 128],
                                    rhs=wbuf[1][:, k * 512:(k + 1) * 512],
                                    start=(k == 0), stop=(k == ND - 1))
                return last
            p.task("tensor", f_mm, waits=[(cW[1], t_wv),
                                          (cC, max(pj_evac[tt % 2], t_xf))], inc=cP)
            t_mm = cP.n

            def f_ev(e, tt=tt):
                return e.copy(
                    out=V_bf[:, tt * 520:(tt + 1) * 520].rearrange(
                        "p (h s) -> p h s", h=8)[:, :, 0:64],
                    in_=p_pj[tt % 2][:, 0:512].rearrange("p (h s) -> p h s", h=8))
            p.task("scalar", f_ev, waits=[(cP, t_mm)], inc=cC)
            pj_evac[tt % 2] = cC.n
        wbuf_used[1] = cP.n

        for ms in range(4):
            b = ms % 2
            t_wq = stream_w(wqt, ms * 512, 512, b)
            for ec in range(4):
                c = ms * 4 + ec
                for tb in range(2):
                    def f_mm(e, c=c, ec=ec, b=b, tb=tb, t_wq=t_wq):
                        last = None
                        for k in range(ND):
                            last = e.matmul(
                                out=p_pj[tb % 2][:, 0:512],
                                lhsT=wbuf[b][:, k * 512 + ec * 128:
                                             k * 512 + (ec + 1) * 128],
                                rhs=xf[k][:, tb * 512:(tb + 1) * 512],
                                start=(k == 0), stop=(k == ND - 1))
                        return last
                    p.task("tensor", f_mm, waits=[(cW[b], t_wq),
                                                  (cC, max(pj_evac[tb % 2], t_xf))],
                           inc=cP)
                    t_mm = cP.n
                    p.task("scalar", lambda e, c=c, tb=tb: e.copy(
                        out=q_fm[c][:, tb * 512:(tb + 1) * 512],
                        in_=p_pj[tb % 2][:, 0:512]),
                        waits=[(cP, t_mm)], inc=cC)
                    pj_evac[tb % 2] = cC.n
            wbuf_used[b] = cP.n
        t_qkv = cC.n

        def dbg_store(aps):
            cOg = p.ctr("og")
            for i, ap2 in enumerate(aps):
                sh = ap2.shape
                p.task("gpsimd", lambda e, i=i, ap2=ap2, sh=sh: e.dma_start(
                    out=x_out[i * 128:i * 128 + sh[0], 0:sh[1]], in_=ap2),
                    waits=[(cV, cV.n), (cC, cC.n), (cP, cP.n)], inc=cOg, inc_k=16)
            p.task("sync", lambda e: e.nop(), waits=[(cOg, cOg.n)])
            if own_prog:
                p.finish()

        if stage in (1, 2, 3):
            xn_es.close()
            if stage == 1:
                dbg_store([K_tm[:, j * 2048:(j + 1) * 2048] for j in range(4)])
            elif stage == 2:
                dbg_store([V_bf[:, j * 520:j * 520 + 520] for j in range(4)])
            else:
                dbg_store([q_fm[c][:] for c in range(8)])
            return

        xn_es.close()
        ADO = [sb(f"ADO{i}", [128, D]) for i in range(NT)]
        K_F = [sb(f"KF{j}", [64, SEQ]) for j in range(HGRP)]

        cQS = [p.ctr("qsd0"), p.ctr("qsd1")]
        ktr_evac = [0]
        exp_done = {0: 0, 1: 0}
        expT_free = {0: 0, 1: 0}
        pv_free = [0]

        for hg in range(2):
            for j in range(HGRP):
                h = hg * HGRP + j
                for tt in range(NTT):
                    def f_tr(e, h=h, tt=tt):
                        return e.matmul(out=p_tr[0:64, 0:128],
                                        lhsT=K_tm[:, tt * 512 + (h % 8) * 64:
                                                  tt * 512 + (h % 8) * 64 + 64],
                                        rhs=ident[:], start=True, stop=True)
                    p.task("tensor", f_tr, waits=[(cC, max(t_qkv, ktr_evac[0])),
                                                  (cV, pv_free[0])],
                           inc=cP)
                    t_trm = cP.n
                    p.task("scalar", lambda e, j=j, tt=tt: e.copy(
                        out=K_F[j][:, tt * 128:(tt + 1) * 128],
                        in_=p_tr[0:64, 0:128]),
                        waits=[(cP, t_trm)], inc=cC)
                    ktr_evac[0] = cC.n
            t_kf = cC.n
            combos = [(qg, hg * HGRP + j) for qg in range(NQG) for j in range(HGRP)]

            def sched_qs(ci):
                qg, h = combos[ci]
                ec = (qg * 512 + h * 64) // 128
                off = (qg * 512 + h * 64) % 128
                qb = ci % 2
                p.task("gpsimd", lambda e, ec=ec, off=off, qb=qb: e.dma_start(
                    out=q_s[qb][:], in_=q_fm[ec][off:off + 64, :]),
                    waits=[(cC, t_qkv), (cP, qs_used[qb])], inc=cQS[qb], inc_k=16)
                qs_t[ci] = cQS[qb].n

            qs_t = {}
            qs_used = {0: cP.n, 1: cP.n}
            sched_qs(0)
            for ci in range(len(combos)):
                qg, h = combos[ci]
                j = h - hg * HGRP
                qb = ci % 2
                if ci + 1 < len(combos):
                    sched_qs(ci + 1)
                t_qs = qs_t[ci]
                if True:
                    for sb_i in range(2):
                        for tt in range(NTT):
                            pb = tt % 2

                            def f_sc(e, j=j, sb_i=sb_i, tt=tt, pb=pb, qb=qb):
                                return e.matmul(
                                    out=p_sc[pb][:, 0:512],
                                    lhsT=K_F[j][:, tt * 128:(tt + 1) * 128],
                                    rhs=q_s[qb][:, sb_i * 512:(sb_i + 1) * 512],
                                    start=True, stop=True)
                            p.task("tensor", f_sc,
                                   waits=[(cQS[qb], t_qs),
                                          (cC, max(t_kf, exp_done[pb])),
                                          (cV, pv_free[0] if tt == 0 else 0)],
                                   inc=cP)
                            t_scm = cP.n

                            def f_exp(e, pb=pb):
                                return e.activation(out=expT[pb][:],
                                                    in_=p_sc[pb][:, 0:512],
                                                    func=AF.Exp, scale=0.125)
                            p.task("scalar", f_exp,
                                   waits=[(cP, max(t_scm, expT_free[pb]))], inc=cC)
                            exp_done[pb] = cC.n
                            t_exp = cC.n

                            def f_pv(e, tt=tt, pb=pb, h=h):
                                last = None
                                pv_banks = [p_pv, p_pj[0], p_pj[1], p_tr]
                                for sj in range(4):
                                    last = e.matmul(
                                        out=pv_banks[sj][:, 0:65],
                                        lhsT=expT[pb][:, sj * 128:(sj + 1) * 128],
                                        rhs=V_bf[:, tt * 520 + (h % 8) * 65:
                                                 tt * 520 + (h % 8) * 65 + 65],
                                        start=(tt == 0), stop=(tt == NTT - 1),
                                        skip_group_check=True)
                                return last
                            p.task("tensor", f_pv, waits=[(cC, t_exp), t_setup],
                                   inc=cP)
                            expT_free[pb] = cP.n
                        t_pvm = cP.n

                        def f_ev(e, qg=qg, h=h, sb_i=sb_i):
                            last = None
                            pv_banks = [p_pv, p_pj[0], p_pj[1], p_tr]
                            for sj in range(4):
                                e.reciprocal(out=rcol[:, sj:sj + 1],
                                             in_=pv_banks[sj][:, 64:65])
                            for sj in range(4):
                                st = sb_i * 4 + sj
                                last = e.tensor_scalar(
                                    out=ADO[st][:, qg * 512 + h * 64:
                                                qg * 512 + (h + 1) * 64],
                                    in0=pv_banks[sj][:, 0:64],
                                    scalar1=rcol[:, sj:sj + 1], scalar2=None,
                                    op0=ALU.mult)
                            return last
                        p.task("vector", f_ev, waits=[(cP, t_pvm)], inc=cV)
                        pv_free[0] = cV.n
                qs_used[qb] = cP.n
        t_attn_v = cV.n
        if stage == 4:
            dbg_store([ADO[i][:] for i in range(NT)]); return

        for i in range(NT):
            p.task("vector", lambda e, i=i: e.tensor_scalar(
                out=ADO[i][:], in0=ADO[i][:], scalar1=wcols[:, 1:2], scalar2=None,
                op0=ALU.mult), waits=[(cV, max(t_attn_v, t_wc))], inc=cV)
        t_ado_w = cV.n

        elu_done = 0
        for c2 in range(ND):
            def f_e1(e, c2=c2):
                return e.tensor_scalar(out=dump[:, 0:T], in0=q_fm[c2][:],
                                       scalar1=0.0, scalar2=None, op0=ALU.min)
            p.task("vector", f_e1, waits=[(cV, t_ado_w), (cC, elu_done)], inc=cV)
            tv1 = cV.n
            p.task("scalar", lambda e: e.activation(out=dump[:, 0:T], in_=dump[:, 0:T],
                                                    func=AF.Exp),
                   waits=[(cV, tv1)], inc=cC)
            tc1 = cC.n

            def f_e2(e, c2=c2):
                return e.scalar_tensor_tensor(out=q_fm[c2][:], in0=dump[:, 0:T],
                                              scalar=1.0, in1=q_fm[c2][:],
                                              op0=ALU.subtract, op1=ALU.max)
            p.task("vector", f_e2, waits=[(cC, tc1)], inc=cV)
            elu_done = tc1
        t_elu = cV.n

        amem_evac = {0: 0, 1: 0}
        dn_evac = [0]
        cmb_done = [0]
        for qg in range(NQG):
            def f_dn(e, qg=qg):
                last = None
                for half in range(2):
                    for k in range(4):
                        last = e.matmul(
                            out=p_dn[0:1, half * 512:(half + 1) * 512],
                            lhsT=mnc_bf[:, k:k + 1],
                            rhs=q_fm[qg * 4 + k][:, half * 512:(half + 1) * 512],
                            start=(k == 0), stop=(k == 3))
                return last
            p.task("tensor", f_dn, waits=[(cV, max(t_elu, t_mnb)),
                                          (cV, dn_evac[0])], inc=cP)
            t_dnm = cP.n
            p.task("vector", lambda e: e.tensor_copy(out=dnr[:], in_=p_dn[:]),
                   waits=[(cP, t_dnm)], inc=cV)
            dn_evac[0] = cV.n
            t_dnr = cV.n

            for e2g in range(2):
                for e2i in range(2):
                    e2c = e2g * 2 + e2i
                    pb = e2i
                    for half in range(2):
                        def f_amh(e, qg=qg, e2c=e2c, pb=pb, half=half):
                            last = None
                            for k in range(4):
                                last = e.matmul(
                                    out=p_pj[pb][:, 0:512],
                                    lhsT=mem_sb[:, k * 512 + e2c * 128:
                                                k * 512 + (e2c + 1) * 128],
                                    rhs=q_fm[qg * 4 + k][:, half * 512:(half + 1) * 512],
                                    start=(k == 0), stop=(k == 3))
                            return last
                        p.task("tensor", f_amh, waits=[(cV, t_elu), (cG, t_mem),
                                                       (cC, amem_evac[pb]),
                                                       (cV, cmb_done[0])], inc=cP)
                        t_amm = cP.n
                        p.task("scalar", lambda e, e2i=e2i, half=half, pb=pb: e.copy(
                            out=amem_sb[e2i][:, half * 512:(half + 1) * 512],
                            in_=p_pj[pb][:, 0:512]),
                            waits=[(cP, t_amm)], inc=cC)
                        amem_evac[pb] = cC.n
                t_amem = cC.n

                for st in range(NT):
                    def f_dc(e, st=st):
                        return e.matmul(out=p_tr[:, 256:257],
                                        lhsT=dnr[0:1, st * 128:(st + 1) * 128],
                                        rhs=onef[:], start=True, stop=True)
                    p.task("tensor", f_dc, waits=[(cV, t_dnr), t_setup,
                                                  (cV, cmb_done[0])], inc=cP)
                    t_dcm = cP.n

                    p.task("vector", lambda e: e.reciprocal(
                        out=rcol[:, 1:2], in_=p_tr[:, 256:257]),
                        waits=[(cP, t_dcm)], inc=cV)
                    t_rc0 = cV.n
                    p.task("vector", lambda e: e.tensor_tensor(
                        out=rcol[:, 2:3], in0=rcol[:, 1:2], in1=wcols[:, 0:1],
                        op=ALU.mult), waits=[(cV, max(t_rc0, t_wc))], inc=cV)
                    t_rc = cV.n
                    for e2i in range(2):
                        e2c = e2g * 2 + e2i

                        def f_tr2(e, e2i=e2i, st=st):
                            return e.matmul(out=p_tr[:, 0:128],
                                            lhsT=amem_sb[e2i][:, st * 128:(st + 1) * 128],
                                            rhs=ident[:], start=True, stop=True)
                        p.task("tensor", f_tr2, waits=[(cC, t_amem),
                                                       (cV, max(cmb_done[0], t_rc))],
                               inc=cP)
                        t_trm = cP.n

                        def f_cmb(e, qg=qg, e2c=e2c, st=st):
                            e.tensor_scalar(out=atmp[:], in0=p_tr[:, 0:128],
                                            scalar1=rcol[:, 2:3], scalar2=None,
                                            op0=ALU.mult)
                            dst = ADO[st][:, qg * 512 + e2c * 128:
                                          qg * 512 + (e2c + 1) * 128]
                            return e.tensor_tensor(out=dst, in0=dst, in1=atmp[:],
                                                   op=ALU.add)
                        p.task("vector", f_cmb, waits=[(cP, t_trm), (cV, t_rc),
                                                       (cV, t_ado_w)], inc=cV)
                        cmb_done[0] = cV.n

        t_all_v = cV.n
        if stage == 5:
            dbg_store([ADO[i][:] for i in range(NT)]); return
        prev_v2 = 0
        xr_store = {0: 0, 1: 0}
        for i in range(NT):
            b = i % 2
            p.task("sync", lambda e, i=i, b=b: e.dma_start(
                out=XR[b][:], in_=x_own[i * 128:(i + 1) * 128, :]),
                waits=[(cO[b], xr_store[b]), (cV, t_all_v if i < 2 else 0)],
                inc=cXR[b], inc_k=16)
            t_ld = cXR[b].n
            tv = emit_ln_postadd_bf16(p, ADO[i], XR[b], dump, cols,
                                      waits=[(cXR[b], t_ld), (cV, t_all_v)],
                                      prev_v=prev_v2)
            prev_v2 = tv
            p.task("sync", lambda e, i=i, b=b: e.dma_start(
                out=x_out[i * 128:(i + 1) * 128, :], in_=XR[b][:]),
                waits=[(cV, tv)], inc=cO[b], inc_k=16)
            xr_store[b] = cO[b].n
        p.task("sync", lambda e: e.nop(), waits=[(cO[0], cO[0].n), (cO[1], cO[1].n)])
        if own_prog:
            p.finish()
    return nc



def _shards(x):
    v = np.asarray(x).reshape(4, 2, 1024, 2048)
    return [v[c // 2, c % 2] for c in range(8)]


def kernel_multi(**inp):
    x = f32c(inp["L0_attn_x"])
    xs = _shards(x)
    zeros = np.zeros((1024, 2048), np.float32)

    nc = bass.Bass("TRN2", target_bir_lowering=False)
    build_l0a(nc)
    com = dict(
        wq=f32c(inp["L0_attn_Wq"]), wk=f32c(inp["L0_attn_Wk"]),
        wv=f32c(inp["L0_attn_Wv"]), wo=f32c(inp["L0_attn_Wo"]),
        wg1=f32c(inp["L0_attn_Wg1"]), wg2=f32c(inp["L0_attn_Wg2"]),
        nw_c=f32c(np.asarray(inp["L0_attn_norm_w"]).reshape(16, 128).T),
        ld_r=f32c(np.asarray(inp["L0_attn_log_decay"]).reshape(1, -1)),
    )
    in_maps = []
    for c in range(8):
        pre = zeros if c % 2 == 0 else xs[c - 1]
        in_maps.append(dict(x_own=xs[c], x_pre=f32c(pre), **com))
    res = run_bass_kernel_spmd(nc, in_maps, core_ids=list(range(8)))
    ys = [res.results[c]["x_out"] for c in range(8)]

    nc = bass.Bass("TRN2", target_bir_lowering=False)
    build_l0f(nc)
    com = dict(
        wup_q=u8c(inp["L0_ffn_w_up_q"]),
        wup_s=f32c(np.asarray(inp["L0_ffn_w_up_scale"]).reshape(64, 128, 32)
                   .transpose(1, 0, 2).reshape(128, -1)),
        bup_c=f32c(np.asarray(inp["L0_ffn_b_up"]).reshape(64, 128).T),
        la_t=f32c(np.asarray(inp["L0_ffn_w_up_lora_a"]).reshape(16, 128, 16)
                  .transpose(1, 0, 2).reshape(128, -1)),
        lora_ub=f32c(inp["L0_ffn_w_up_lora_b"]),
        wdn_q=u8c(inp["L0_ffn_w_down_q"]),
        wdn_s=f32c(np.asarray(inp["L0_ffn_w_down_scale"]).reshape(16, 128, 128)
                   .transpose(1, 0, 2).reshape(128, -1)),
        b_dn=f32c(np.asarray(inp["L0_ffn_b_down"]).reshape(1, -1)),
        da_t=f32c(np.asarray(inp["L0_ffn_w_down_lora_a"]).reshape(64, 128, 16)
                  .transpose(1, 0, 2).reshape(128, -1)),
        lora_db=f32c(inp["L0_ffn_w_down_lora_b"]),
    )
    res = run_bass_kernel_spmd(nc, [dict(x_in=f32c(ys[c]), **com) for c in range(8)],
                               core_ids=list(range(8)))
    zs = [res.results[c]["x_out"] for c in range(8)]

    nc = bass.Bass("TRN2", target_bir_lowering=False)
    build_l1a(nc)
    com = dict(
        wqt=f32c(np.asarray(inp["L1_attn_Wq"]).T),
        wkt=f32c(np.asarray(inp["L1_attn_Wk"]).T),
        wvt=f32c(np.asarray(inp["L1_attn_Wv"]).T),
        mem=f32c(inp["L1_attn_memory"]),
        mn_c=f32c(np.asarray(inp["L1_attn_memory_norm"]).reshape(4, 128).T),
        mw=f32c(np.asarray(inp["L1_attn_memory_weight"]).reshape(1, 1)),
    )
    in_maps = []
    for c in range(8):
        oth = zs[c + 1] if c % 2 == 0 else zs[c - 1]
        in_maps.append(dict(x_own=f32c(zs[c]), x_oth=f32c(oth), **com))
    res = run_bass_kernel_spmd(nc, in_maps, core_ids=list(range(8)))
    us = [res.results[c]["x_out"] for c in range(8)]

    nc = bass.Bass("TRN2", target_bir_lowering=False)
    build_l1f(nc)
    com = dict(
        gw_q=u8c(inp["L1_ffn_gate_w"]),
        uw_q=u8c(inp["L1_ffn_up_w"]),
        dw_q=u8c(inp["L1_ffn_down_w"]),
        gs_c=f32c(np.asarray(inp["L1_ffn_gate_scale"]).reshape(64, 128).T),
        us_c=f32c(np.asarray(inp["L1_ffn_up_scale"]).reshape(64, 128).T),
        ds_c=f32c(np.asarray(inp["L1_ffn_down_scale"]).reshape(16, 128).T),
    )
    res = run_bass_kernel_spmd(nc, [dict(x_in=f32c(us[c]), **com) for c in range(8)],
                               core_ids=list(range(8)))
    outs = [res.results[c]["x_out"] for c in range(8)]

    out = np.stack(outs, axis=0).reshape(4, 2, 1024, 2048).reshape(4, 2048, 2048)
    return np.ascontiguousarray(out, dtype=np.float32)



def build_fused(nc):
    ext = lambda n, sh, dt=F32: nc.dram_tensor(n, sh, dt, kind="ExternalInput").ap()
    x_own = ext("x_own", [T, D])
    x_par = ext("x_par", [T, D])
    xpre_own = ext("xpre_own", [T, D])
    xpre_par = ext("xpre_par", [T, D])
    l0a_w = dict(wq=ext("wq", [D, D]), wk=ext("wk", [D, D]), wv=ext("wv", [D, D]),
                 wo=ext("wo", [D, D]), wg1=ext("wg1", [D, 128]),
                 wg2=ext("wg2", [128, D]), nw_c=ext("nw_c", [128, NHEAD]),
                 ld_r=ext("ld_r", [1, NHEAD]))
    l0f_w = dict(wup_q=ext("wup_q", [H, D], U8), wup_s=ext("wup_s", [128, NH * 32]),
                 bup_c=ext("bup_c", [128, NH]), la_t=ext("la_t", [128, ND * 16]),
                 lora_ub=ext("lora_ub", [16, H]), wdn_q=ext("wdn_q", [D, H], U8),
                 wdn_s=ext("wdn_s", [128, ND * 128]), b_dn=ext("b_dn", [1, D]),
                 da_t=ext("da_t", [128, NH * 16]), lora_db=ext("lora_db", [16, D]))
    l1a_w = dict(wqt=ext("wqt", [D, D]), wkt=ext("wkt", [D, DKV]),
                 wvt=ext("wvt", [D, DKV]), mem=ext("mem", [DKV, DKV]),
                 mn_c=ext("mn_c", [128, 4]), mw=ext("mw", [1, 1]))
    l1f_w = dict(gw_q=ext("gw_q", [H, D], U8), uw_q=ext("uw_q", [H, D], U8),
                 dw_q=ext("dw_q", [D, H], U8), gs_c=ext("gs_c", [128, NH]),
                 us_c=ext("us_c", [128, NH]), ds_c=ext("ds_c", [128, ND]))
    x_out = nc.dram_tensor("x_out", [T, D], F32, kind="ExternalOutput").ap()
    scr = {n: nc.dram_tensor(n, [T, D], F32).ap()
           for n in ("y_own", "y_par", "z_own", "z_par", "u_own")}

    p = Prog(nc)

    def gate():
        return [(c, c.n) for c in p.ctrs.values()]

    build_l0a(nc, p=p, io=dict(x_own=x_own, x_pre=xpre_own,
                               x_out=scr["y_own"], **l0a_w), gate=[])
    build_l0a(nc, p=p, io=dict(x_own=x_par, x_pre=xpre_par,
                               x_out=scr["y_par"], **l0a_w), gate=gate())
    build_l0f(nc, p=p, io=dict(x_in=scr["y_own"], x_out=scr["z_own"], **l0f_w),
              gate=gate())
    build_l0f(nc, p=p, io=dict(x_in=scr["y_par"], x_out=scr["z_par"], **l0f_w),
              gate=gate())
    build_l1a(nc, p=p, io=dict(x_own=scr["z_own"], x_oth=scr["z_par"],
                               x_out=scr["u_own"], **l1a_w), gate=gate())
    build_l1f(nc, p=p, io=dict(x_in=scr["u_own"], x_out=x_out, **l1f_w),
              gate=gate())
    p.finish()
    return nc


def _fused_inmaps(inp):
    x = f32c(inp["L0_attn_x"])
    xs = _shards(x)
    zeros = np.zeros((1024, 2048), np.float32)
    com = dict(
        wq=f32c(inp["L0_attn_Wq"]), wk=f32c(inp["L0_attn_Wk"]),
        wv=f32c(inp["L0_attn_Wv"]), wo=f32c(inp["L0_attn_Wo"]),
        wg1=f32c(inp["L0_attn_Wg1"]), wg2=f32c(inp["L0_attn_Wg2"]),
        nw_c=f32c(np.asarray(inp["L0_attn_norm_w"]).reshape(16, 128).T),
        ld_r=f32c(np.asarray(inp["L0_attn_log_decay"]).reshape(1, -1)),
        wup_q=u8c(inp["L0_ffn_w_up_q"]),
        wup_s=f32c(np.asarray(inp["L0_ffn_w_up_scale"]).reshape(64, 128, 32)
                   .transpose(1, 0, 2).reshape(128, -1)),
        bup_c=f32c(np.asarray(inp["L0_ffn_b_up"]).reshape(64, 128).T),
        la_t=f32c(np.asarray(inp["L0_ffn_w_up_lora_a"]).reshape(16, 128, 16)
                  .transpose(1, 0, 2).reshape(128, -1)),
        lora_ub=f32c(inp["L0_ffn_w_up_lora_b"]),
        wdn_q=u8c(inp["L0_ffn_w_down_q"]),
        wdn_s=f32c(np.asarray(inp["L0_ffn_w_down_scale"]).reshape(16, 128, 128)
                   .transpose(1, 0, 2).reshape(128, -1)),
        b_dn=f32c(np.asarray(inp["L0_ffn_b_down"]).reshape(1, -1)),
        da_t=f32c(np.asarray(inp["L0_ffn_w_down_lora_a"]).reshape(64, 128, 16)
                  .transpose(1, 0, 2).reshape(128, -1)),
        lora_db=f32c(inp["L0_ffn_w_down_lora_b"]),
        wqt=f32c(np.asarray(inp["L1_attn_Wq"]).T),
        wkt=f32c(np.asarray(inp["L1_attn_Wk"]).T),
        wvt=f32c(np.asarray(inp["L1_attn_Wv"]).T),
        mem=f32c(inp["L1_attn_memory"]),
        mn_c=f32c(np.asarray(inp["L1_attn_memory_norm"]).reshape(4, 128).T),
        mw=f32c(np.asarray(inp["L1_attn_memory_weight"]).reshape(1, 1)),
        gw_q=u8c(inp["L1_ffn_gate_w"]),
        uw_q=u8c(inp["L1_ffn_up_w"]),
        dw_q=u8c(inp["L1_ffn_down_w"]),
        gs_c=f32c(np.asarray(inp["L1_ffn_gate_scale"]).reshape(64, 128).T),
        us_c=f32c(np.asarray(inp["L1_ffn_up_scale"]).reshape(64, 128).T),
        ds_c=f32c(np.asarray(inp["L1_ffn_down_scale"]).reshape(16, 128).T),
    )
    in_maps = []
    for c in range(8):
        h = c % 2
        par = c + 1 if h == 0 else c - 1
        in_maps.append(dict(
            x_own=xs[c], x_par=xs[par],
            xpre_own=f32c(zeros if h == 0 else xs[par]),
            xpre_par=f32c(xs[c] if h == 0 else zeros),
            **com))
    return in_maps


def kernel(**inp):
    nc = bass.Bass("TRN2", target_bir_lowering=False)
    build_fused(nc)
    res = run_bass_kernel_spmd(nc, _fused_inmaps(inp), core_ids=list(range(8)))
    outs = [res.results[c]["x_out"] for c in range(8)]
    out = np.stack(outs, axis=0).reshape(4, 2, 1024, 2048).reshape(4, 2048, 2048)
    return np.ascontiguousarray(out, dtype=np.float32)


# version 11 (final)
# speedup vs baseline: 1.0646x; 1.0646x over previous
import contextlib
import numpy as np
import ml_dtypes
import concourse.bass as bass
import concourse.mybir as mybir
from concourse.bass_utils import run_bass_kernel_spmd

F32 = mybir.dt.float32
BF16 = mybir.dt.bfloat16
I32 = mybir.dt.int32
U8 = mybir.dt.uint8
AF = mybir.ActivationFunctionType
ALU = mybir.AluOpType
AX = mybir.AxisListType

D = 2048
T = 1024
NT = T // 128
ND = D // 128
EPS = 1e-5
NBF = np.dtype(ml_dtypes.bfloat16)


class Ctr:
    def __init__(self, sem):
        self.sem = sem
        self.n = 0

    def bump(self, k):
        self.n += k
        return self.n


class SharedCell(dict):
    def __getitem__(self, k):
        return dict.get(self, "x", 0)

    def __setitem__(self, k, v):
        dict.__setitem__(self, "x", v)


class Prog:

    ENGINES = ("sync", "gpsimd", "vector", "scalar", "tensor")

    def __init__(self, nc):
        self.nc = nc
        self.streams = {e: [] for e in self.ENGINES}
        self.waited = {e: {} for e in self.ENGINES}
        self.ctrs = {}
        self.max_waited = {}
        self._es = contextlib.ExitStack()

    def ctr(self, name):
        if name not in self.ctrs:
            sem = self._es.enter_context(self.nc.semaphore(name))
            self.ctrs[name] = Ctr(sem)
        return self.ctrs[name]

    def task(self, eng, f, waits=(), inc=None, inc_k=1):
        eff = []
        waits = list(waits) + list(getattr(self, "default_waits", ()))
        for (c, n) in waits:
            if n is None or n <= 0:
                continue
            key = id(c)
            if self.waited[eng].get(key, -1) >= n:
                continue
            self.waited[eng][key] = n
            eff.append((c.sem, n))
        tgt = inc.bump(inc_k) if inc is not None else None
        sem = inc.sem if inc is not None else None
        for (c, n) in waits:
            if n:
                ent = self.max_waited.setdefault(id(c), [c, 0])
                ent[1] = max(ent[1], n)

        def g(e, f=f, eff=tuple(eff), sem=sem, inc_k=inc_k):
            for (s, n) in eff:
                e.wait_ge(s, n)
            last = f(e)
            if sem is not None:
                assert last is not None, "task emit fn must return its last instruction"
                last.then_inc(sem, inc_k)
        self.streams[eng].append(g)
        return tgt

    def finish(self):
        for key, (c, mw) in self.max_waited.items():
            name = next((n for n, cc in self.ctrs.items() if cc is c), "?")
            assert mw <= c.n, f"wait {mw} exceeds total {c.n} on ctr '{name}'"
        with self.nc.Block() as blk:
            for eng in self.ENGINES:
                fs = self.streams[eng]
                if not fs:
                    continue

                def body(e, fs=fs):
                    for f in fs:
                        f(e)
                getattr(blk, eng)(body)
        self._es.close()


_UID = [0]


def _uid():
    _UID[0] += 1
    return f"t{_UID[0]}_"


def bfc(x):
    return np.ascontiguousarray(np.asarray(x), dtype=NBF)


def f32c(x):
    return np.ascontiguousarray(np.asarray(x), dtype=np.float32)


def u8c(x):
    x = np.ascontiguousarray(np.asarray(x))
    assert x.dtype == np.int32
    return np.ascontiguousarray(x.view(np.uint8)[..., ::4])



def emit_identity(p, ident, one1):
    cQ = p.ctr("q")
    p.task("gpsimd", lambda e: e.memset(ident[:], 0.0), inc=cQ)
    t0 = cQ.n
    p.task("gpsimd", lambda e: e.affine_select(
        out=ident[:], in_=ident[:], compare_op=ALU.not_equal,
        fill=1.0, base=0, pattern=[[-1, 128]], channel_multiplier=1),
        waits=[(cQ, t0)], inc=cQ)
    p.task("gpsimd", lambda e: e.memset(one1[:], 1.0), inc=cQ)
    return (cQ, cQ.n)


def emit_ln_chunk(p, x_ap, xn_ap, dump, cols, width=D, waits=(), prev_v=0,
                  out_ap=None, resid_ap=None):
    cV, cC = p.ctr("v"), p.ctr("c")
    hw = width // 2
    inv = 1.0 / width

    def f_sq(e):
        e.activation(out=dump[:, 0:hw], in_=x_ap[:, 0:hw], func=AF.Square,
                     accum_out=cols[:, 1:2])
        return e.activation(out=dump[:, hw:width], in_=x_ap[:, hw:width],
                            func=AF.Square, accum_out=cols[:, 7:8])
    p.task("scalar", f_sq, waits=list(waits) + [(cV, prev_v)], inc=cC)
    t_sq = cC.n

    def f_v(e):
        e.reduce_sum(out=cols[:, 0:1], in_=x_ap[:], axis=AX.X)
        e.tensor_tensor(out=cols[:, 1:2], in0=cols[:, 1:2], in1=cols[:, 7:8],
                        op=ALU.add)
        e.tensor_scalar(out=cols[:, 2:3], in0=cols[:, 0:1], scalar1=inv,
                        scalar2=None, op0=ALU.mult)
        e.tensor_scalar(out=cols[:, 3:4], in0=cols[:, 1:2], scalar1=inv,
                        scalar2=None, op0=ALU.mult)
        e.tensor_tensor(out=cols[:, 5:6], in0=cols[:, 2:3], in1=cols[:, 2:3],
                        op=ALU.mult)
        e.tensor_tensor(out=cols[:, 3:4], in0=cols[:, 3:4],
                        in1=cols[:, 5:6], op=ALU.subtract)
        return e.tensor_scalar(out=cols[:, 3:4], in0=cols[:, 3:4], scalar1=EPS,
                               scalar2=None, op0=ALU.add)
    p.task("vector", f_v, waits=list(waits) + [(cC, t_sq)], inc=cV)
    t_var = cV.n
    p.task("scalar", lambda e: e.activation(
        out=cols[:, 6:7], in_=cols[:, 3:4], func=AF.Sqrt),
        waits=[(cV, t_var)], inc=cC)
    t_std = cC.n

    if xn_ap is not None:
        def f_n(e):
            e.reciprocal(out=cols[:, 4:5], in_=cols[:, 6:7])
            return e.tensor_scalar(out=xn_ap[:], in0=x_ap[:],
                                   scalar1=cols[:, 2:3], scalar2=cols[:, 4:5],
                                   op0=ALU.subtract, op1=ALU.mult)
    else:
        def f_n(e):
            e.reciprocal(out=cols[:, 4:5], in_=cols[:, 6:7])
            e.tensor_scalar(out=x_ap[:], in0=x_ap[:],
                            scalar1=cols[:, 2:3], scalar2=cols[:, 4:5],
                            op0=ALU.subtract, op1=ALU.mult)
            return e.tensor_tensor(out=out_ap[:], in0=resid_ap[:], in1=x_ap[:],
                                   op=ALU.add)
    p.task("vector", f_n, waits=[(cC, t_std)], inc=cV)
    return cV.n


def emit_tm_to_fm(p, tiles, ident, psum, waits=(), t_ident=None, psum2=None):
    cP, cC, cV = p.ctr("pe"), p.ctr("c"), p.ctr("v")
    if not hasattr(p, "_tr_ev"):
        p._tr_ev = {}
    psums = [psum, psum2 if psum2 is not None else psum]
    last_c, last_v = 0, 0
    nb = (len(tiles) + 3) // 4
    for bi in range(nb):
        grp = tiles[bi * 4:bi * 4 + 4]
        pi = bi % 2
        ps_i = psums[pi]
        key = id(ps_i)

        def f_t(e, grp=grp, ps_i=ps_i):
            last = None
            for j, (src, dst) in enumerate(grp):
                last = e.matmul(out=ps_i[:, j * 128:(j + 1) * 128],
                                lhsT=src, rhs=ident[:], start=True, stop=True)
            return last
        wl = list(waits)
        prev = p._tr_ev.get(key)
        if prev is not None:
            wl.append(prev)
        if t_ident is not None:
            wl.append(t_ident)
        p.task("tensor", f_t, waits=wl, inc=cP)
        t_tr = cP.n

        def f_e(e, grp=grp, ps_i=ps_i):
            last = None
            for j, (src, dst) in enumerate(grp):
                last = e.copy(out=dst, in_=ps_i[:, j * 128:(j + 1) * 128])
            return last
        p.task("scalar", f_e, waits=[(cP, t_tr)], inc=cC)
        p._tr_ev[key] = (cC, cC.n)
        last_c = cC.n
    return last_c



H = 8192
NH = H // 128
GRP = 8
NG = NH // GRP


def build_l0f(nc, stage=99, p=None, io=None, gate=()):
    x_in = io["x_in"] if io is not None else nc.dram_tensor("x_in", [T, D], F32, kind="ExternalInput").ap()
    wup_q = io["wup_q"] if io is not None else nc.dram_tensor("wup_q", [H, D], U8, kind="ExternalInput").ap()
    wup_s = io["wup_s"] if io is not None else nc.dram_tensor("wup_s", [128, NH * 32], F32, kind="ExternalInput").ap()
    bup_c = io["bup_c"] if io is not None else nc.dram_tensor("bup_c", [128, NH], F32, kind="ExternalInput").ap()
    la_t = io["la_t"] if io is not None else nc.dram_tensor("la_t", [128, ND * 16], F32, kind="ExternalInput").ap()
    lora_ub = io["lora_ub"] if io is not None else nc.dram_tensor("lora_ub", [16, H], F32, kind="ExternalInput").ap()
    wdn_q = io["wdn_q"] if io is not None else nc.dram_tensor("wdn_q", [D, H], U8, kind="ExternalInput").ap()
    wdn_s = io["wdn_s"] if io is not None else nc.dram_tensor("wdn_s", [128, ND * 128], F32, kind="ExternalInput").ap()
    b_dn = io["b_dn"] if io is not None else nc.dram_tensor("b_dn", [1, D], F32, kind="ExternalInput").ap()
    da_t = io["da_t"] if io is not None else nc.dram_tensor("da_t", [128, NH * 16], F32, kind="ExternalInput").ap()
    lora_db = io["lora_db"] if io is not None else nc.dram_tensor("lora_db", [16, D], F32, kind="ExternalInput").ap()
    x_out = io["x_out"] if io is not None else nc.dram_tensor("x_out", [T, D], F32, kind="ExternalOutput").ap()

    es = contextlib.ExitStack()
    with es:
        U = _uid()
        sb = lambda name, shape, dt=BF16: es.enter_context(nc.sbuf_tensor(U + name, shape, dt))
        ps = lambda name, shape: es.enter_context(nc.psum_tensor(U + name, shape, F32))
        XR = [sb(f"XR{b}", [128, D], F32) for b in range(2)]
        xf = [sb(f"xf{c}", [128, T]) for c in range(ND)]
        y1 = [sb(f"y1{m}", [128, T]) for m in range(GRP)]
        wub = [sb(f"wub{b}", [128, ND * 128]) for b in range(2)]
        wuraw = [sb(f"wuraw{b}", [128, D], U8) for b in range(2)]
        wdraw = [sb(f"wdraw{b}", [128, GRP * 128], U8) for b in range(2)]
        wdb = [sb(f"wdb{b}", [128, GRP * 512]) for b in range(2)]
        wus = sb("wus", [128, NH * 32], F32)
        wds = sb("wds", [128, ND * 128], F32)
        la = sb("la", [128, ND * 16])
        lbg = [sb(f"lbg{b}", [16, GRP * 128]) for b in range(2)]
        da = sb("da", [128, NH * 16])
        db_t = sb("db", [16, D])
        bup = sb("bup", [128, NH], F32)
        bdn_r = sb("bdn_r", [1, D])
        l1 = sb("l1", [16, T])
        l1d_b = sb("l1d_b", [16, T])
        ident = sb("ident", [128, 128])
        one1 = sb("one1", [1, 128])
        dump = sb("dump", [128, D])
        cols = sb("cols", [128, 8], F32)
        wq_nat = sb("wq_nat", [128, 4 * 1024])
        p_tr = ps("p_tr", [128, 512])
        p_tr2 = ps("p_tr2", [128, 512])
        p_up = [ps(f"p_up{b}", [128, 512]) for b in range(2)]
        p_l1 = ps("p_l1", [16, T])
        p_dn = [ps(f"p_dn{b}", [128, 512]) for b in range(2)]
        xn_es = contextlib.ExitStack()
        xn = [xn_es.enter_context(nc.sbuf_tensor(U + f"xn{i}", [128, D], BF16))
              for i in range(NT)]

        own_prog = p is None
        if own_prog:
            p = Prog(nc)
        p.default_waits = list(gate)
        cV, cC, cP = (p.ctr(n) for n in ("v", "c", "pe"))
        cSc = p.ctr("sc")
        cG = p.ctr("g")
        cXR = [p.ctr("xr0"), p.ctr("xr1")]
        cWU = [p.ctr("wu0"), p.ctr("wu1")]
        cWD = [p.ctr("wd0"), p.ctr("wd1")]
        cLB = [p.ctr("lb0"), p.ctr("lb1")]
        cO = [p.ctr("o0"), p.ctr("o1")]

        t_id = emit_identity(p, ident, one1)
        p.task("sync", lambda e: e.dma_start(out=wus[:], in_=wup_s), inc=cSc, inc_k=16)
        p.task("sync", lambda e: e.dma_start(out=wds[:], in_=wdn_s), inc=cSc, inc_k=16)
        p.task("sync", lambda e: e.dma_start(out=bup[:], in_=bup_c), inc=cSc, inc_k=16)
        t_scales = cSc.n
        for (dst, src) in ((la, la_t), (da, da_t), (db_t, lora_db), (bdn_r, b_dn)):
            p.task("gpsimd", lambda e, dst=dst, src=src: e.dma_start(
                out=dst[:], in_=src), inc=cG, inc_k=16)
        t_const = cG.n

        prev_v = 0
        xr_free = {0: 0, 1: 0}
        for i in range(NT):
            b = i % 2
            p.task("sync", lambda e, i=i, b=b: e.dma_start(
                out=XR[b][:], in_=x_in[i * 128:(i + 1) * 128, :]),
                waits=[(cV, xr_free[b])], inc=cXR[b], inc_k=16)
            t_ld = cXR[b].n
            prev_v = emit_ln_chunk(p, XR[b], xn[i], dump, cols,
                                   waits=[(cXR[b], t_ld)], prev_v=prev_v)
            xr_free[b] = prev_v

        tiles = [(xn[i][:, c * 128:(c + 1) * 128], xf[c][:, i * 128:(i + 1) * 128])
                 for c in range(ND) for i in range(NT)]
        t_xf = emit_tm_to_fm(p, tiles, ident, p_tr,
                             waits=[(cV, prev_v)], t_ident=t_id, psum2=p_tr2)

        xn_es.close()
        Y2 = [sb(f"Y2{i}", [128, D], F32) for i in range(NT)]

        def f_l1(e):
            last = None
            for tb in range(2):
                for k in range(ND):
                    last = e.matmul(out=p_l1[:, tb * 512:(tb + 1) * 512],
                                    lhsT=la[:, k * 16:(k + 1) * 16],
                                    rhs=xf[k][:, tb * 512:(tb + 1) * 512],
                                    start=(k == 0), stop=(k == ND - 1))
            return last
        import os as _os
        if _os.environ.get("SKIP_L1") == "1":
            t_l1 = 0
        else:
            p.task("tensor", f_l1, waits=[(cC, t_xf), (cG, t_const)], inc=cP)
            p.task("scalar", lambda e: e.copy(out=l1[:], in_=p_l1[:]),
                   waits=[(cP, cP.n)], inc=cC)
            t_l1 = cC.n

        up_tiles_done = {}
        wuraw_free = {0: 0, 1: 0}
        wdraw_free = {0: 0, 1: 0}
        wqnat_free = {0: 0, 1: 0, "dn": 0}
        relu_done = {}
        y1_consumed = {}
        y2_evac = {}
        dn_psum_free = {0: None, 1: None}
        lbg_loaded = {}
        lbg_used = {0: 0, 1: 0}
        wub_used = {0: 0, 1: 0}
        wdb_used = {0: 0, 1: 0}

        def sched_up_weight(m):
            UPW_PART = int(os.environ.get("UPW_PART", "3"))
            b = m % 2
            p.task("sync", lambda e, m=m, b=b: e.dma_start(
                out=wuraw[b][:], in_=wup_q[m * 128:(m + 1) * 128, :]),
                waits=[(cV, wuraw_free[b])], inc=cWU[b], inc_k=16)
            t_dma = cWU[b].n
            base = (m % 2) * 2048
            for s4 in range(4):
                def f_dq(e, m=m, b=b, s4=s4, base=base):
                    sc = wus[:, m * 32 + s4 * 8: m * 32 + (s4 + 1) * 8]
                    return e.scalar_tensor_tensor(
                        out=wq_nat[:, base + s4 * 512:base + (s4 + 1) * 512].rearrange(
                            "p (blk f) -> p blk f", blk=8),
                        in0=wuraw[b][:, s4 * 512:(s4 + 1) * 512].rearrange(
                            "p (blk f) -> p blk f", blk=8),
                        scalar=7.5,
                        in1=sc.rearrange("p blk -> p blk ()").broadcast_to([128, 8, 64]),
                        op0=ALU.subtract, op1=ALU.mult)
                p.task("vector", f_dq,
                       waits=[(cWU[b], t_dma), (cSc, t_scales),
                              (cC, wqnat_free[m % 2])], inc=cV)
            wuraw_free[b] = cV.n
            t_dq = cV.n
            tiles = [(wq_nat[:, base + k * 128:base + (k + 1) * 128],
                      wub[b][:, k * 128:(k + 1) * 128]) for k in range(ND)]
            up_tiles_done[m] = emit_tm_to_fm(p, tiles, ident, p_tr,
                                             waits=[(cV, t_dq),
                                                    (cP, wub_used[b])],
                                             t_ident=t_id, psum2=p_tr2)
            wqnat_free[m % 2] = up_tiles_done[m]

        import os
        NO_LORA_MM = os.environ.get("NO_LORA_MM") == "1"
        NO_RELU_BIAS = os.environ.get("NO_RELU_BIAS") == "1"
        UP_MIN = os.environ.get("UP_MIN") == "1"
        RELU_COPY = os.environ.get("RELU_COPY") == "1"

        def sched_up_compute(m, g_idx, gi):
            b = m % 2
            for tb in range(2):
                MMVAR = os.environ.get("MMVAR", "")

                def f_mm(e, m=m, b=b, tb=tb, gi=gi):
                    nk = 1 if UP_MIN else ND
                    if MMVAR == "T1":
                        return e.matmul(out=p_up[b][:, 0:128], lhsT=ident[:],
                                        rhs=ident[:], start=True, stop=True)
                    if MMVAR == "T2":
                        return e.matmul(out=p_up[b][:, 0:128], lhsT=wub[b][:, 0:128],
                                        rhs=ident[:], start=True, stop=True)
                    if MMVAR == "T3":
                        return e.matmul(out=p_up[b][:, 0:512], lhsT=ident[:],
                                        rhs=xf[0][:, 0:512], start=True, stop=True)
                    if MMVAR == "T4":
                        return e.matmul(out=p_up[b][:, 0:256], lhsT=ident[:],
                                        rhs=xf[0][:, 0:256], start=True, stop=True)
                    if MMVAR == "T5":
                        return e.matmul(out=p_up[b][:, 0:512], lhsT=ident[:],
                                        rhs=wub[b][:, 0:512], start=True, stop=True)
                    if MMVAR == "T6":
                        return e.matmul(out=p_up[b][:, 0:384], lhsT=ident[:],
                                        rhs=xf[0][:, 0:384], start=True, stop=True)
                    if MMVAR == "T7":
                        return e.matmul(out=p_up[b][:, 0:500], lhsT=ident[:],
                                        rhs=xf[0][:, 0:500], start=True, stop=True)
                    if MMVAR == "T8":
                        return e.matmul(out=p_tr[:, 0:512], lhsT=ident[:],
                                        rhs=xf[0][:, 0:512], start=True, stop=True)
                    if MMVAR == "T10":
                        return e.matmul(out=p_up[1][:, 0:512], lhsT=ident[:],
                                        rhs=xf[0][:, 0:512], start=True, stop=True)
                    if MMVAR == "T11":
                        return e.matmul(out=p_dn[0][:, 0:512], lhsT=ident[:],
                                        rhs=xf[0][:, 0:512], start=True, stop=True)
                    for k in range(nk):
                        last = e.matmul(out=p_up[tb][:, 0:512],
                                 lhsT=wub[b][:, k * 128:(k + 1) * 128],
                                 rhs=xf[k][:, tb * 512:(tb + 1) * 512],
                                 start=(k == 0), stop=(NO_LORA_MM and k == nk - 1))
                    if NO_LORA_MM:
                        return last
                    mj = m - gi * GRP
                    return e.matmul(out=p_up[tb][:, 0:512],
                                    lhsT=lbg[gi % 2][:, mj * 128:(mj + 1) * 128],
                                    rhs=l1[:, tb * 512:(tb + 1) * 512],
                                    start=False, stop=True)
                p.task("tensor", f_mm,
                       waits=[(cC, max(up_tiles_done[m], t_l1,
                                       relu_done.get((m - 1, tb), 0))),
                              (cLB[gi % 2], lbg_loaded[gi])], inc=cP)
                t_mm = cP.n

                def f_relu(e, m=m, b=b, tb=tb, g_idx=g_idx):
                    if RELU_COPY:
                        return e.copy(out=y1[g_idx][:, tb * 512:(tb + 1) * 512],
                                      in_=p_up[tb][:, 0:512])
                    if NO_RELU_BIAS:
                        return e.activation(out=y1[g_idx][:, tb * 512:(tb + 1) * 512],
                                            in_=p_up[tb][:, 0:512], func=AF.Relu)
                    return e.activation(out=y1[g_idx][:, tb * 512:(tb + 1) * 512],
                                        in_=p_up[tb][:, 0:512], func=AF.Relu,
                                        bias=bup[:, m:m + 1])
                p.task("scalar", f_relu,
                       waits=[(cP, max(t_mm, y1_consumed.get(gi - 1, 0))),
                              (cSc, t_scales)], inc=cC)
                relu_done[(m, tb)] = cC.n
            wub_used[b] = cP.n

        def sched_l1d_mms(gi):
            def f(e, gi=gi):
                last = None
                for mj in range(GRP):
                    m = gi * GRP + mj
                    for tb in range(2):
                        last = e.matmul(out=p_l1[:, tb * 512:(tb + 1) * 512],
                                        lhsT=da[:, m * 16:(m + 1) * 16],
                                        rhs=y1[mj][:, tb * 512:(tb + 1) * 512],
                                        start=(gi == 0 and mj == 0),
                                        stop=(gi == NG - 1 and mj == GRP - 1),
                                        skip_group_check=True)
                return last
            p.task("tensor", f, waits=[(cC, max(relu_done[(gi * GRP + GRP - 1, 1)],
                                                t_l1)), (cG, t_const)], inc=cP)
            return cP.n

        def sched_dn_weight(gi, ds, buf):
            for dci in range(4):
                dc = ds * 4 + dci
                p.task("sync", lambda e, dc=dc, gi=gi, buf=buf: e.dma_start(
                    out=wdraw[buf][:],
                    in_=wdn_q[dc * 128:(dc + 1) * 128,
                              gi * GRP * 128:(gi + 1) * GRP * 128]),
                    waits=[(cV, wdraw_free[buf])], inc=cWD[buf], inc_k=16)
                t_dma = cWD[buf].n

                def f_dq(e, dc=dc, gi=gi, buf=buf, dci=dci):
                    sc = wds[:, dc * 128 + gi * 16: dc * 128 + (gi + 1) * 16]
                    return e.scalar_tensor_tensor(
                        out=wq_nat[:, dci * 1024:(dci + 1) * 1024].rearrange(
                            "p (blk f) -> p blk f", blk=16),
                        in0=wdraw[buf][:].rearrange("p (blk f) -> p blk f", blk=16),
                        scalar=7.5,
                        in1=sc.rearrange("p blk -> p blk ()").broadcast_to([128, 16, 64]),
                        op0=ALU.subtract, op1=ALU.mult)
                p.task("vector", f_dq, waits=[(cWD[buf], t_dma), (cSc, t_scales),
                                              (cC, max(wqnat_free[0], wqnat_free[1],
                                                       wqnat_free["dn"]))], inc=cV)
                wdraw_free[buf] = cV.n
            t_dq = cV.n
            tiles = [(wq_nat[:, dci * 1024 + mj * 128: dci * 1024 + (mj + 1) * 128],
                      wdb[buf][:, mj * 512 + dci * 128: mj * 512 + (dci + 1) * 128])
                     for mj in range(GRP) for dci in range(4)]
            t_w = emit_tm_to_fm(p, tiles, ident, p_tr,
                                waits=[(cV, t_dq), (cP, wdb_used[buf])],
                                t_ident=t_id, psum2=p_tr2)
            wqnat_free[0] = wqnat_free[1] = wqnat_free["dn"] = t_w
            return t_w

        def sched_dn_compute(gi, ds, buf, t_w, t_l1db):
            for tt in range(NT):
                pb = tt % 2

                def f_mm(e, gi=gi, ds=ds, buf=buf, tt=tt, pb=pb):
                    last = None
                    for mj in range(GRP):
                        last = e.matmul(out=p_dn[pb][:, 0:512],
                                        lhsT=y1[mj][:, tt * 128:(tt + 1) * 128],
                                        rhs=wdb[buf][:, mj * 512:(mj + 1) * 512],
                                        start=(mj == 0),
                                        stop=(mj == GRP - 1 and gi != NG - 1))
                    if gi == NG - 1:
                        e.matmul(out=p_dn[pb][:, 0:512],
                                 lhsT=l1d_b[:, tt * 128:(tt + 1) * 128],
                                 rhs=db_t[:, ds * 512:(ds + 1) * 512],
                                 start=False, stop=False)
                        last = e.matmul(out=p_dn[pb][:, 0:512],
                                        lhsT=one1[:],
                                        rhs=bdn_r[:, ds * 512:(ds + 1) * 512],
                                        start=False, stop=True)
                    return last
                wl = [(cC, max(t_w, relu_done[(gi * GRP + GRP - 1, 1)], t_l1db))]
                if dn_psum_free[pb] is not None:
                    wl.append(dn_psum_free[pb])
                prev = y2_evac.get((tt, ds))
                if prev is not None:
                    wl.append(prev)
                p.task("tensor", f_mm, waits=wl, inc=cP)
                t_mm = cP.n

                if gi == 0:
                    p.task("scalar", lambda e, tt=tt, ds=ds, pb=pb: e.copy(
                        out=Y2[tt][:, ds * 512:(ds + 1) * 512], in_=p_dn[pb][:, 0:512]),
                        waits=[(cP, t_mm)], inc=cC)
                    y2_evac[(tt, ds)] = (cC, cC.n)
                    dn_psum_free[pb] = (cC, cC.n)
                else:
                    def f_add(e, tt=tt, ds=ds, pb=pb):
                        return e.tensor_tensor(
                            out=Y2[tt][:, ds * 512:(ds + 1) * 512],
                            in0=Y2[tt][:, ds * 512:(ds + 1) * 512],
                            in1=p_dn[pb][:, 0:512], op=ALU.add)
                    wl2 = [(cP, t_mm)]
                    prev2 = y2_evac.get((tt, ds))
                    if prev2 is not None and prev2[0] is cC:
                        wl2.append(prev2)
                    p.task("vector", f_add, waits=wl2, inc=cV)
                    y2_evac[(tt, ds)] = (cV, cV.n)
                    dn_psum_free[pb] = (cV, cV.n)

        def dbg_store(bufs, n):
            cOg = p.ctr("og")
            for i, bp in enumerate(bufs[:n]):
                sh = bp.ap().shape if hasattr(bp, 'ap') else bp.shape
                p.task("gpsimd", lambda e, i=i, bp=bp, sh=sh: e.dma_start(
                    out=x_out[i * 128:i * 128 + sh[0], 0:sh[1]], in_=bp[:]),
                    waits=[(cV, cV.n), (cC, cC.n), (cP, cP.n)], inc=cOg, inc_k=16)
            p.task("sync", lambda e: e.nop(), waits=[(cOg, cOg.n)])
            if own_prog:
                p.finish()

        if stage == 3:
            dbg_store([l1], 1); return
        t_l1db = 0
        for gi in range(NG):
            bg = gi % 2
            p.task("gpsimd", lambda e, gi=gi, bg=bg: e.dma_start(
                out=lbg[bg][:], in_=lora_ub[:, gi * GRP * 128:(gi + 1) * GRP * 128]),
                waits=[(cP, lbg_used[bg])], inc=cLB[bg], inc_k=16)
            lbg_loaded[gi] = cLB[bg].n
            for mj in range(GRP):
                m = gi * GRP + mj
                if os.environ.get("SKIP_UPW") == "1":
                    up_tiles_done[m] = 0
                else:
                    sched_up_weight(m)
                if stage == 35 and m == 0:
                    dbg_store([wub[0]], 1); return
                sched_up_compute(m, mj, gi)
                if stage == 36 and m == 0:
                    import os as _os
                    dbg_store([wub[0] if _os.environ.get("DBG_WUB") == "1" else y1[0]], 1)
                    return
            lbg_used[bg] = cP.n
            t_lora_acc = sched_l1d_mms(gi)
            if gi == NG - 1:
                p.task("scalar", lambda e: e.copy(out=l1d_b[:], in_=p_l1[:]),
                       waits=[(cP, t_lora_acc)], inc=cC)
                t_l1db = cC.n
            for ds in range(4):
                buf = (gi * 4 + ds) % 2
                t_w = sched_dn_weight(gi, ds, buf)
                sched_dn_compute(gi, ds, buf, t_w, t_l1db)
            y1_consumed[gi] = cP.n
            if stage == 4 and gi == 0:
                dbg_store(y1, GRP); return
            if stage == 5 and gi == 0:
                dbg_store(Y2, NT); return

        if stage == 6:
            dbg_store(Y2, NT); return

        t_y2_all_c = cC.n
        t_y2_all_v = cV.n
        prev_v2 = 0
        xr_store = {0: 0, 1: 0}
        for i in range(NT):
            b = i % 2
            p.task("sync", lambda e, i=i, b=b: e.dma_start(
                out=XR[b][:], in_=x_in[i * 128:(i + 1) * 128, :]),
                waits=[(cO[b], xr_store[b]), (cV, t_y2_all_v if i < 2 else 0)],
                inc=cXR[b], inc_k=16)
            t_ld = cXR[b].n
            tv = emit_ln_chunk(p, Y2[i], None, dump, cols,
                               waits=[(cXR[b], t_ld), (cV, t_y2_all_v),
                                      (cC, t_y2_all_c)],
                               prev_v=prev_v2, out_ap=XR[b], resid_ap=XR[b])
            prev_v2 = tv
            p.task("sync", lambda e, i=i, b=b: e.dma_start(
                out=x_out[i * 128:(i + 1) * 128, :], in_=XR[b][:]),
                waits=[(cV, tv)], inc=cO[b], inc_k=16)
            xr_store[b] = cO[b].n
        p.task("sync", lambda e: e.nop(), waits=[(cO[0], cO[0].n), (cO[1], cO[1].n)])
        if own_prog:
            p.finish()
    return nc



def build_l1f(nc, stage=99, p=None, io=None, gate=()):
    x_in = io["x_in"] if io is not None else nc.dram_tensor("x_in", [T, D], F32, kind="ExternalInput").ap()
    gw_q = io["gw_q"] if io is not None else nc.dram_tensor("gw_q", [H, D], U8, kind="ExternalInput").ap()
    uw_q = io["uw_q"] if io is not None else nc.dram_tensor("uw_q", [H, D], U8, kind="ExternalInput").ap()
    dw_q = io["dw_q"] if io is not None else nc.dram_tensor("dw_q", [D, H], U8, kind="ExternalInput").ap()
    gs_c = io["gs_c"] if io is not None else nc.dram_tensor("gs_c", [128, NH], F32, kind="ExternalInput").ap()
    us_c = io["us_c"] if io is not None else nc.dram_tensor("us_c", [128, NH], F32, kind="ExternalInput").ap()
    ds_c = io["ds_c"] if io is not None else nc.dram_tensor("ds_c", [128, ND], F32, kind="ExternalInput").ap()
    x_out = io["x_out"] if io is not None else nc.dram_tensor("x_out", [T, D], F32, kind="ExternalOutput").ap()

    es = contextlib.ExitStack()
    with es:
        U = _uid()
        sb = lambda name, shape, dt=BF16: es.enter_context(nc.sbuf_tensor(U + name, shape, dt))
        ps = lambda name, shape: es.enter_context(nc.psum_tensor(U + name, shape, F32))
        XR = [sb(f"XR{b}", [128, D], F32) for b in range(2)]
        xf = [sb(f"xf{c}", [128, T]) for c in range(ND)]
        y1 = [sb(f"y1{m}", [128, T]) for m in range(GRP)]
        wgb = [sb(f"wgb{b}", [128, ND * 128]) for b in range(2)]
        wub = [sb(f"wub{b}", [128, ND * 128]) for b in range(2)]
        wgraw = [sb(f"wgraw{b}", [128, D], U8) for b in range(2)]
        wuraw = [sb(f"wuraw{b}", [128, D], U8) for b in range(2)]
        wdraw = [sb(f"wdraw{b}", [128, GRP * 128], U8) for b in range(2)]
        wdb = [sb(f"wdb{b}", [128, GRP * 512]) for b in range(2)]
        gsc = sb("gsc", [128, NH], F32)
        gzc = sb("gzc", [128, NH], F32)
        usc = sb("usc", [128, NH], F32)
        uzc = sb("uzc", [128, NH], F32)
        dsc = sb("dsc", [128, ND], F32)
        dzc = sb("dzc", [128, ND], F32)
        g_sb = [sb(f"g_sb{b}", [128, 512]) for b in range(2)]
        ident = sb("ident", [128, 128])
        one1 = sb("one1", [1, 128])
        dump = sb("dump", [128, D])
        cols = sb("cols", [128, 8], F32)
        wq_nat = sb("wq_nat", [128, 4 * 1024])
        p_tr = ps("p_tr", [128, 512])
        p_tr2 = ps("p_tr2", [128, 512])
        p_g = [ps(f"p_g{b}", [128, 512]) for b in range(2)]
        p_u = [ps(f"p_u{b}", [128, 512]) for b in range(2)]
        p_dn = [ps(f"p_dn{b}", [128, 512]) for b in range(2)]
        xn_es = contextlib.ExitStack()
        xn = [xn_es.enter_context(nc.sbuf_tensor(U + f"xn{i}", [128, D], BF16))
              for i in range(NT)]

        own_prog = p is None
        if own_prog:
            p = Prog(nc)
        p.default_waits = list(gate)
        cV, cC, cP = (p.ctr(n) for n in ("v", "c", "pe"))
        cSc = p.ctr("sc")
        cXR = [p.ctr("xr0"), p.ctr("xr1")]
        cWG = [p.ctr("wg0"), p.ctr("wg1")]
        cWU = [p.ctr("wu0"), p.ctr("wu1")]
        cWD = [p.ctr("wd0"), p.ctr("wd1")]
        cO = [p.ctr("o0"), p.ctr("o1")]

        t_id = emit_identity(p, ident, one1)
        p.task("sync", lambda e: e.dma_start(out=gsc[:], in_=gs_c), inc=cSc, inc_k=16)
        p.task("sync", lambda e: e.dma_start(out=usc[:], in_=us_c), inc=cSc, inc_k=16)
        p.task("sync", lambda e: e.dma_start(out=dsc[:], in_=ds_c), inc=cSc, inc_k=16)
        t_sc_dma = cSc.n

        def f_z(e):
            e.tensor_scalar(out=gzc[:], in0=gsc[:], scalar1=128.0, scalar2=None,
                            op0=ALU.mult)
            e.tensor_scalar(out=uzc[:], in0=usc[:], scalar1=128.0, scalar2=None,
                            op0=ALU.mult)
            return e.tensor_scalar(out=dzc[:], in0=dsc[:], scalar1=128.0, scalar2=None,
                                   op0=ALU.mult)
        p.task("vector", f_z, waits=[(cSc, t_sc_dma)], inc=cV)
        t_scales = cV.n

        prev_v = t_scales
        xr_free = {0: 0, 1: 0}
        for i in range(NT):
            b = i % 2
            p.task("sync", lambda e, i=i, b=b: e.dma_start(
                out=XR[b][:], in_=x_in[i * 128:(i + 1) * 128, :]),
                waits=[(cV, xr_free[b])], inc=cXR[b], inc_k=16)
            t_ld = cXR[b].n
            prev_v = emit_ln_chunk(p, XR[b], xn[i], dump, cols,
                                   waits=[(cXR[b], t_ld)], prev_v=prev_v)
            xr_free[b] = prev_v
        tiles = [(xn[i][:, c * 128:(c + 1) * 128], xf[c][:, i * 128:(i + 1) * 128])
                 for c in range(ND) for i in range(NT)]
        t_xf = emit_tm_to_fm(p, tiles, ident, p_tr, waits=[(cV, prev_v)], t_ident=t_id, psum2=p_tr2)
        xn_es.close()
        Y2 = [sb(f"Y2{i}", [128, D], F32) for i in range(NT)]

        up_tiles_done = {}
        raw_free = {id(wgraw): {0: 0, 1: 0}, id(wuraw): {0: 0, 1: 0}}
        wdraw_free = {0: 0, 1: 0}
        wqnat_free = {0: 0, 1: 0, "dn": 0}
        h_done = {}
        y1_consumed = {}
        y2_evac = {}
        dn_psum_free = {0: None, 1: None}
        wb_used = {0: 0, 1: 0}
        wdb_used = {0: 0, 1: 0}

        def sched_gu_weight(m, raw, wb, q_ap, s_sb, z_sb, ctr, hb):
            b = m % 2
            base = hb * 2048
            p.task("sync", lambda e, m=m, b=b, raw=raw, q_ap=q_ap: e.dma_start(
                out=raw[b][:], in_=q_ap[m * 128:(m + 1) * 128, :]),
                waits=[(cV, raw_free[id(raw)][b])], inc=ctr[b], inc_k=16)
            t_dma = ctr[b].n
            for s4 in range(4):
                def f_dq(e, m=m, b=b, s4=s4, raw=raw, s_sb=s_sb, z_sb=z_sb, base=base):
                    return e.tensor_scalar(
                        out=wq_nat[:, base + s4 * 512:base + (s4 + 1) * 512],
                        in0=raw[b][:, s4 * 512:(s4 + 1) * 512],
                        scalar1=s_sb[:, m:m + 1], scalar2=z_sb[:, m:m + 1],
                        op0=ALU.mult, op1=ALU.subtract)
                p.task("vector", f_dq,
                       waits=[(ctr[b], t_dma), (cV, t_scales),
                              (cC, wqnat_free[hb])], inc=cV)
            raw_free[id(raw)][b] = cV.n
            t_dq = cV.n
            tiles = [(wq_nat[:, base + k * 128:base + (k + 1) * 128],
                      wb[b][:, k * 128:(k + 1) * 128]) for k in range(ND)]
            t = emit_tm_to_fm(p, tiles, ident, p_tr,
                              waits=[(cV, t_dq), (cP, wb_used[b])], t_ident=t_id, psum2=p_tr2)
            wqnat_free[hb] = t
            return t

        def sched_gu_compute(m, g_idx, gi):
            b = m % 2
            for tb in range(2):
                def f_g(e, m=m, b=b, tb=tb):
                    last = None
                    for k in range(ND):
                        last = e.matmul(out=p_g[tb][:, 0:512],
                                        lhsT=wgb[b][:, k * 128:(k + 1) * 128],
                                        rhs=xf[k][:, tb * 512:(tb + 1) * 512],
                                        start=(k == 0), stop=(k == ND - 1))
                    return last
                p.task("tensor", f_g,
                       waits=[(cC, up_tiles_done[m]),
                              (cV, h_done.get((m - 1, tb), 0))],
                       inc=cP)
                t_gmm = cP.n

                def f_silu(e, tb=tb, b=b):
                    return e.activation(out=g_sb[tb][:], in_=p_g[tb][:, 0:512],
                                        func=AF.Silu)
                p.task("scalar", f_silu, waits=[(cP, t_gmm)], inc=cC)
                t_silu = cC.n

                def f_u(e, m=m, b=b, tb=tb):
                    last = None
                    for k in range(ND):
                        last = e.matmul(out=p_u[tb][:, 0:512],
                                        lhsT=wub[b][:, k * 128:(k + 1) * 128],
                                        rhs=xf[k][:, tb * 512:(tb + 1) * 512],
                                        start=(k == 0), stop=(k == ND - 1))
                    return last
                p.task("tensor", f_u,
                       waits=[(cC, up_tiles_done[m]),
                              (cV, h_done.get((m - 1, tb), 0))],
                       inc=cP)
                t_umm = cP.n

                def f_h(e, m=m, tb=tb, g_idx=g_idx):
                    return e.tensor_tensor(out=y1[g_idx][:, tb * 512:(tb + 1) * 512],
                                           in0=g_sb[tb][:], in1=p_u[tb][:, 0:512],
                                           op=ALU.mult)
                p.task("vector", f_h, waits=[(cP, t_umm), (cC, t_silu),
                                             (cP, y1_consumed.get(gi - 1, 0))], inc=cV)
                h_done[(m, tb)] = cV.n
            wb_used[b] = cP.n

        def sched_dn_weight(gi, ds, buf):
            for dci in range(4):
                dc = ds * 4 + dci
                p.task("sync", lambda e, dc=dc, gi=gi, buf=buf: e.dma_start(
                    out=wdraw[buf][:],
                    in_=dw_q[dc * 128:(dc + 1) * 128,
                             gi * GRP * 128:(gi + 1) * GRP * 128]),
                    waits=[(cV, wdraw_free[buf])], inc=cWD[buf], inc_k=16)
                t_dma = cWD[buf].n

                def f_dq(e, dc=dc, buf=buf, dci=dci):
                    return e.tensor_scalar(
                        out=wq_nat[:, dci * 1024:(dci + 1) * 1024],
                        in0=wdraw[buf][:],
                        scalar1=dsc[:, dc:dc + 1], scalar2=dzc[:, dc:dc + 1],
                        op0=ALU.mult, op1=ALU.subtract)
                p.task("vector", f_dq, waits=[(cWD[buf], t_dma), (cV, t_scales),
                                              (cC, max(wqnat_free[0], wqnat_free[1],
                                                       wqnat_free["dn"]))], inc=cV)
                wdraw_free[buf] = cV.n
            t_dq = cV.n
            tiles = [(wq_nat[:, dci * 1024 + mj * 128: dci * 1024 + (mj + 1) * 128],
                      wdb[buf][:, mj * 512 + dci * 128: mj * 512 + (dci + 1) * 128])
                     for mj in range(GRP) for dci in range(4)]
            t_w = emit_tm_to_fm(p, tiles, ident, p_tr,
                                waits=[(cV, t_dq), (cP, wdb_used[buf])], t_ident=t_id, psum2=p_tr2)
            wqnat_free[0] = wqnat_free[1] = wqnat_free["dn"] = t_w
            return t_w

        def sched_dn_compute(gi, ds, buf, t_w):
            for tt in range(NT):
                pb = tt % 2

                def f_mm(e, gi=gi, ds=ds, buf=buf, tt=tt, pb=pb):
                    last = None
                    for mj in range(GRP):
                        last = e.matmul(out=p_dn[pb][:, 0:512],
                                        lhsT=y1[mj][:, tt * 128:(tt + 1) * 128],
                                        rhs=wdb[buf][:, mj * 512:(mj + 1) * 512],
                                        start=(mj == 0), stop=(mj == GRP - 1))
                    return last
                wl = [(cC, t_w), (cV, h_done[(gi * GRP + GRP - 1, 1)])]
                if dn_psum_free[pb] is not None:
                    wl.append(dn_psum_free[pb])
                prev = y2_evac.get((tt, ds))
                if prev is not None:
                    wl.append(prev)
                p.task("tensor", f_mm, waits=wl, inc=cP)
                t_mm = cP.n

                if gi == 0:
                    p.task("scalar", lambda e, tt=tt, ds=ds, pb=pb: e.copy(
                        out=Y2[tt][:, ds * 512:(ds + 1) * 512], in_=p_dn[pb][:, 0:512]),
                        waits=[(cP, t_mm)], inc=cC)
                    y2_evac[(tt, ds)] = (cC, cC.n)
                    dn_psum_free[pb] = (cC, cC.n)
                else:
                    def f_add(e, tt=tt, ds=ds, pb=pb):
                        return e.tensor_tensor(
                            out=Y2[tt][:, ds * 512:(ds + 1) * 512],
                            in0=Y2[tt][:, ds * 512:(ds + 1) * 512],
                            in1=p_dn[pb][:, 0:512], op=ALU.add)
                    wl2 = [(cP, t_mm)]
                    prev2 = y2_evac.get((tt, ds))
                    if prev2 is not None and prev2[0] is cC:
                        wl2.append(prev2)
                    p.task("vector", f_add, waits=wl2, inc=cV)
                    y2_evac[(tt, ds)] = (cV, cV.n)
                    dn_psum_free[pb] = (cV, cV.n)
            wdb_used[buf] = cP.n

        import os as _os

        def dbg_store(bufs, n):
            cOg = p.ctr("og")
            for i, bp in enumerate(bufs[:n]):
                sh = bp.ap().shape if hasattr(bp, 'ap') else bp.shape
                p.task("gpsimd", lambda e, i=i, bp=bp, sh=sh: e.dma_start(
                    out=x_out[i * 128:i * 128 + sh[0], 0:sh[1]], in_=bp[:]),
                    waits=[(cV, cV.n), (cC, cC.n), (cP, cP.n)], inc=cOg, inc_k=16)
            p.task("sync", lambda e: e.nop(), waits=[(cOg, cOg.n)])
            if own_prog:
                p.finish()

        def gu_w(m):
            tg = sched_gu_weight(m, wgraw, wgb, gw_q, gsc, gzc, cWG, 0)
            tu = sched_gu_weight(m, wuraw, wub, uw_q, usc, uzc, cWU, 1)
            up_tiles_done[m] = max(tg, tu)

        gu_w(0)
        dn_w = {}
        for gi in range(NG):
            for mj in range(GRP):
                m = gi * GRP + mj
                if m + 1 < NH:
                    gu_w(m + 1)
                sched_gu_compute(m, mj, gi)
            dn_w[(gi, 0)] = sched_dn_weight(gi, 0, (gi * 4) % 2)
            for ds in range(4):
                buf = (gi * 4 + ds) % 2
                if ds + 1 < 4:
                    dn_w[(gi, ds + 1)] = sched_dn_weight(gi, ds + 1, (gi * 4 + ds + 1) % 2)
                sched_dn_compute(gi, ds, buf, dn_w[(gi, ds)])
            y1_consumed[gi] = cP.n

        t_y2_all_c = cC.n
        t_y2_all_v = cV.n
        prev_v2 = 0
        xr_store = {0: 0, 1: 0}
        for i in range(NT):
            b = i % 2
            p.task("sync", lambda e, i=i, b=b: e.dma_start(
                out=XR[b][:], in_=x_in[i * 128:(i + 1) * 128, :]),
                waits=[(cO[b], xr_store[b]), (cV, t_y2_all_v if i < 2 else 0)],
                inc=cXR[b], inc_k=16)
            t_ld = cXR[b].n
            tv = emit_ln_chunk(p, Y2[i], None, dump, cols,
                               waits=[(cXR[b], t_ld), (cV, t_y2_all_v),
                                      (cC, t_y2_all_c)],
                               prev_v=prev_v2, out_ap=XR[b], resid_ap=XR[b])
            prev_v2 = tv
            p.task("sync", lambda e, i=i, b=b: e.dma_start(
                out=x_out[i * 128:(i + 1) * 128, :], in_=XR[b][:]),
                waits=[(cV, tv)], inc=cO[b], inc_k=16)
            xr_store[b] = cO[b].n
        p.task("sync", lambda e: e.nop(), waits=[(cO[0], cO[0].n), (cO[1], cO[1].n)])
        if own_prog:
            p.finish()
    return nc


NHEAD = 16
HSLAB = 4
NSLAB = NHEAD // HSLAB


def build_l0a(nc, stage=99, p=None, io=None, gate=()):
    x_own = io["x_own"] if io is not None else nc.dram_tensor("x_own", [T, D], F32, kind="ExternalInput").ap()
    x_pre = io["x_pre"] if io is not None else nc.dram_tensor("x_pre", [T, D], F32, kind="ExternalInput").ap()
    wq = io["wq"] if io is not None else nc.dram_tensor("wq", [D, D], F32, kind="ExternalInput").ap()
    wk = io["wk"] if io is not None else nc.dram_tensor("wk", [D, D], F32, kind="ExternalInput").ap()
    wv = io["wv"] if io is not None else nc.dram_tensor("wv", [D, D], F32, kind="ExternalInput").ap()
    wo = io["wo"] if io is not None else nc.dram_tensor("wo", [D, D], F32, kind="ExternalInput").ap()
    wg1 = io["wg1"] if io is not None else nc.dram_tensor("wg1", [D, 128], F32, kind="ExternalInput").ap()
    wg2 = io["wg2"] if io is not None else nc.dram_tensor("wg2", [128, D], F32, kind="ExternalInput").ap()
    nw_c = io["nw_c"] if io is not None else nc.dram_tensor("nw_c", [128, NHEAD], F32, kind="ExternalInput").ap()
    ld_r = io["ld_r"] if io is not None else nc.dram_tensor("ld_r", [1, NHEAD], F32, kind="ExternalInput").ap()
    x_out = io["x_out"] if io is not None else nc.dram_tensor("x_out", [T, D], F32, kind="ExternalOutput").ap()

    es = contextlib.ExitStack()
    with es:
        U = _uid()
        sb = lambda name, shape, dt=BF16: es.enter_context(nc.sbuf_tensor(U + name, shape, dt))
        ps = lambda name, shape, dt=F32: es.enter_context(nc.psum_tensor(U + name, shape, dt))
        XR = [sb(f"XR{b}", [128, D], F32) for b in range(2)]
        xf = [sb(f"xf{c}", [128, T]) for c in range(ND)]
        O_tm = [sb(f"O{c}", [128, D]) for c in range(NT)]
        S_sb = sb("S_sb", [128, NHEAD * 128], F32)
        S_bf = sb("S_bf", [128, NHEAD * 128])
        DTm = sb("DTm", [128, NHEAD * 128])
        ldc = sb("ldc", [128, NHEAD], F32)
        qdec = sb("qdec", [128, NHEAD], F32)
        kdec = sb("kdec", [128, NHEAD], F32)
        lam = sb("lam", [128, NHEAD], F32)
        idxc = sb("idxc", [128, 2], F32)
        diff = sb("diff", [128, 128], F32)
        nwc = sb("nwc", [128, NHEAD], F32)
        wg2_sb = sb("wg2_sb", [128, D])
        wg1_sb = sb("wg1_sb", [128, ND * 128])
        g1 = sb("g1", [128, T])
        ident = sb("ident", [128, 128])
        one1 = sb("one1", [1, 128])
        ld_row = sb("ld_row", [1, NHEAD], F32)
        one1f = sb("one1f", [1, 128], F32)
        dump = sb("dump", [128, D])
        cols = sb("cols", [128, 8], F32)
        gate_sb = sb("gate_sb", [128, T])
        oc_f32 = sb("oc_f32", [128, 128], F32)
        kfm = sb("kfm", [128, 128])
        ksc = sb("ksc", [128, 128])
        scb = sb("scb", [128, 128])
        xnc = sb("xnc", [128, D])
        means = sb("means", [128, NHEAD], F32)
        rst = sb("rst", [128, NHEAD], F32)
        rst2 = sb("rst2", [128, NHEAD], F32)
        p_tr = ps("p_tr", [128, 512])
        p_tr2 = ps("p_tr2", [128, 512])
        p_pj = [ps(f"p_pj{b}", [128, 512]) for b in range(2)]
        p_sc = ps("p_sc", [128, 512])
        p_oi = ps("p_oi", [128, 512])
        p_oe = ps("p_oe", [128, 512])
        p_st = ps("p_st", [128, 512])

        own_prog = p is None
        if own_prog:
            p = Prog(nc)
        p.default_waits = list(gate)
        cV, cC, cP = (p.ctr(n) for n in ("v", "c", "pe"))
        cQ = p.ctr("q")
        cG = p.ctr("g")
        cXR = [p.ctr("xr0"), p.ctr("xr1")]
        cW = [p.ctr("w0"), p.ctr("w1")]
        cO = [p.ctr("o0"), p.ctr("o1")]

        t_id = emit_identity(p, ident, one1)
        p.task("gpsimd", lambda e: e.memset(one1f[:], 1.0), inc=p.ctr("q"))
        t_id = (t_id[0], t_id[0].n)

        cSc = p.ctr("sc")
        cLd = p.ctr("ldr")
        p.task("sync", lambda e: e.dma_start(out=ld_row[:], in_=ld_r), inc=cLd, inc_k=16)
        t_ld = cLd.n

        def f_iota(e):
            e.iota(idxc[:, 0:1], [[1, 1]], base=0, channel_multiplier=1,
                   allow_small_or_imprecise_dtypes=True)
            return e.iota(diff[:], [[1, 128]], base=0, channel_multiplier=-1,
                          allow_small_or_imprecise_dtypes=True)
        p.task("gpsimd", f_iota, inc=cQ)
        t_iota0 = cQ.n
        p.task("gpsimd", lambda e: e.affine_select(
            out=diff[:], in_=diff[:], compare_op=ALU.is_ge, fill=0.0, base=0,
            pattern=[[1, 128]], channel_multiplier=-1),
            waits=[(cQ, t_iota0)], inc=cQ)
        t_iota = cQ.n

        def f_ldc(e):
            return e.matmul(out=p_tr[:, 0:NHEAD], lhsT=one1f[:], rhs=ld_row[:],
                            start=True, stop=True)
        p.task("tensor", f_ldc, waits=[(cLd, t_ld), (cQ, t_id[1])], inc=cP)
        p.task("scalar", lambda e: e.copy(out=ldc[:], in_=p_tr[:, 0:NHEAD]),
               waits=[(cP, cP.n)], inc=cC)
        t_ldc = cC.n

        def f_idx(e):
            return e.tensor_scalar(out=idxc[:, 1:2], in0=idxc[:, 0:1], scalar1=1.0,
                                   scalar2=None, op0=ALU.add)
        p.task("vector", f_idx, waits=[(cQ, t_iota)], inc=cV)

        def f_idx2(e):
            return e.tensor_scalar(out=idxc[:, 0:1], in0=idxc[:, 0:1], scalar1=127.0,
                                   scalar2=-1.0, op0=ALU.subtract, op1=ALU.mult)
        p.task("vector", f_idx2, waits=[(cV, cV.n)], inc=cV)
        t_idx = cV.n

        def f_dec(e):
            e.activation(out=qdec[:], in_=ldc[:], func=AF.Exp, scale=idxc[:, 1:2])
            e.activation(out=kdec[:], in_=ldc[:], func=AF.Exp, scale=idxc[:, 0:1])
            return e.activation(out=lam[:], in_=ldc[:], func=AF.Exp, scale=128.0)
        p.task("scalar", f_dec, waits=[(cC, t_ldc), (cV, t_idx)], inc=cC)
        t_dec = cC.n

        for h in range(NHEAD):
            p.task("scalar", lambda e, h=h: e.activation(
                out=DTm[:, h * 128:(h + 1) * 128], in_=diff[:], func=AF.Exp,
                scale=ldc[:, h:h + 1]), waits=[(cC, t_ldc), (cQ, t_iota)], inc=cC)
            t_e = cC.n
            p.task("gpsimd", lambda e, h=h: e.affine_select(
                out=DTm[:, h * 128:(h + 1) * 128], in_=DTm[:, h * 128:(h + 1) * 128],
                compare_op=ALU.is_ge, fill=0.0, base=0, pattern=[[1, 128]],
                channel_multiplier=-1), waits=[(cC, t_e)], inc=cQ)
        t_dtm = cQ.n

        p.task("gpsimd", lambda e: e.dma_start(out=wg2_sb[:], in_=wg2), inc=cG, inc_k=16)
        p.task("gpsimd", lambda e: e.dma_start(
            out=wg1_sb[:].rearrange("p (k r) -> p k r", k=ND),
            in_=wg1.rearrange("(k p) r -> p k r", p=128)), inc=cG, inc_k=16)
        t_wg = cG.n
        p.task("sync", lambda e: e.dma_start(out=nwc[:], in_=nw_c), inc=cSc, inc_k=16)
        t_nwb = 0
        t_nwc = cSc.n

        def f_s0(e):
            e.memset(S_sb[:], 0.0)
            return e.memset(S_bf[:], 0.0)
        p.task("gpsimd", lambda e: e.memset(S_sb[:], 0.0), inc=cQ)
        p.task("gpsimd", lambda e: e.memset(S_bf[:], 0.0), inc=cQ)
        t_s0 = cQ.n

        wbuf = [sb(f"wbuf{b}", [128, 16 * 512]) for b in range(2)]
        kq_sl = [sb(f"kq{b}", [128, HSLAB * 1024]) for b in range(2)]
        K_sl = [sb(f"Ksl{b}", [128, NT * 512]) for b in range(2)]
        V_sl = [sb(f"Vsl{b}", [128, NT * 512]) for b in range(2)]

        wbuf_used = {0: 0, 1: 0}
        kv_used = {0: (cP, 0), 1: (cP, 0)}
        s_ready = {}
        ln_prev = [0]

        def emit_half_ln(x_ap_dram, out_xf=True):
            prev_v = ln_prev[0]
            for i in range(NT):
                b = i % 2
                p.task("sync", lambda e, i=i, b=b: e.dma_start(
                    out=XR[b][:], in_=x_ap_dram[i * 128:(i + 1) * 128, :]),
                    waits=[(cV, xr_free_ln[b])], inc=cXR[b], inc_k=16)
                t_ldx = cXR[b].n
                prev_v = emit_ln_chunk(p, XR[b], xnc, dump, cols,
                                       waits=[(cXR[b], t_ldx), xnc_free[0]],
                                       prev_v=prev_v)
                xr_free_ln[b] = prev_v
                tiles = [(xnc[:, c * 128:(c + 1) * 128], xf[c][:, i * 128:(i + 1) * 128])
                         for c in range(ND)]
                t_tr = emit_tm_to_fm(p, tiles, ident, p_tr,
                                     waits=[(cV, prev_v)], t_ident=t_id, psum2=p_tr2)
                xnc_free[0] = (cP, p.ctrs["pe"].n)
                prev_v = prev_v
                ln_gate[0] = t_tr
            ln_prev[0] = prev_v
            return ln_gate[0]

        xr_free_ln = {0: 0, 1: 0}
        xnc_free = [(cP, 0)]
        ln_gate = [0]


        def sched_proj_slab(ms, xf_src, own):
            b = ms % 2
            col0 = ms * 512
            projs = [(wk, 0), (wv, 1)] + ([(wq, 2)] if own else [])
            for (w_ap, wi) in projs:
                p.task("gpsimd", lambda e, w_ap=w_ap, wi=wi, col0=col0, b=b: e.dma_start(
                    out=wbuf[b][:, wi * 512 * 0:16 * 512].rearrange(
                        "p (k f) -> p k f", k=16) if False else
                    wbuf[b][:].rearrange("p (k f) -> p k f", k=16),
                    in_=w_ap[:, col0:col0 + 512].rearrange("(k p) f -> p k f", p=128)),
                    waits=[(cP, wbuf_used[b])], inc=cW[b], inc_k=16)
                t_w = cW[b].n
                if wi == 0 or wi == 1:
                    dst = K_sl[b] if wi == 0 else V_sl[b]
                    for c in range(NT):
                        def f_mm(e, c=c, b=b, t_w=t_w):
                            last = None
                            for k in range(ND):
                                last = e.matmul(out=p_pj[c % 2][:, 0:512],
                                                lhsT=xf_src[k][:, c * 128:(c + 1) * 128],
                                                rhs=wbuf[b][:, k * 512:(k + 1) * 512],
                                                start=(k == 0), stop=(k == ND - 1))
                            return last
                        p.task("tensor", f_mm,
                               waits=[(cW[b], t_w),
                                      (cC, max(pj_evac[c % 2], ln_gate[0]))],
                               inc=cP)
                        t_mm = cP.n
                        p.task("scalar", lambda e, c=c, b=b, dst=dst: e.activation(
                            out=dst[:, c * 512:(c + 1) * 512], in_=p_pj[c % 2][:, 0:512],
                            func=AF.Silu), waits=[(cP, t_mm)], inc=cC)
                        pj_evac[c % 2] = cC.n
                else:
                    for hh in range(HSLAB):
                        for tb in range(2):
                            def f_mm(e, hh=hh, tb=tb, b=b, col0=col0, t_w=t_w):
                                last = None
                                for k in range(ND):
                                    last = e.matmul(
                                        out=p_pj[tb % 2][:, 0:512],
                                        lhsT=wbuf[b][:, k * 512 + hh * 128:
                                                     k * 512 + (hh + 1) * 128],
                                        rhs=xf_src[k][:, tb * 512:(tb + 1) * 512],
                                        start=(k == 0), stop=(k == ND - 1))
                                return last
                            p.task("tensor", f_mm,
                                   waits=[(cW[b], t_w),
                                          (cC, max(pj_evac[tb % 2], ln_gate[0]))],
                                   inc=cP)
                            t_mm = cP.n
                            p.task("scalar", lambda e, hh=hh, tb=tb, b=b: e.activation(
                                out=kq_sl[b][:, hh * 1024 + tb * 512:
                                             hh * 1024 + (tb + 1) * 512],
                                in_=p_pj[tb % 2][:, 0:512], func=AF.Silu),
                                waits=[(cP, t_mm)], inc=cC)
                            pj_evac[tb % 2] = cC.n
                wbuf_used[b] = cP.n
            return cC.n

        pj_evac = {0: 0, 1: 0}

        def sched_state_chunk(ms, b, c, t_kv, own):
            for hh in range(HSLAB):
                h = ms * HSLAB + hh
                if own:
                    def f_tr(e, b=b, c=c, hh=hh):
                        return e.matmul(out=p_tr[:, 0:128],
                                        lhsT=K_sl[b][:, c * 512 + hh * 128:
                                                     c * 512 + (hh + 1) * 128],
                                        rhs=ident[:], start=True, stop=True)
                    p.task("tensor", f_tr, waits=[(cC, max(t_kv, kfm_free[0]))], inc=cP)
                    t_trm = cP.n
                    p.task("scalar", lambda e: e.copy(out=kfm[:], in_=p_tr[:, 0:128]),
                           waits=[(cP, t_trm)], inc=cC)
                    t_kfm = cC.n
                    def f_sc(e, b=b, c=c, hh=hh):
                        return e.matmul(out=p_sc[:, 0:128], lhsT=kfm[:],
                                        rhs=kq_sl[b][:, hh * 1024 + c * 128:
                                                     hh * 1024 + (c + 1) * 128],
                                        start=True, stop=True)
                    p.task("tensor", f_sc, waits=[(cC, t_kfm), (cV, scb_free[0])],
                           inc=cP)
                    t_scm = cP.n
                    def f_msk(e, h=h):
                        return e.tensor_tensor(out=scb[:], in0=p_sc[:, 0:128],
                                               in1=DTm[:, h * 128:(h + 1) * 128],
                                               op=ALU.mult)
                    p.task("vector", f_msk, waits=[(cP, t_scm), (cQ, t_dtm)], inc=cV)
                    t_msk = cV.n
                    scb_free[0] = t_msk
                    def f_oi(e, b=b, c=c, hh=hh):
                        return e.matmul(out=p_oi[:, 0:128], lhsT=scb[:],
                                        rhs=V_sl[b][:, c * 512 + hh * 128:
                                                    c * 512 + (hh + 1) * 128],
                                        start=True, stop=True)
                    p.task("tensor", f_oi, waits=[(cV, t_msk), (cV, oi_free[0])], inc=cP)
                    t_oim = cP.n
                    def f_oe(e, b=b, c=c, hh=hh, h=h):
                        return e.matmul(out=p_oe[:, 0:128],
                                        lhsT=kq_sl[b][:, hh * 1024 + c * 128:
                                                      hh * 1024 + (c + 1) * 128],
                                        rhs=S_bf[:, h * 128:(h + 1) * 128],
                                        start=True, stop=True)
                    p.task("tensor", f_oe,
                           waits=[(cV, max(s_ready.get(h, 0), oi_free[0]))], inc=cP)
                    t_oem = cP.n
                    def f_cmb1(e, h=h):
                        return e.tensor_scalar(out=oc_f32[:], in0=p_oe[:, 0:128],
                                               scalar1=qdec[:, h:h + 1], scalar2=None,
                                               op0=ALU.mult)
                    p.task("vector", f_cmb1, waits=[(cP, max(t_oem, t_oim)),
                                                    (cC, t_dec)], inc=cV)

                    def f_cmb2(e, c=c, h=h):
                        return e.tensor_tensor(out=O_tm[c][:, h * 128:(h + 1) * 128],
                                               in0=oc_f32[:], in1=p_oi[:, 0:128],
                                               op=ALU.add)
                    p.task("vector", f_cmb2, waits=[(cV, cV.n)], inc=cV)
                    oi_free[0] = cV.n
                def f_ks(e, b=b, c=c, hh=hh, h=h):
                    return e.tensor_scalar(out=ksc[:],
                                           in0=K_sl[b][:, c * 512 + hh * 128:
                                                       c * 512 + (hh + 1) * 128],
                                           scalar1=kdec[:, h:h + 1], scalar2=None,
                                           op0=ALU.mult)
                p.task("vector", f_ks, waits=[(cC, max(t_kv, t_dec)),
                                              (cP, ksc_free[0])], inc=cV)
                t_ks = cV.n

                def f_st(e, b=b, c=c, hh=hh):
                    return e.matmul(out=p_st[:, 0:128], lhsT=ksc[:],
                                    rhs=V_sl[b][:, c * 512 + hh * 128:
                                                c * 512 + (hh + 1) * 128],
                                    start=True, stop=True)
                p.task("tensor", f_st, waits=[(cV, max(t_ks, st_free[0]))], inc=cP)
                t_stm = cP.n
                ksc_free[0] = cP.n

                def f_su(e, h=h):
                    e.tensor_scalar(out=S_sb[:, h * 128:(h + 1) * 128],
                                    in0=S_sb[:, h * 128:(h + 1) * 128],
                                    scalar1=lam[:, h:h + 1], scalar2=None,
                                    op0=ALU.mult)
                    e.tensor_tensor(out=S_sb[:, h * 128:(h + 1) * 128],
                                    in0=S_sb[:, h * 128:(h + 1) * 128],
                                    in1=p_st[:, 0:128], op=ALU.add)
                    return e.tensor_copy(out=S_bf[:, h * 128:(h + 1) * 128],
                                         in_=S_sb[:, h * 128:(h + 1) * 128])
                wl = [(cP, t_stm), (cC, t_dec)]
                p.task("vector", f_su, waits=wl, inc=cV)
                s_ready[h] = cV.n
                st_free[0] = cV.n

        kfm_free = [0]
        scb_free = [0]
        oi_free = [0]
        ksc_free = [0]
        st_free = [0]

        def dbg_store(bufs, n):
            cOg = p.ctr("og")
            for i, bp in enumerate(bufs[:n]):
                sh = bp.ap().shape if hasattr(bp, 'ap') else bp.shape
                p.task("gpsimd", lambda e, i=i, bp=bp, sh=sh: e.dma_start(
                    out=x_out[i * 128:i * 128 + sh[0], 0:sh[1]], in_=bp[:]),
                    waits=[(cV, cV.n), (cC, cC.n), (cP, cP.n), (cQ, cQ.n)],
                    inc=cOg, inc_k=16)
            p.task("sync", lambda e: e.nop(), waits=[(cOg, cOg.n)])
            if own_prog:
                p.finish()

        emit_half_ln(x_pre)
        for ms in range(NSLAB):
            b = ms % 2
            t_kv = sched_proj_slab(ms, xf, own=False)
            for c in range(NT):
                sched_state_chunk(ms, b, c, t_kv, own=False)
            kv_used[b] = (cP, p.ctrs["pe"].n)
            if stage == 1 and ms == 0:
                dbg_store([S_sb], 1); return
        if stage == 2:
            dbg_store([S_sb], 1); return

        emit_half_ln(x_own)
        def f_g1(e):
            last = None
            for tb in range(2):
                for k in range(ND):
                    last = e.matmul(out=p_pj[0][:, 0:512],
                                    lhsT=wg1_sb[:, k * 128:(k + 1) * 128],
                                    rhs=xf[k][:, tb * 512:(tb + 1) * 512],
                                    start=(k == 0), stop=(k == ND - 1))
            return last
        for tb in range(2):
            def f_g1t(e, tb=tb):
                last = None
                for k in range(ND):
                    last = e.matmul(out=p_pj[0][:, 0:512],
                                    lhsT=wg1_sb[:, k * 128:(k + 1) * 128],
                                    rhs=xf[k][:, tb * 512:(tb + 1) * 512],
                                    start=(k == 0), stop=(k == ND - 1))
                return last
            p.task("tensor", f_g1t, waits=[(cC, max(ln_gate[0], pj_evac[0])),
                                           (cG, t_wg)], inc=cP)
            t_g1 = cP.n
            p.task("scalar", lambda e, tb=tb: e.copy(
                out=g1[:, tb * 512:(tb + 1) * 512], in_=p_pj[0][:, 0:512]),
                waits=[(cP, t_g1)], inc=cC)
            pj_evac[0] = cC.n

        for ms in range(NSLAB):
            b = ms % 2
            t_kv = sched_proj_slab(ms, xf, own=True)
            for c in range(NT):
                sched_state_chunk(ms, b, c, t_kv, own=True)
            kv_used[b] = (cP, p.ctrs["pe"].n)
            if stage == 3 and ms == 0:
                dbg_store(O_tm, 1); return
        if stage == 4:
            dbg_store(O_tm, NT); return

        gn_done = []
        gsb_free = {}
        for c in range(NT):
            t_og_l = 0
            for s4 in range(4):
                def f_gt(e, c=c, s4=s4):
                    return e.matmul(out=p_pj[s4 % 2][:, 0:512],
                                    lhsT=g1[:, c * 128:(c + 1) * 128],
                                    rhs=wg2_sb[:, s4 * 512:(s4 + 1) * 512],
                                    start=True, stop=True)
                p.task("tensor", f_gt, waits=[(cC, max(pj_evac[s4 % 2], 0)),
                                              (cG, t_wg)], inc=cP)
                t_gt = cP.n
                p.task("scalar", lambda e, s4=s4: e.activation(
                    out=gate_sb[:, (s4 % 2) * 512:(s4 % 2) * 512 + 512],
                    in_=p_pj[s4 % 2][:, 0:512], func=AF.Sigmoid),
                    waits=[(cP, t_gt), (cV, gsb_free.get(s4 % 2, 0))], inc=cC)
                pj_evac[s4 % 2] = cC.n
                t_gate = cC.n

                def f_og(e, c=c, s4=s4):
                    return e.tensor_tensor(
                        out=O_tm[c][:, s4 * 512:(s4 + 1) * 512],
                        in0=O_tm[c][:, s4 * 512:(s4 + 1) * 512],
                        in1=gate_sb[:, (s4 % 2) * 512:(s4 % 2) * 512 + 512],
                        op=ALU.mult)
                p.task("vector", f_og, waits=[(cC, t_gate),
                                              (cV, gn_done[-1] if gn_done else 0)],
                       inc=cV)
                gsb_free[s4 % 2] = cV.n
                t_og_l = cV.n
            t_og = t_og_l
            for h in range(NHEAD):
                p.task("scalar", lambda e, c=c, h=h: e.activation(
                    out=dump[:, h * 128:(h + 1) * 128],
                    in_=O_tm[c][:, h * 128:(h + 1) * 128], func=AF.Square,
                    accum_out=means[:, h:h + 1]),
                    waits=[(cV, t_og)], inc=cC)
            t_mean = cC.n

            def f_rst(e):
                return e.tensor_scalar(out=rst2[:], in0=means[:], scalar1=1.0 / 128.0,
                                       scalar2=EPS, op0=ALU.mult, op1=ALU.add)
            p.task("vector", f_rst, waits=[(cC, t_mean)], inc=cV)
            t_v1 = cV.n
            p.task("scalar", lambda e: e.activation(out=means[:], in_=rst2[:],
                                                    func=AF.Sqrt),
                   waits=[(cV, t_v1)], inc=cC)
            t_sq = cC.n
            p.task("vector", lambda e: e.reciprocal(out=rst[:], in_=means[:]),
                   waits=[(cC, t_sq)], inc=cV)
            t_rc = cV.n
            for h in range(NHEAD):
                p.task("vector", lambda e, c=c, h=h: e.tensor_scalar(
                    out=O_tm[c][:, h * 128:(h + 1) * 128],
                    in0=O_tm[c][:, h * 128:(h + 1) * 128],
                    scalar1=rst[:, h:h + 1], scalar2=None, op0=ALU.mult),
                    waits=[(cV, t_rc)], inc=cV)
            gn_done.append(cV.n)

        if stage == 5:
            dbg_store(O_tm, NT); return
        tiles = [(O_tm[i][:, c * 128:(c + 1) * 128], xf[c][:, i * 128:(i + 1) * 128])
                 for c in range(ND) for i in range(NT)]
        t_o4f = emit_tm_to_fm(p, tiles, ident, p_tr,
                              waits=[(cV, gn_done[-1])], t_ident=t_id, psum2=p_tr2)

        delta = O_tm
        wo_evac = {0: 0, 1: 0}
        wo_scaled = {}
        for ms in range(4):
            b = ms % 2
            p.task("gpsimd", lambda e, ms=ms, b=b: e.dma_start(
                out=wbuf[b][:].rearrange("p (k f) -> p k f", k=16),
                in_=wo[:, ms * 512:(ms + 1) * 512].rearrange("(k p) f -> p k f", p=128)),
                waits=[(cP, wbuf_used[b]), (cV, wo_scaled.get(b, 0))],
                inc=cW[b], inc_k=16)
            t_wdma = cW[b].n

            def f_wsc(e, b=b):
                last = None
                for k in range(ND):
                    last = e.tensor_scalar(
                        out=wbuf[b][:, k * 512:(k + 1) * 512],
                        in0=wbuf[b][:, k * 512:(k + 1) * 512],
                        scalar1=nwc[:, k:k + 1], scalar2=None, op0=ALU.mult)
                return last
            p.task("vector", f_wsc, waits=[(cW[b], t_wdma), (cSc, t_nwc)], inc=cV)
            wo_scaled[b] = cV.n
            t_w = None
            for tt in range(NT):
                def f_mm(e, ms=ms, b=b, tt=tt):
                    last = None
                    for k in range(ND):
                        last = e.matmul(out=p_pj[tt % 2][:, 0:512],
                                        lhsT=xf[k][:, tt * 128:(tt + 1) * 128],
                                        rhs=wbuf[b][:, k * 512:(k + 1) * 512],
                                        start=(k == 0), stop=(k == ND - 1))
                    return last
                p.task("tensor", f_mm, waits=[(cV, wo_scaled[b]), (cC, t_o4f),
                                              (cC, wo_evac[tt % 2]),
                                              (cV, gn_done[-1])], inc=cP)
                t_mm = cP.n
                p.task("scalar", lambda e, ms=ms, tt=tt: e.copy(
                    out=delta[tt][:, ms * 512:(ms + 1) * 512],
                    in_=p_pj[tt % 2][:, 0:512]),
                    waits=[(cP, t_mm)], inc=cC)
                wo_evac[tt % 2] = cC.n
            wbuf_used[b] = cP.n
        t_delta = cC.n

        prev_v2 = 0
        xr_store = {0: 0, 1: 0}
        for i in range(NT):
            b = i % 2
            p.task("sync", lambda e, i=i, b=b: e.dma_start(
                out=XR[b][:], in_=x_own[i * 128:(i + 1) * 128, :]),
                waits=[(cO[b], xr_store[b]), (cC, t_delta if i < 2 else 0)],
                inc=cXR[b], inc_k=16)
            t_ldx = cXR[b].n
            tv = emit_ln_postadd_bf16(p, delta[i], XR[b], dump, cols,
                                      waits=[(cXR[b], t_ldx), (cC, t_delta)],
                                      prev_v=prev_v2)
            prev_v2 = tv
            p.task("sync", lambda e, i=i, b=b: e.dma_start(
                out=x_out[i * 128:(i + 1) * 128, :], in_=XR[b][:]),
                waits=[(cV, tv)], inc=cO[b], inc_k=16)
            xr_store[b] = cO[b].n
        p.task("sync", lambda e: e.nop(), waits=[(cO[0], cO[0].n), (cO[1], cO[1].n)])
        if own_prog:
            p.finish()
    return nc


def emit_ln_postadd_bf16(p, y_bf, xr, dump, cols, waits=(), prev_v=0, width=D):
    cV, cC = p.ctr("v"), p.ctr("c")
    hw = width // 2
    inv = 1.0 / width

    def f_sq(e):
        e.activation(out=dump[:, 0:hw], in_=y_bf[:, 0:hw], func=AF.Square,
                     accum_out=cols[:, 1:2])
        return e.activation(out=dump[:, hw:width], in_=y_bf[:, hw:width],
                            func=AF.Square, accum_out=cols[:, 7:8])
    p.task("scalar", f_sq, waits=list(waits) + [(cV, prev_v)], inc=cC)
    t_sq = cC.n

    def f_v(e):
        e.reduce_sum(out=cols[:, 0:1], in_=y_bf[:], axis=AX.X)
        e.tensor_tensor(out=cols[:, 1:2], in0=cols[:, 1:2], in1=cols[:, 7:8],
                        op=ALU.add)
        e.tensor_scalar(out=cols[:, 2:3], in0=cols[:, 0:1], scalar1=inv,
                        scalar2=None, op0=ALU.mult)
        e.tensor_scalar(out=cols[:, 3:4], in0=cols[:, 1:2], scalar1=inv,
                        scalar2=None, op0=ALU.mult)
        e.tensor_tensor(out=cols[:, 5:6], in0=cols[:, 2:3], in1=cols[:, 2:3],
                        op=ALU.mult)
        e.tensor_tensor(out=cols[:, 3:4], in0=cols[:, 3:4], in1=cols[:, 5:6],
                        op=ALU.subtract)
        return e.tensor_scalar(out=cols[:, 3:4], in0=cols[:, 3:4], scalar1=EPS,
                               scalar2=None, op0=ALU.add)
    p.task("vector", f_v, waits=list(waits) + [(cC, t_sq)], inc=cV)
    t_var = cV.n
    p.task("scalar", lambda e: e.activation(out=cols[:, 6:7], in_=cols[:, 3:4],
                                            func=AF.Sqrt),
           waits=[(cV, t_var)], inc=cC)
    t_std = cC.n

    def f_n(e):
        e.reciprocal(out=cols[:, 4:5], in_=cols[:, 6:7])
        e.tensor_scalar(out=dump[:], in0=y_bf[:],
                        scalar1=cols[:, 2:3], scalar2=cols[:, 4:5],
                        op0=ALU.subtract, op1=ALU.mult)
        return e.tensor_tensor(out=xr[:], in0=xr[:], in1=dump[:], op=ALU.add)
    p.task("vector", f_n, waits=[(cC, t_std)], inc=cV)
    return cV.n


NQG = 4
NHD = 8
DH = 64
DKV = 512
SEQ = 2048
NTT = SEQ // 128
HGRP = 4


def build_l1a(nc, stage=99, p=None, io=None, gate=()):
    x_own = io["x_own"] if io is not None else nc.dram_tensor("x_own", [T, D], F32, kind="ExternalInput").ap()
    x_oth = io["x_oth"] if io is not None else nc.dram_tensor("x_oth", [T, D], F32, kind="ExternalInput").ap()
    wqt = io["wqt"] if io is not None else nc.dram_tensor("wqt", [D, D], F32, kind="ExternalInput").ap()
    wkt = io["wkt"] if io is not None else nc.dram_tensor("wkt", [D, DKV], F32, kind="ExternalInput").ap()
    wvt = io["wvt"] if io is not None else nc.dram_tensor("wvt", [D, DKV], F32, kind="ExternalInput").ap()
    mem = io["mem"] if io is not None else nc.dram_tensor("mem", [DKV, DKV], F32, kind="ExternalInput").ap()
    mn_c = io["mn_c"] if io is not None else nc.dram_tensor("mn_c", [128, 4], F32, kind="ExternalInput").ap()
    mw = io["mw"] if io is not None else nc.dram_tensor("mw", [1, 1], F32, kind="ExternalInput").ap()
    x_out = io["x_out"] if io is not None else nc.dram_tensor("x_out", [T, D], F32, kind="ExternalOutput").ap()

    es = contextlib.ExitStack()
    with es:
        U = _uid()
        sb = lambda name, shape, dt=BF16: es.enter_context(nc.sbuf_tensor(U + name, shape, dt))
        ps = lambda name, shape: es.enter_context(nc.psum_tensor(U + name, shape, F32))
        XR = [sb(f"XR{b}", [128, D], F32) for b in range(2)]
        q_fm = [sb(f"qfm{c}", [128, T]) for c in range(ND)]
        K_tm = sb("K_tm", [128, NTT * 512])
        V_bf = sb("V_bf", [128, NTT * 520])
        q_s = [sb(f"q_s{b}", [64, T]) for b in range(2)]
        expT = [sb(f"expT{b}", [128, 512]) for b in range(2)]
        amem_sb = [sb(f"amem{c}", [128, T]) for c in range(2)]
        atmp = sb("atmp", [128, 128], F32)
        mem_sb = sb("mem_sb", [128, 4 * 512])
        mnc_sb = sb("mnc_sb", [128, 4], F32)
        mnc_bf = sb("mnc_bf", [128, 4])
        wcols = sb("wcols", [128, 4], F32)
        mwrow = sb("mwrow", [1, 2], F32)
        dnr = sb("dnr", [1, T], F32)
        rcol = sb("rcol", [128, 4], F32)
        ident = sb("ident", [128, 128])
        one1 = sb("one1", [1, 128])
        one1f = sb("one1f", [1, 128], F32)
        onef = sb("onef", [1, 1], F32)
        dump = sb("dump", [128, D])
        dump2 = sb("dump2", [128, D])
        cols = sb("cols", [128, 8], F32)
        wbuf = [sb(f"wbuf{b}", [128, 16 * 512]) for b in range(2)]
        p_tr = ps("p_tr", [128, 512])
        p_pj = [ps(f"p_pj{b}", [128, 512]) for b in range(2)]
        p_sc = [ps(f"p_sc{b}", [128, 512]) for b in range(2)]
        p_dn = ps("p_dn", [1, T])
        p_pv = ps("p_pv", [128, 260])
        xn_es = contextlib.ExitStack()
        xf = [xn_es.enter_context(nc.sbuf_tensor(U + f"xf{c}", [128, 2 * T], BF16))
              for c in range(ND)]

        own_prog = p is None
        if own_prog:
            p = Prog(nc)
        p.default_waits = list(gate)
        cV, cC, cP = (p.ctr(n) for n in ("v", "c", "pe"))
        cQ = p.ctr("q")
        cG = p.ctr("g")
        cSc = p.ctr("sc")
        cXR = [p.ctr("xr0"), p.ctr("xr1")]
        cW = [p.ctr("w0"), p.ctr("w1")]
        cO = [p.ctr("o0"), p.ctr("o1")]

        t_id = emit_identity(p, ident, one1)
        p.task("gpsimd", lambda e: e.memset(one1f[:], 1.0), inc=cQ)
        p.task("gpsimd", lambda e: e.memset(onef[:], 1.0), inc=cQ)
        p.task("gpsimd", lambda e: e.memset(
            V_bf[:].rearrange("p (tt h s) -> p (tt h) s", tt=NTT, h=8)[:, :, 64:65],
            1.0), inc=cQ)
        t_setup = (cQ, cQ.n)

        p.task("gpsimd", lambda e: e.dma_start(
            out=mem_sb[:].rearrange("p (k f) -> p k f", k=4),
            in_=mem.rearrange("(k p) f -> p k f", p=128)), inc=cG, inc_k=16)
        t_mem = cG.n
        p.task("sync", lambda e: e.dma_start(out=mnc_sb[:], in_=mn_c), inc=cSc, inc_k=16)
        p.task("sync", lambda e: e.dma_start(out=mwrow[:, 0:1], in_=mw), inc=cSc, inc_k=16)
        t_mw = cSc.n
        p.task("vector", lambda e: e.tensor_copy(out=mnc_bf[:], in_=mnc_sb[:]),
               waits=[(cSc, t_mw)], inc=cV)
        t_mnb = cV.n
        p.task("scalar", lambda e: e.activation(out=mwrow[:, 1:2], in_=mwrow[:, 0:1],
                                                func=AF.Sigmoid),
               waits=[(cSc, t_mw)], inc=cC)
        t_sig = cC.n
        p.task("tensor", lambda e: e.matmul(out=p_tr[:, 0:1], lhsT=one1f[:],
                                            rhs=mwrow[:, 1:2], start=True, stop=True),
               waits=[(cC, t_sig), t_setup], inc=cP)
        p.task("scalar", lambda e: e.copy(out=wcols[:, 0:1], in_=p_tr[:, 0:1]),
               waits=[(cP, cP.n)], inc=cC)
        t_w1 = cC.n
        p._tr_evac = t_w1
        p.task("vector", lambda e: e.tensor_scalar(
            out=wcols[:, 1:2], in0=wcols[:, 0:1], scalar1=1.0, scalar2=-1.0,
            op0=ALU.subtract, op1=ALU.mult), waits=[(cC, t_w1)], inc=cV)
        t_wc = cV.n

        prev_v = [0]
        xr_free = {0: 0, 1: 0}
        t_lastpe = [0]

        def half_ln(x_dram, half):
            for i in range(NT):
                b = i % 2
                p.task("sync", lambda e, i=i, b=b, x_dram=x_dram: e.dma_start(
                    out=XR[b][:], in_=x_dram[i * 128:(i + 1) * 128, :]),
                    waits=[(cV, xr_free[b])], inc=cXR[b], inc_k=16)
                t_ld = cXR[b].n
                pv = emit_ln_chunk(p, XR[b], dump, dump2, cols,
                                   waits=[(cXR[b], t_ld), (cP, t_lastpe[0])],
                                   prev_v=prev_v[0])
                prev_v[0] = pv
                xr_free[b] = pv
                tiles = [(dump[:, c * 128:(c + 1) * 128],
                          xf[c][:, half * T + i * 128: half * T + (i + 1) * 128])
                         for c in range(ND)]
                emit_tm_to_fm(p, tiles, ident, p_tr, waits=[(cV, pv)], t_ident=t_id)
                t_lastpe[0] = p.ctrs["pe"].n

        half_ln(x_own, 0)
        half_ln(x_oth, 1)
        t_xf = p.ctrs["c"].n

        wbuf_used = SharedCell()
        pj_evac = {0: 0, 1: 0}

        def stream_w(w_ap, col0, width, b):
            p.task("gpsimd", lambda e, w_ap=w_ap, col0=col0, width=width, b=b:
                   e.dma_start(
                       out=wbuf[b][:, 0:16 * width].rearrange("p (k f) -> p k f", k=16),
                       in_=w_ap[:, col0:col0 + width].rearrange("(k p) f -> p k f", p=128)),
                   waits=[(cP, wbuf_used[b])], inc=cW[b], inc_k=16)
            return cW[b].n

        t_wk = stream_w(wkt, 0, 512, 0)
        for tt in range(NTT):
            def f_mm(e, tt=tt, t_wk=t_wk):
                last = None
                for k in range(ND):
                    last = e.matmul(out=p_pj[tt % 2][:, 0:512],
                                    lhsT=xf[k][:, tt * 128:(tt + 1) * 128],
                                    rhs=wbuf[0][:, k * 512:(k + 1) * 512],
                                    start=(k == 0), stop=(k == ND - 1))
                return last
            p.task("tensor", f_mm, waits=[(cW[0], t_wk),
                                          (cC, max(pj_evac[tt % 2], t_xf))], inc=cP)
            t_mm = cP.n
            p.task("scalar", lambda e, tt=tt: e.copy(
                out=K_tm[:, tt * 512:(tt + 1) * 512], in_=p_pj[tt % 2][:, 0:512]),
                waits=[(cP, t_mm)], inc=cC)
            pj_evac[tt % 2] = cC.n
        wbuf_used[0] = cP.n

        t_wv = stream_w(wvt, 0, 512, 1)
        for tt in range(NTT):
            def f_mm(e, tt=tt, t_wv=t_wv):
                last = None
                for k in range(ND):
                    last = e.matmul(out=p_pj[tt % 2][:, 0:512],
                                    lhsT=xf[k][:, tt * 128:(tt + 1) * 128],
                                    rhs=wbuf[1][:, k * 512:(k + 1) * 512],
                                    start=(k == 0), stop=(k == ND - 1))
                return last
            p.task("tensor", f_mm, waits=[(cW[1], t_wv),
                                          (cC, max(pj_evac[tt % 2], t_xf))], inc=cP)
            t_mm = cP.n

            def f_ev(e, tt=tt):
                return e.copy(
                    out=V_bf[:, tt * 520:(tt + 1) * 520].rearrange(
                        "p (h s) -> p h s", h=8)[:, :, 0:64],
                    in_=p_pj[tt % 2][:, 0:512].rearrange("p (h s) -> p h s", h=8))
            p.task("scalar", f_ev, waits=[(cP, t_mm)], inc=cC)
            pj_evac[tt % 2] = cC.n
        wbuf_used[1] = cP.n

        for ms in range(4):
            b = ms % 2
            t_wq = stream_w(wqt, ms * 512, 512, b)
            for ec in range(4):
                c = ms * 4 + ec
                for tb in range(2):
                    def f_mm(e, c=c, ec=ec, b=b, tb=tb, t_wq=t_wq):
                        last = None
                        for k in range(ND):
                            last = e.matmul(
                                out=p_pj[tb % 2][:, 0:512],
                                lhsT=wbuf[b][:, k * 512 + ec * 128:
                                             k * 512 + (ec + 1) * 128],
                                rhs=xf[k][:, tb * 512:(tb + 1) * 512],
                                start=(k == 0), stop=(k == ND - 1))
                        return last
                    p.task("tensor", f_mm, waits=[(cW[b], t_wq),
                                                  (cC, max(pj_evac[tb % 2], t_xf))],
                           inc=cP)
                    t_mm = cP.n
                    p.task("scalar", lambda e, c=c, tb=tb: e.copy(
                        out=q_fm[c][:, tb * 512:(tb + 1) * 512],
                        in_=p_pj[tb % 2][:, 0:512]),
                        waits=[(cP, t_mm)], inc=cC)
                    pj_evac[tb % 2] = cC.n
            wbuf_used[b] = cP.n
        t_qkv = cC.n

        def dbg_store(aps):
            cOg = p.ctr("og")
            for i, ap2 in enumerate(aps):
                sh = ap2.shape
                p.task("gpsimd", lambda e, i=i, ap2=ap2, sh=sh: e.dma_start(
                    out=x_out[i * 128:i * 128 + sh[0], 0:sh[1]], in_=ap2),
                    waits=[(cV, cV.n), (cC, cC.n), (cP, cP.n)], inc=cOg, inc_k=16)
            p.task("sync", lambda e: e.nop(), waits=[(cOg, cOg.n)])
            if own_prog:
                p.finish()

        if stage in (1, 2, 3):
            xn_es.close()
            if stage == 1:
                dbg_store([K_tm[:, j * 2048:(j + 1) * 2048] for j in range(4)])
            elif stage == 2:
                dbg_store([V_bf[:, j * 520:j * 520 + 520] for j in range(4)])
            else:
                dbg_store([q_fm[c][:] for c in range(8)])
            return

        xn_es.close()
        ADO = [sb(f"ADO{i}", [128, D]) for i in range(NT)]
        K_F = [sb(f"KF{j}", [64, SEQ]) for j in range(HGRP)]

        cQS = [p.ctr("qsd0"), p.ctr("qsd1")]
        ktr_evac = [0]
        exp_done = {0: 0, 1: 0}
        expT_free = {0: 0, 1: 0}
        pv_free = [0]

        for hg in range(2):
            for j in range(HGRP):
                h = hg * HGRP + j
                for tt in range(NTT):
                    def f_tr(e, h=h, tt=tt):
                        return e.matmul(out=p_tr[0:64, 0:128],
                                        lhsT=K_tm[:, tt * 512 + (h % 8) * 64:
                                                  tt * 512 + (h % 8) * 64 + 64],
                                        rhs=ident[:], start=True, stop=True)
                    p.task("tensor", f_tr, waits=[(cC, max(t_qkv, ktr_evac[0])),
                                                  (cV, pv_free[0])],
                           inc=cP)
                    t_trm = cP.n
                    p.task("scalar", lambda e, j=j, tt=tt: e.copy(
                        out=K_F[j][:, tt * 128:(tt + 1) * 128],
                        in_=p_tr[0:64, 0:128]),
                        waits=[(cP, t_trm)], inc=cC)
                    ktr_evac[0] = cC.n
            t_kf = cC.n
            combos = [(qg, hg * HGRP + j) for qg in range(NQG) for j in range(HGRP)]

            def sched_qs(ci):
                qg, h = combos[ci]
                ec = (qg * 512 + h * 64) // 128
                off = (qg * 512 + h * 64) % 128
                qb = ci % 2
                p.task("gpsimd", lambda e, ec=ec, off=off, qb=qb: e.dma_start(
                    out=q_s[qb][:], in_=q_fm[ec][off:off + 64, :]),
                    waits=[(cC, t_qkv), (cP, qs_used[qb])], inc=cQS[qb], inc_k=16)
                qs_t[ci] = cQS[qb].n

            qs_t = {}
            qs_used = {0: cP.n, 1: cP.n}
            sched_qs(0)
            for ci in range(len(combos)):
                qg, h = combos[ci]
                j = h - hg * HGRP
                qb = ci % 2
                if ci + 1 < len(combos):
                    sched_qs(ci + 1)
                t_qs = qs_t[ci]
                if True:
                    for sb_i in range(2):
                        t_scm = {}
                        t_exp_d = {}

                        def sched_sc(tt):
                            pb = tt % 2

                            def f_sc(e, j=j, sb_i=sb_i, tt=tt, pb=pb, qb=qb):
                                return e.matmul(
                                    out=p_sc[pb][:, 0:512],
                                    lhsT=K_F[j][:, tt * 128:(tt + 1) * 128],
                                    rhs=q_s[qb][:, sb_i * 512:(sb_i + 1) * 512],
                                    start=True, stop=True)
                            p.task("tensor", f_sc,
                                   waits=[(cQS[qb], t_qs),
                                          (cC, max(t_kf, exp_done[pb])),
                                          (cV, pv_free[0] if tt == 0 else 0)],
                                   inc=cP)
                            t_scm[tt] = cP.n

                        def sched_exp(tt):
                            pb = tt % 2

                            def f_exp(e, pb=pb):
                                return e.activation(out=expT[pb][:],
                                                    in_=p_sc[pb][:, 0:512],
                                                    func=AF.Exp, scale=0.125)
                            p.task("scalar", f_exp,
                                   waits=[(cP, max(t_scm[tt], expT_free[pb]))],
                                   inc=cC)
                            exp_done[pb] = cC.n
                            t_exp_d[tt] = cC.n

                        sched_sc(0)
                        sched_exp(0)
                        for tt in range(NTT):
                            pb = tt % 2
                            if tt + 1 < NTT:
                                sched_sc(tt + 1)
                                sched_exp(tt + 1)

                            def f_pv(e, tt=tt, pb=pb, h=h):
                                last = None
                                pv_banks = [p_pv, p_pj[0], p_pj[1], p_tr]
                                for sj in range(4):
                                    last = e.matmul(
                                        out=pv_banks[sj][:, 0:65],
                                        lhsT=expT[pb][:, sj * 128:(sj + 1) * 128],
                                        rhs=V_bf[:, tt * 520 + (h % 8) * 65:
                                                 tt * 520 + (h % 8) * 65 + 65],
                                        start=(tt == 0), stop=(tt == NTT - 1),
                                        skip_group_check=True)
                                return last
                            p.task("tensor", f_pv, waits=[(cC, t_exp_d[tt]), t_setup],
                                   inc=cP)
                            expT_free[pb] = cP.n
                        t_pvm = cP.n

                        def f_ev(e, qg=qg, h=h, sb_i=sb_i):
                            last = None
                            pv_banks = [p_pv, p_pj[0], p_pj[1], p_tr]
                            for sj in range(4):
                                e.reciprocal(out=rcol[:, sj:sj + 1],
                                             in_=pv_banks[sj][:, 64:65])
                            for sj in range(4):
                                st = sb_i * 4 + sj
                                last = e.tensor_scalar(
                                    out=ADO[st][:, qg * 512 + h * 64:
                                                qg * 512 + (h + 1) * 64],
                                    in0=pv_banks[sj][:, 0:64],
                                    scalar1=rcol[:, sj:sj + 1], scalar2=None,
                                    op0=ALU.mult)
                            return last
                        p.task("vector", f_ev, waits=[(cP, t_pvm)], inc=cV)
                        pv_free[0] = cV.n
                qs_used[qb] = cP.n
        t_attn_v = cV.n
        if stage == 4:
            dbg_store([ADO[i][:] for i in range(NT)]); return

        for i in range(NT):
            p.task("vector", lambda e, i=i: e.tensor_scalar(
                out=ADO[i][:], in0=ADO[i][:], scalar1=wcols[:, 1:2], scalar2=None,
                op0=ALU.mult), waits=[(cV, max(t_attn_v, t_wc))], inc=cV)
        t_ado_w = cV.n

        elu_done = 0
        for c2 in range(ND):
            def f_e1(e, c2=c2):
                return e.tensor_scalar(out=dump[:, 0:T], in0=q_fm[c2][:],
                                       scalar1=0.0, scalar2=None, op0=ALU.min)
            p.task("vector", f_e1, waits=[(cV, t_ado_w), (cC, elu_done)], inc=cV)
            tv1 = cV.n
            p.task("scalar", lambda e: e.activation(out=dump[:, 0:T], in_=dump[:, 0:T],
                                                    func=AF.Exp),
                   waits=[(cV, tv1)], inc=cC)
            tc1 = cC.n

            def f_e2(e, c2=c2):
                return e.scalar_tensor_tensor(out=q_fm[c2][:], in0=dump[:, 0:T],
                                              scalar=1.0, in1=q_fm[c2][:],
                                              op0=ALU.subtract, op1=ALU.max)
            p.task("vector", f_e2, waits=[(cC, tc1)], inc=cV)
            elu_done = tc1
        t_elu = cV.n

        amem_evac = {0: 0, 1: 0}
        dn_evac = [0]
        cmb_done = [0]
        for qg in range(NQG):
            def f_dn(e, qg=qg):
                last = None
                for half in range(2):
                    for k in range(4):
                        last = e.matmul(
                            out=p_dn[0:1, half * 512:(half + 1) * 512],
                            lhsT=mnc_bf[:, k:k + 1],
                            rhs=q_fm[qg * 4 + k][:, half * 512:(half + 1) * 512],
                            start=(k == 0), stop=(k == 3))
                return last
            p.task("tensor", f_dn, waits=[(cV, max(t_elu, t_mnb)),
                                          (cV, dn_evac[0])], inc=cP)
            t_dnm = cP.n
            p.task("vector", lambda e: e.tensor_copy(out=dnr[:], in_=p_dn[:]),
                   waits=[(cP, t_dnm)], inc=cV)
            dn_evac[0] = cV.n
            t_dnr = cV.n

            for e2g in range(2):
                for e2i in range(2):
                    e2c = e2g * 2 + e2i
                    pb = e2i
                    for half in range(2):
                        def f_amh(e, qg=qg, e2c=e2c, pb=pb, half=half):
                            last = None
                            for k in range(4):
                                last = e.matmul(
                                    out=p_pj[pb][:, 0:512],
                                    lhsT=mem_sb[:, k * 512 + e2c * 128:
                                                k * 512 + (e2c + 1) * 128],
                                    rhs=q_fm[qg * 4 + k][:, half * 512:(half + 1) * 512],
                                    start=(k == 0), stop=(k == 3))
                            return last
                        p.task("tensor", f_amh, waits=[(cV, t_elu), (cG, t_mem),
                                                       (cC, amem_evac[pb]),
                                                       (cV, cmb_done[0])], inc=cP)
                        t_amm = cP.n
                        p.task("scalar", lambda e, e2i=e2i, half=half, pb=pb: e.copy(
                            out=amem_sb[e2i][:, half * 512:(half + 1) * 512],
                            in_=p_pj[pb][:, 0:512]),
                            waits=[(cP, t_amm)], inc=cC)
                        amem_evac[pb] = cC.n
                t_amem = cC.n

                for st in range(NT):
                    def f_dc(e, st=st):
                        return e.matmul(out=p_tr[:, 256:257],
                                        lhsT=dnr[0:1, st * 128:(st + 1) * 128],
                                        rhs=onef[:], start=True, stop=True)
                    p.task("tensor", f_dc, waits=[(cV, t_dnr), t_setup,
                                                  (cV, cmb_done[0])], inc=cP)
                    t_dcm = cP.n

                    p.task("vector", lambda e: e.reciprocal(
                        out=rcol[:, 1:2], in_=p_tr[:, 256:257]),
                        waits=[(cP, t_dcm)], inc=cV)
                    t_rc0 = cV.n
                    p.task("vector", lambda e: e.tensor_tensor(
                        out=rcol[:, 2:3], in0=rcol[:, 1:2], in1=wcols[:, 0:1],
                        op=ALU.mult), waits=[(cV, max(t_rc0, t_wc))], inc=cV)
                    t_rc = cV.n
                    for e2i in range(2):
                        e2c = e2g * 2 + e2i

                        def f_tr2(e, e2i=e2i, st=st):
                            return e.matmul(out=p_tr[:, 0:128],
                                            lhsT=amem_sb[e2i][:, st * 128:(st + 1) * 128],
                                            rhs=ident[:], start=True, stop=True)
                        p.task("tensor", f_tr2, waits=[(cC, t_amem),
                                                       (cV, max(cmb_done[0], t_rc))],
                               inc=cP)
                        t_trm = cP.n

                        def f_cmb(e, qg=qg, e2c=e2c, st=st):
                            e.tensor_scalar(out=atmp[:], in0=p_tr[:, 0:128],
                                            scalar1=rcol[:, 2:3], scalar2=None,
                                            op0=ALU.mult)
                            dst = ADO[st][:, qg * 512 + e2c * 128:
                                          qg * 512 + (e2c + 1) * 128]
                            return e.tensor_tensor(out=dst, in0=dst, in1=atmp[:],
                                                   op=ALU.add)
                        p.task("vector", f_cmb, waits=[(cP, t_trm), (cV, t_rc),
                                                       (cV, t_ado_w)], inc=cV)
                        cmb_done[0] = cV.n

        t_all_v = cV.n
        if stage == 5:
            dbg_store([ADO[i][:] for i in range(NT)]); return
        prev_v2 = 0
        xr_store = {0: 0, 1: 0}
        for i in range(NT):
            b = i % 2
            p.task("sync", lambda e, i=i, b=b: e.dma_start(
                out=XR[b][:], in_=x_own[i * 128:(i + 1) * 128, :]),
                waits=[(cO[b], xr_store[b]), (cV, t_all_v if i < 2 else 0)],
                inc=cXR[b], inc_k=16)
            t_ld = cXR[b].n
            tv = emit_ln_postadd_bf16(p, ADO[i], XR[b], dump, cols,
                                      waits=[(cXR[b], t_ld), (cV, t_all_v)],
                                      prev_v=prev_v2)
            prev_v2 = tv
            p.task("sync", lambda e, i=i, b=b: e.dma_start(
                out=x_out[i * 128:(i + 1) * 128, :], in_=XR[b][:]),
                waits=[(cV, tv)], inc=cO[b], inc_k=16)
            xr_store[b] = cO[b].n
        p.task("sync", lambda e: e.nop(), waits=[(cO[0], cO[0].n), (cO[1], cO[1].n)])
        if own_prog:
            p.finish()
    return nc



def _shards(x):
    v = np.asarray(x).reshape(4, 2, 1024, 2048)
    return [v[c // 2, c % 2] for c in range(8)]


def kernel_multi(**inp):
    x = f32c(inp["L0_attn_x"])
    xs = _shards(x)
    zeros = np.zeros((1024, 2048), np.float32)

    nc = bass.Bass("TRN2", target_bir_lowering=False)
    build_l0a(nc)
    com = dict(
        wq=f32c(inp["L0_attn_Wq"]), wk=f32c(inp["L0_attn_Wk"]),
        wv=f32c(inp["L0_attn_Wv"]), wo=f32c(inp["L0_attn_Wo"]),
        wg1=f32c(inp["L0_attn_Wg1"]), wg2=f32c(inp["L0_attn_Wg2"]),
        nw_c=f32c(np.asarray(inp["L0_attn_norm_w"]).reshape(16, 128).T),
        ld_r=f32c(np.asarray(inp["L0_attn_log_decay"]).reshape(1, -1)),
    )
    in_maps = []
    for c in range(8):
        pre = zeros if c % 2 == 0 else xs[c - 1]
        in_maps.append(dict(x_own=xs[c], x_pre=f32c(pre), **com))
    res = run_bass_kernel_spmd(nc, in_maps, core_ids=list(range(8)))
    ys = [res.results[c]["x_out"] for c in range(8)]

    nc = bass.Bass("TRN2", target_bir_lowering=False)
    build_l0f(nc)
    com = dict(
        wup_q=u8c(inp["L0_ffn_w_up_q"]),
        wup_s=f32c(np.asarray(inp["L0_ffn_w_up_scale"]).reshape(64, 128, 32)
                   .transpose(1, 0, 2).reshape(128, -1)),
        bup_c=f32c(np.asarray(inp["L0_ffn_b_up"]).reshape(64, 128).T),
        la_t=f32c(np.asarray(inp["L0_ffn_w_up_lora_a"]).reshape(16, 128, 16)
                  .transpose(1, 0, 2).reshape(128, -1)),
        lora_ub=f32c(inp["L0_ffn_w_up_lora_b"]),
        wdn_q=u8c(inp["L0_ffn_w_down_q"]),
        wdn_s=f32c(np.asarray(inp["L0_ffn_w_down_scale"]).reshape(16, 128, 128)
                   .transpose(1, 0, 2).reshape(128, -1)),
        b_dn=f32c(np.asarray(inp["L0_ffn_b_down"]).reshape(1, -1)),
        da_t=f32c(np.asarray(inp["L0_ffn_w_down_lora_a"]).reshape(64, 128, 16)
                  .transpose(1, 0, 2).reshape(128, -1)),
        lora_db=f32c(inp["L0_ffn_w_down_lora_b"]),
    )
    res = run_bass_kernel_spmd(nc, [dict(x_in=f32c(ys[c]), **com) for c in range(8)],
                               core_ids=list(range(8)))
    zs = [res.results[c]["x_out"] for c in range(8)]

    nc = bass.Bass("TRN2", target_bir_lowering=False)
    build_l1a(nc)
    com = dict(
        wqt=f32c(np.asarray(inp["L1_attn_Wq"]).T),
        wkt=f32c(np.asarray(inp["L1_attn_Wk"]).T),
        wvt=f32c(np.asarray(inp["L1_attn_Wv"]).T),
        mem=f32c(inp["L1_attn_memory"]),
        mn_c=f32c(np.asarray(inp["L1_attn_memory_norm"]).reshape(4, 128).T),
        mw=f32c(np.asarray(inp["L1_attn_memory_weight"]).reshape(1, 1)),
    )
    in_maps = []
    for c in range(8):
        oth = zs[c + 1] if c % 2 == 0 else zs[c - 1]
        in_maps.append(dict(x_own=f32c(zs[c]), x_oth=f32c(oth), **com))
    res = run_bass_kernel_spmd(nc, in_maps, core_ids=list(range(8)))
    us = [res.results[c]["x_out"] for c in range(8)]

    nc = bass.Bass("TRN2", target_bir_lowering=False)
    build_l1f(nc)
    com = dict(
        gw_q=u8c(inp["L1_ffn_gate_w"]),
        uw_q=u8c(inp["L1_ffn_up_w"]),
        dw_q=u8c(inp["L1_ffn_down_w"]),
        gs_c=f32c(np.asarray(inp["L1_ffn_gate_scale"]).reshape(64, 128).T),
        us_c=f32c(np.asarray(inp["L1_ffn_up_scale"]).reshape(64, 128).T),
        ds_c=f32c(np.asarray(inp["L1_ffn_down_scale"]).reshape(16, 128).T),
    )
    res = run_bass_kernel_spmd(nc, [dict(x_in=f32c(us[c]), **com) for c in range(8)],
                               core_ids=list(range(8)))
    outs = [res.results[c]["x_out"] for c in range(8)]

    out = np.stack(outs, axis=0).reshape(4, 2, 1024, 2048).reshape(4, 2048, 2048)
    return np.ascontiguousarray(out, dtype=np.float32)



def build_fused(nc):
    ext = lambda n, sh, dt=F32: nc.dram_tensor(n, sh, dt, kind="ExternalInput").ap()
    x_own = ext("x_own", [T, D])
    x_par = ext("x_par", [T, D])
    xpre_own = ext("xpre_own", [T, D])
    xpre_par = ext("xpre_par", [T, D])
    l0a_w = dict(wq=ext("wq", [D, D]), wk=ext("wk", [D, D]), wv=ext("wv", [D, D]),
                 wo=ext("wo", [D, D]), wg1=ext("wg1", [D, 128]),
                 wg2=ext("wg2", [128, D]), nw_c=ext("nw_c", [128, NHEAD]),
                 ld_r=ext("ld_r", [1, NHEAD]))
    l0f_w = dict(wup_q=ext("wup_q", [H, D], U8), wup_s=ext("wup_s", [128, NH * 32]),
                 bup_c=ext("bup_c", [128, NH]), la_t=ext("la_t", [128, ND * 16]),
                 lora_ub=ext("lora_ub", [16, H]), wdn_q=ext("wdn_q", [D, H], U8),
                 wdn_s=ext("wdn_s", [128, ND * 128]), b_dn=ext("b_dn", [1, D]),
                 da_t=ext("da_t", [128, NH * 16]), lora_db=ext("lora_db", [16, D]))
    l1a_w = dict(wqt=ext("wqt", [D, D]), wkt=ext("wkt", [D, DKV]),
                 wvt=ext("wvt", [D, DKV]), mem=ext("mem", [DKV, DKV]),
                 mn_c=ext("mn_c", [128, 4]), mw=ext("mw", [1, 1]))
    l1f_w = dict(gw_q=ext("gw_q", [H, D], U8), uw_q=ext("uw_q", [H, D], U8),
                 dw_q=ext("dw_q", [D, H], U8), gs_c=ext("gs_c", [128, NH]),
                 us_c=ext("us_c", [128, NH]), ds_c=ext("ds_c", [128, ND]))
    x_out = nc.dram_tensor("x_out", [T, D], F32, kind="ExternalOutput").ap()
    scr = {n: nc.dram_tensor(n, [T, D], F32).ap()
           for n in ("y_own", "y_par", "z_own", "z_par", "u_own")}

    p = Prog(nc)

    def gate():
        return [(c, c.n) for c in p.ctrs.values()]

    build_l0a(nc, p=p, io=dict(x_own=x_own, x_pre=xpre_own,
                               x_out=scr["y_own"], **l0a_w), gate=[])
    build_l0a(nc, p=p, io=dict(x_own=x_par, x_pre=xpre_par,
                               x_out=scr["y_par"], **l0a_w), gate=gate())
    build_l0f(nc, p=p, io=dict(x_in=scr["y_own"], x_out=scr["z_own"], **l0f_w),
              gate=gate())
    build_l0f(nc, p=p, io=dict(x_in=scr["y_par"], x_out=scr["z_par"], **l0f_w),
              gate=gate())
    build_l1a(nc, p=p, io=dict(x_own=scr["z_own"], x_oth=scr["z_par"],
                               x_out=scr["u_own"], **l1a_w), gate=gate())
    build_l1f(nc, p=p, io=dict(x_in=scr["u_own"], x_out=x_out, **l1f_w),
              gate=gate())
    p.finish()
    return nc


def _fused_inmaps(inp):
    x = f32c(inp["L0_attn_x"])
    xs = _shards(x)
    zeros = np.zeros((1024, 2048), np.float32)
    com = dict(
        wq=f32c(inp["L0_attn_Wq"]), wk=f32c(inp["L0_attn_Wk"]),
        wv=f32c(inp["L0_attn_Wv"]), wo=f32c(inp["L0_attn_Wo"]),
        wg1=f32c(inp["L0_attn_Wg1"]), wg2=f32c(inp["L0_attn_Wg2"]),
        nw_c=f32c(np.asarray(inp["L0_attn_norm_w"]).reshape(16, 128).T),
        ld_r=f32c(np.asarray(inp["L0_attn_log_decay"]).reshape(1, -1)),
        wup_q=u8c(inp["L0_ffn_w_up_q"]),
        wup_s=f32c(np.asarray(inp["L0_ffn_w_up_scale"]).reshape(64, 128, 32)
                   .transpose(1, 0, 2).reshape(128, -1)),
        bup_c=f32c(np.asarray(inp["L0_ffn_b_up"]).reshape(64, 128).T),
        la_t=f32c(np.asarray(inp["L0_ffn_w_up_lora_a"]).reshape(16, 128, 16)
                  .transpose(1, 0, 2).reshape(128, -1)),
        lora_ub=f32c(inp["L0_ffn_w_up_lora_b"]),
        wdn_q=u8c(inp["L0_ffn_w_down_q"]),
        wdn_s=f32c(np.asarray(inp["L0_ffn_w_down_scale"]).reshape(16, 128, 128)
                   .transpose(1, 0, 2).reshape(128, -1)),
        b_dn=f32c(np.asarray(inp["L0_ffn_b_down"]).reshape(1, -1)),
        da_t=f32c(np.asarray(inp["L0_ffn_w_down_lora_a"]).reshape(64, 128, 16)
                  .transpose(1, 0, 2).reshape(128, -1)),
        lora_db=f32c(inp["L0_ffn_w_down_lora_b"]),
        wqt=f32c(np.asarray(inp["L1_attn_Wq"]).T),
        wkt=f32c(np.asarray(inp["L1_attn_Wk"]).T),
        wvt=f32c(np.asarray(inp["L1_attn_Wv"]).T),
        mem=f32c(inp["L1_attn_memory"]),
        mn_c=f32c(np.asarray(inp["L1_attn_memory_norm"]).reshape(4, 128).T),
        mw=f32c(np.asarray(inp["L1_attn_memory_weight"]).reshape(1, 1)),
        gw_q=u8c(inp["L1_ffn_gate_w"]),
        uw_q=u8c(inp["L1_ffn_up_w"]),
        dw_q=u8c(inp["L1_ffn_down_w"]),
        gs_c=f32c(np.asarray(inp["L1_ffn_gate_scale"]).reshape(64, 128).T),
        us_c=f32c(np.asarray(inp["L1_ffn_up_scale"]).reshape(64, 128).T),
        ds_c=f32c(np.asarray(inp["L1_ffn_down_scale"]).reshape(16, 128).T),
    )
    in_maps = []
    for c in range(8):
        h = c % 2
        par = c + 1 if h == 0 else c - 1
        in_maps.append(dict(
            x_own=xs[c], x_par=xs[par],
            xpre_own=f32c(zeros if h == 0 else xs[par]),
            xpre_par=f32c(xs[c] if h == 0 else zeros),
            **com))
    return in_maps


def kernel(**inp):
    nc = bass.Bass("TRN2", target_bir_lowering=False)
    build_fused(nc)
    res = run_bass_kernel_spmd(nc, _fused_inmaps(inp), core_ids=list(range(8)))
    outs = [res.results[c]["x_out"] for c in range(8)]
    out = np.stack(outs, axis=0).reshape(4, 2, 1024, 2048).reshape(4, 2048, 2048)
    return np.ascontiguousarray(out, dtype=np.float32)


# version 12
# speedup vs baseline: 1.0720x; 1.0069x over previous
import contextlib
import numpy as np
import ml_dtypes
import concourse.bass as bass
import concourse.mybir as mybir
from concourse.bass_utils import run_bass_kernel_spmd

F32 = mybir.dt.float32
BF16 = mybir.dt.bfloat16
I32 = mybir.dt.int32
U8 = mybir.dt.uint8
AF = mybir.ActivationFunctionType
ALU = mybir.AluOpType
AX = mybir.AxisListType

D = 2048
T = 1024
NT = T // 128
ND = D // 128
EPS = 1e-5
NBF = np.dtype(ml_dtypes.bfloat16)


class Ctr:
    def __init__(self, sem):
        self.sem = sem
        self.n = 0

    def bump(self, k):
        self.n += k
        return self.n


class SharedCell(dict):
    def __getitem__(self, k):
        return dict.get(self, "x", 0)

    def __setitem__(self, k, v):
        dict.__setitem__(self, "x", v)


class Prog:

    ENGINES = ("sync", "gpsimd", "vector", "scalar", "tensor")

    def __init__(self, nc):
        self.nc = nc
        self.streams = {e: [] for e in self.ENGINES}
        self.waited = {e: {} for e in self.ENGINES}
        self.ctrs = {}
        self.max_waited = {}
        self._es = contextlib.ExitStack()

    def ctr(self, name):
        if name not in self.ctrs:
            sem = self._es.enter_context(self.nc.semaphore(name))
            self.ctrs[name] = Ctr(sem)
        return self.ctrs[name]

    def task(self, eng, f, waits=(), inc=None, inc_k=1):
        eff = []
        waits = list(waits) + list(getattr(self, "default_waits", ()))
        for (c, n) in waits:
            if n is None or n <= 0:
                continue
            key = id(c)
            if self.waited[eng].get(key, -1) >= n:
                continue
            self.waited[eng][key] = n
            eff.append((c.sem, n))
        tgt = inc.bump(inc_k) if inc is not None else None
        sem = inc.sem if inc is not None else None
        for (c, n) in waits:
            if n:
                ent = self.max_waited.setdefault(id(c), [c, 0])
                ent[1] = max(ent[1], n)

        def g(e, f=f, eff=tuple(eff), sem=sem, inc_k=inc_k):
            for (s, n) in eff:
                e.wait_ge(s, n)
            last = f(e)
            if sem is not None:
                assert last is not None, "task emit fn must return its last instruction"
                last.then_inc(sem, inc_k)
        self.streams[eng].append(g)
        return tgt

    def finish(self):
        for key, (c, mw) in self.max_waited.items():
            name = next((n for n, cc in self.ctrs.items() if cc is c), "?")
            assert mw <= c.n, f"wait {mw} exceeds total {c.n} on ctr '{name}'"
        with self.nc.Block() as blk:
            for eng in self.ENGINES:
                fs = self.streams[eng]
                if not fs:
                    continue

                def body(e, fs=fs):
                    for f in fs:
                        f(e)
                getattr(blk, eng)(body)
        self._es.close()


_UID = [0]


def _uid():
    _UID[0] += 1
    return f"t{_UID[0]}_"


def bfc(x):
    return np.ascontiguousarray(np.asarray(x), dtype=NBF)


def f32c(x):
    return np.ascontiguousarray(np.asarray(x), dtype=np.float32)


def u8c(x):
    x = np.ascontiguousarray(np.asarray(x))
    assert x.dtype == np.int32
    return np.ascontiguousarray(x.view(np.uint8)[..., ::4])



def emit_identity(p, ident, one1):
    cQ = p.ctr("q")
    p.task("gpsimd", lambda e: e.memset(ident[:], 0.0), inc=cQ)
    t0 = cQ.n
    p.task("gpsimd", lambda e: e.affine_select(
        out=ident[:], in_=ident[:], compare_op=ALU.not_equal,
        fill=1.0, base=0, pattern=[[-1, 128]], channel_multiplier=1),
        waits=[(cQ, t0)], inc=cQ)
    p.task("gpsimd", lambda e: e.memset(one1[:], 1.0), inc=cQ)
    return (cQ, cQ.n)


def emit_ln_chunk(p, x_ap, xn_ap, dump, cols, width=D, waits=(), prev_v=0,
                  out_ap=None, resid_ap=None):
    cV, cC = p.ctr("v"), p.ctr("c")
    hw = width // 2
    inv = 1.0 / width

    def f_sq(e):
        e.activation(out=dump[:, 0:hw], in_=x_ap[:, 0:hw], func=AF.Square,
                     accum_out=cols[:, 1:2])
        return e.activation(out=dump[:, hw:width], in_=x_ap[:, hw:width],
                            func=AF.Square, accum_out=cols[:, 7:8])
    p.task("scalar", f_sq, waits=list(waits) + [(cV, prev_v)], inc=cC)
    t_sq = cC.n

    def f_v(e):
        e.reduce_sum(out=cols[:, 0:1], in_=x_ap[:], axis=AX.X)
        e.tensor_tensor(out=cols[:, 1:2], in0=cols[:, 1:2], in1=cols[:, 7:8],
                        op=ALU.add)
        e.tensor_scalar(out=cols[:, 2:3], in0=cols[:, 0:1], scalar1=inv,
                        scalar2=None, op0=ALU.mult)
        e.tensor_scalar(out=cols[:, 3:4], in0=cols[:, 1:2], scalar1=inv,
                        scalar2=None, op0=ALU.mult)
        e.tensor_tensor(out=cols[:, 5:6], in0=cols[:, 2:3], in1=cols[:, 2:3],
                        op=ALU.mult)
        e.tensor_tensor(out=cols[:, 3:4], in0=cols[:, 3:4],
                        in1=cols[:, 5:6], op=ALU.subtract)
        return e.tensor_scalar(out=cols[:, 3:4], in0=cols[:, 3:4], scalar1=EPS,
                               scalar2=None, op0=ALU.add)
    p.task("vector", f_v, waits=list(waits) + [(cC, t_sq)], inc=cV)
    t_var = cV.n
    p.task("scalar", lambda e: e.activation(
        out=cols[:, 6:7], in_=cols[:, 3:4], func=AF.Sqrt),
        waits=[(cV, t_var)], inc=cC)
    t_std = cC.n

    if xn_ap is not None:
        def f_n(e):
            e.reciprocal(out=cols[:, 4:5], in_=cols[:, 6:7])
            return e.tensor_scalar(out=xn_ap[:], in0=x_ap[:],
                                   scalar1=cols[:, 2:3], scalar2=cols[:, 4:5],
                                   op0=ALU.subtract, op1=ALU.mult)
    else:
        def f_n(e):
            e.reciprocal(out=cols[:, 4:5], in_=cols[:, 6:7])
            e.tensor_scalar(out=x_ap[:], in0=x_ap[:],
                            scalar1=cols[:, 2:3], scalar2=cols[:, 4:5],
                            op0=ALU.subtract, op1=ALU.mult)
            return e.tensor_tensor(out=out_ap[:], in0=resid_ap[:], in1=x_ap[:],
                                   op=ALU.add)
    p.task("vector", f_n, waits=[(cC, t_std)], inc=cV)
    return cV.n


def emit_tm_to_fm(p, tiles, ident, psum, waits=(), t_ident=None, psum2=None):
    cP, cC, cV = p.ctr("pe"), p.ctr("c"), p.ctr("v")
    if not hasattr(p, "_tr_ev"):
        p._tr_ev = {}
    psums = [psum, psum2 if psum2 is not None else psum]
    last_c, last_v = 0, 0
    nb = (len(tiles) + 3) // 4
    for bi in range(nb):
        grp = tiles[bi * 4:bi * 4 + 4]
        pi = bi % 2
        ps_i = psums[pi]
        key = id(ps_i)

        def f_t(e, grp=grp, ps_i=ps_i):
            last = None
            for j, (src, dst) in enumerate(grp):
                last = e.matmul(out=ps_i[:, j * 128:(j + 1) * 128],
                                lhsT=src, rhs=ident[:], start=True, stop=True)
            return last
        wl = list(waits)
        prev = p._tr_ev.get(key)
        if prev is not None:
            wl.append(prev)
        if t_ident is not None:
            wl.append(t_ident)
        p.task("tensor", f_t, waits=wl, inc=cP)
        t_tr = cP.n

        def f_e(e, grp=grp, ps_i=ps_i):
            last = None
            for j, (src, dst) in enumerate(grp):
                last = e.copy(out=dst, in_=ps_i[:, j * 128:(j + 1) * 128])
            return last
        p.task("scalar", f_e, waits=[(cP, t_tr)], inc=cC)
        p._tr_ev[key] = (cC, cC.n)
        last_c = cC.n
    return last_c



H = 8192
NH = H // 128
GRP = 8
NG = NH // GRP


def build_l0f(nc, stage=99, p=None, io=None, gate=()):
    x_in = io["x_in"] if io is not None else nc.dram_tensor("x_in", [T, D], F32, kind="ExternalInput").ap()
    wup_q = io["wup_q"] if io is not None else nc.dram_tensor("wup_q", [H, D], U8, kind="ExternalInput").ap()
    wup_s = io["wup_s"] if io is not None else nc.dram_tensor("wup_s", [128, NH * 32], F32, kind="ExternalInput").ap()
    bup_c = io["bup_c"] if io is not None else nc.dram_tensor("bup_c", [128, NH], F32, kind="ExternalInput").ap()
    la_t = io["la_t"] if io is not None else nc.dram_tensor("la_t", [128, ND * 16], F32, kind="ExternalInput").ap()
    lora_ub = io["lora_ub"] if io is not None else nc.dram_tensor("lora_ub", [16, H], F32, kind="ExternalInput").ap()
    wdn_q = io["wdn_q"] if io is not None else nc.dram_tensor("wdn_q", [D, H], U8, kind="ExternalInput").ap()
    wdn_s = io["wdn_s"] if io is not None else nc.dram_tensor("wdn_s", [128, ND * 128], F32, kind="ExternalInput").ap()
    b_dn = io["b_dn"] if io is not None else nc.dram_tensor("b_dn", [1, D], F32, kind="ExternalInput").ap()
    da_t = io["da_t"] if io is not None else nc.dram_tensor("da_t", [128, NH * 16], F32, kind="ExternalInput").ap()
    lora_db = io["lora_db"] if io is not None else nc.dram_tensor("lora_db", [16, D], F32, kind="ExternalInput").ap()
    x_out = io["x_out"] if io is not None else nc.dram_tensor("x_out", [T, D], F32, kind="ExternalOutput").ap()

    es = contextlib.ExitStack()
    with es:
        U = _uid()
        sb = lambda name, shape, dt=BF16: es.enter_context(nc.sbuf_tensor(U + name, shape, dt))
        ps = lambda name, shape: es.enter_context(nc.psum_tensor(U + name, shape, F32))
        XR = [sb(f"XR{b}", [128, D], F32) for b in range(2)]
        xf = [sb(f"xf{c}", [128, T]) for c in range(ND)]
        y1 = [sb(f"y1{m}", [128, T]) for m in range(GRP)]
        wub = [sb(f"wub{b}", [128, ND * 128]) for b in range(2)]
        wuraw = [sb(f"wuraw{b}", [128, D], U8) for b in range(2)]
        wdraw = [sb(f"wdraw{b}", [128, GRP * 128], U8) for b in range(2)]
        wdb = [sb(f"wdb{b}", [128, GRP * 512]) for b in range(2)]
        wus = sb("wus", [128, NH * 32], F32)
        wds = sb("wds", [128, ND * 128], F32)
        la = sb("la", [128, ND * 16])
        lbg = [sb(f"lbg{b}", [16, GRP * 128]) for b in range(2)]
        da = sb("da", [128, NH * 16])
        db_t = sb("db", [16, D])
        bup = sb("bup", [128, NH], F32)
        bdn_r = sb("bdn_r", [1, D])
        l1 = sb("l1", [16, T])
        l1d_b = sb("l1d_b", [16, T])
        ident = sb("ident", [128, 128])
        one1 = sb("one1", [1, 128])
        dump = sb("dump", [128, D])
        cols = sb("cols", [128, 8], F32)
        wq_nat = sb("wq_nat", [128, 4 * 1024])
        p_tr = ps("p_tr", [128, 512])
        p_tr2 = ps("p_tr2", [128, 512])
        p_up = [ps(f"p_up{b}", [128, 512]) for b in range(2)]
        p_l1 = ps("p_l1", [16, T])
        p_dn = [ps(f"p_dn{b}", [128, 512]) for b in range(2)]
        xn_es = contextlib.ExitStack()
        xn = [xn_es.enter_context(nc.sbuf_tensor(U + f"xn{i}", [128, D], BF16))
              for i in range(NT)]

        own_prog = p is None
        if own_prog:
            p = Prog(nc)
        p.default_waits = list(gate)
        cV, cC, cP = (p.ctr(n) for n in ("v", "c", "pe"))
        cSc = p.ctr("sc")
        cG = p.ctr("g")
        cXR = [p.ctr("xr0"), p.ctr("xr1")]
        cWU = [p.ctr("wu0"), p.ctr("wu1")]
        cWD = [p.ctr("wd0"), p.ctr("wd1")]
        cLB = [p.ctr("lb0"), p.ctr("lb1")]
        cO = [p.ctr("o0"), p.ctr("o1")]

        t_id = emit_identity(p, ident, one1)
        p.task("sync", lambda e: e.dma_start(out=wus[:], in_=wup_s), inc=cSc, inc_k=16)
        p.task("sync", lambda e: e.dma_start(out=wds[:], in_=wdn_s), inc=cSc, inc_k=16)
        p.task("sync", lambda e: e.dma_start(out=bup[:], in_=bup_c), inc=cSc, inc_k=16)
        t_scales = cSc.n
        for (dst, src) in ((la, la_t), (da, da_t), (db_t, lora_db), (bdn_r, b_dn)):
            p.task("gpsimd", lambda e, dst=dst, src=src: e.dma_start(
                out=dst[:], in_=src), inc=cG, inc_k=16)
        t_const = cG.n

        prev_v = 0
        xr_free = {0: 0, 1: 0}
        for i in range(NT):
            b = i % 2
            p.task("sync", lambda e, i=i, b=b: e.dma_start(
                out=XR[b][:], in_=x_in[i * 128:(i + 1) * 128, :]),
                waits=[(cV, xr_free[b])], inc=cXR[b], inc_k=16)
            t_ld = cXR[b].n
            prev_v = emit_ln_chunk(p, XR[b], xn[i], dump, cols,
                                   waits=[(cXR[b], t_ld)], prev_v=prev_v)
            xr_free[b] = prev_v

        tiles = [(xn[i][:, c * 128:(c + 1) * 128], xf[c][:, i * 128:(i + 1) * 128])
                 for c in range(ND) for i in range(NT)]
        t_xf = emit_tm_to_fm(p, tiles, ident, p_tr,
                             waits=[(cV, prev_v)], t_ident=t_id, psum2=p_tr2)

        xn_es.close()
        Y2 = [sb(f"Y2{i}", [128, D], F32) for i in range(NT)]

        def f_l1(e):
            last = None
            for tb in range(2):
                for k in range(ND):
                    last = e.matmul(out=p_l1[:, tb * 512:(tb + 1) * 512],
                                    lhsT=la[:, k * 16:(k + 1) * 16],
                                    rhs=xf[k][:, tb * 512:(tb + 1) * 512],
                                    start=(k == 0), stop=(k == ND - 1))
            return last
        import os as _os
        if _os.environ.get("SKIP_L1") == "1":
            t_l1 = 0
        else:
            p.task("tensor", f_l1, waits=[(cC, t_xf), (cG, t_const)], inc=cP)
            p.task("scalar", lambda e: e.copy(out=l1[:], in_=p_l1[:]),
                   waits=[(cP, cP.n)], inc=cC)
            t_l1 = cC.n

        up_tiles_done = {}
        wuraw_free = {0: 0, 1: 0}
        wdraw_free = {0: 0, 1: 0}
        wqnat_free = {0: 0, 1: 0, "dn": 0}
        relu_done = {}
        y1_consumed = {}
        y2_evac = {}
        dn_psum_free = {0: None, 1: None}
        lbg_loaded = {}
        lbg_used = {0: 0, 1: 0}
        wub_used = {0: 0, 1: 0}
        wdb_used = {0: 0, 1: 0}

        def sched_up_weight(m):
            UPW_PART = int(os.environ.get("UPW_PART", "3"))
            b = m % 2
            p.task("sync", lambda e, m=m, b=b: e.dma_start(
                out=wuraw[b][:], in_=wup_q[m * 128:(m + 1) * 128, :]),
                waits=[(cV, wuraw_free[b])], inc=cWU[b], inc_k=16)
            t_dma = cWU[b].n
            base = (m % 2) * 2048
            for s4 in range(4):
                def f_dq(e, m=m, b=b, s4=s4, base=base):
                    sc = wus[:, m * 32 + s4 * 8: m * 32 + (s4 + 1) * 8]
                    return e.scalar_tensor_tensor(
                        out=wq_nat[:, base + s4 * 512:base + (s4 + 1) * 512].rearrange(
                            "p (blk f) -> p blk f", blk=8),
                        in0=wuraw[b][:, s4 * 512:(s4 + 1) * 512].rearrange(
                            "p (blk f) -> p blk f", blk=8),
                        scalar=7.5,
                        in1=sc.rearrange("p blk -> p blk ()").broadcast_to([128, 8, 64]),
                        op0=ALU.subtract, op1=ALU.mult)
                p.task("vector", f_dq,
                       waits=[(cWU[b], t_dma), (cSc, t_scales),
                              (cC, wqnat_free[m % 2])], inc=cV)
            wuraw_free[b] = cV.n
            t_dq = cV.n
            tiles = [(wq_nat[:, base + k * 128:base + (k + 1) * 128],
                      wub[b][:, k * 128:(k + 1) * 128]) for k in range(ND)]
            up_tiles_done[m] = emit_tm_to_fm(p, tiles, ident, p_tr,
                                             waits=[(cV, t_dq),
                                                    (cP, wub_used[b])],
                                             t_ident=t_id, psum2=p_tr2)
            wqnat_free[m % 2] = up_tiles_done[m]

        import os
        NO_LORA_MM = os.environ.get("NO_LORA_MM") == "1"
        NO_RELU_BIAS = os.environ.get("NO_RELU_BIAS") == "1"
        UP_MIN = os.environ.get("UP_MIN") == "1"
        RELU_COPY = os.environ.get("RELU_COPY") == "1"

        def sched_up_compute(m, g_idx, gi):
            b = m % 2
            for tb in range(2):
                MMVAR = os.environ.get("MMVAR", "")

                def f_mm(e, m=m, b=b, tb=tb, gi=gi):
                    nk = 1 if UP_MIN else ND
                    if MMVAR == "T1":
                        return e.matmul(out=p_up[b][:, 0:128], lhsT=ident[:],
                                        rhs=ident[:], start=True, stop=True)
                    if MMVAR == "T2":
                        return e.matmul(out=p_up[b][:, 0:128], lhsT=wub[b][:, 0:128],
                                        rhs=ident[:], start=True, stop=True)
                    if MMVAR == "T3":
                        return e.matmul(out=p_up[b][:, 0:512], lhsT=ident[:],
                                        rhs=xf[0][:, 0:512], start=True, stop=True)
                    if MMVAR == "T4":
                        return e.matmul(out=p_up[b][:, 0:256], lhsT=ident[:],
                                        rhs=xf[0][:, 0:256], start=True, stop=True)
                    if MMVAR == "T5":
                        return e.matmul(out=p_up[b][:, 0:512], lhsT=ident[:],
                                        rhs=wub[b][:, 0:512], start=True, stop=True)
                    if MMVAR == "T6":
                        return e.matmul(out=p_up[b][:, 0:384], lhsT=ident[:],
                                        rhs=xf[0][:, 0:384], start=True, stop=True)
                    if MMVAR == "T7":
                        return e.matmul(out=p_up[b][:, 0:500], lhsT=ident[:],
                                        rhs=xf[0][:, 0:500], start=True, stop=True)
                    if MMVAR == "T8":
                        return e.matmul(out=p_tr[:, 0:512], lhsT=ident[:],
                                        rhs=xf[0][:, 0:512], start=True, stop=True)
                    if MMVAR == "T10":
                        return e.matmul(out=p_up[1][:, 0:512], lhsT=ident[:],
                                        rhs=xf[0][:, 0:512], start=True, stop=True)
                    if MMVAR == "T11":
                        return e.matmul(out=p_dn[0][:, 0:512], lhsT=ident[:],
                                        rhs=xf[0][:, 0:512], start=True, stop=True)
                    for k in range(nk):
                        last = e.matmul(out=p_up[tb][:, 0:512],
                                 lhsT=wub[b][:, k * 128:(k + 1) * 128],
                                 rhs=xf[k][:, tb * 512:(tb + 1) * 512],
                                 start=(k == 0), stop=(NO_LORA_MM and k == nk - 1))
                    if NO_LORA_MM:
                        return last
                    mj = m - gi * GRP
                    return e.matmul(out=p_up[tb][:, 0:512],
                                    lhsT=lbg[gi % 2][:, mj * 128:(mj + 1) * 128],
                                    rhs=l1[:, tb * 512:(tb + 1) * 512],
                                    start=False, stop=True)
                p.task("tensor", f_mm,
                       waits=[(cC, max(up_tiles_done[m], t_l1,
                                       relu_done.get((m - 1, tb), 0))),
                              (cLB[gi % 2], lbg_loaded[gi])], inc=cP)
                t_mm = cP.n

                def f_relu(e, m=m, b=b, tb=tb, g_idx=g_idx):
                    if RELU_COPY:
                        return e.copy(out=y1[g_idx][:, tb * 512:(tb + 1) * 512],
                                      in_=p_up[tb][:, 0:512])
                    if NO_RELU_BIAS:
                        return e.activation(out=y1[g_idx][:, tb * 512:(tb + 1) * 512],
                                            in_=p_up[tb][:, 0:512], func=AF.Relu)
                    return e.activation(out=y1[g_idx][:, tb * 512:(tb + 1) * 512],
                                        in_=p_up[tb][:, 0:512], func=AF.Relu,
                                        bias=bup[:, m:m + 1])
                p.task("scalar", f_relu,
                       waits=[(cP, max(t_mm, y1_consumed.get(gi - 1, 0))),
                              (cSc, t_scales)], inc=cC)
                relu_done[(m, tb)] = cC.n
            wub_used[b] = cP.n

        def sched_l1d_mms(gi):
            def f(e, gi=gi):
                last = None
                for mj in range(GRP):
                    m = gi * GRP + mj
                    for tb in range(2):
                        last = e.matmul(out=p_l1[:, tb * 512:(tb + 1) * 512],
                                        lhsT=da[:, m * 16:(m + 1) * 16],
                                        rhs=y1[mj][:, tb * 512:(tb + 1) * 512],
                                        start=(gi == 0 and mj == 0),
                                        stop=(gi == NG - 1 and mj == GRP - 1),
                                        skip_group_check=True)
                return last
            p.task("tensor", f, waits=[(cC, max(relu_done[(gi * GRP + GRP - 1, 1)],
                                                t_l1)), (cG, t_const)], inc=cP)
            return cP.n

        def sched_dn_weight(gi, ds, buf):
            for dci in range(4):
                dc = ds * 4 + dci
                p.task("sync", lambda e, dc=dc, gi=gi, buf=buf: e.dma_start(
                    out=wdraw[buf][:],
                    in_=wdn_q[dc * 128:(dc + 1) * 128,
                              gi * GRP * 128:(gi + 1) * GRP * 128]),
                    waits=[(cV, wdraw_free[buf])], inc=cWD[buf], inc_k=16)
                t_dma = cWD[buf].n

                def f_dq(e, dc=dc, gi=gi, buf=buf, dci=dci):
                    sc = wds[:, dc * 128 + gi * 16: dc * 128 + (gi + 1) * 16]
                    return e.scalar_tensor_tensor(
                        out=wq_nat[:, dci * 1024:(dci + 1) * 1024].rearrange(
                            "p (blk f) -> p blk f", blk=16),
                        in0=wdraw[buf][:].rearrange("p (blk f) -> p blk f", blk=16),
                        scalar=7.5,
                        in1=sc.rearrange("p blk -> p blk ()").broadcast_to([128, 16, 64]),
                        op0=ALU.subtract, op1=ALU.mult)
                p.task("vector", f_dq, waits=[(cWD[buf], t_dma), (cSc, t_scales),
                                              (cC, max(wqnat_free[0], wqnat_free[1],
                                                       wqnat_free["dn"]))], inc=cV)
                wdraw_free[buf] = cV.n
            t_dq = cV.n
            tiles = [(wq_nat[:, dci * 1024 + mj * 128: dci * 1024 + (mj + 1) * 128],
                      wdb[buf][:, mj * 512 + dci * 128: mj * 512 + (dci + 1) * 128])
                     for mj in range(GRP) for dci in range(4)]
            t_w = emit_tm_to_fm(p, tiles, ident, p_tr,
                                waits=[(cV, t_dq), (cP, wdb_used[buf])],
                                t_ident=t_id, psum2=p_tr2)
            wqnat_free[0] = wqnat_free[1] = wqnat_free["dn"] = t_w
            return t_w

        def sched_dn_compute(gi, ds, buf, t_w, t_l1db):
            for tt in range(NT):
                pb = tt % 2

                def f_mm(e, gi=gi, ds=ds, buf=buf, tt=tt, pb=pb):
                    last = None
                    for mj in range(GRP):
                        last = e.matmul(out=p_dn[pb][:, 0:512],
                                        lhsT=y1[mj][:, tt * 128:(tt + 1) * 128],
                                        rhs=wdb[buf][:, mj * 512:(mj + 1) * 512],
                                        start=(mj == 0),
                                        stop=(mj == GRP - 1 and gi != NG - 1))
                    if gi == NG - 1:
                        e.matmul(out=p_dn[pb][:, 0:512],
                                 lhsT=l1d_b[:, tt * 128:(tt + 1) * 128],
                                 rhs=db_t[:, ds * 512:(ds + 1) * 512],
                                 start=False, stop=False)
                        last = e.matmul(out=p_dn[pb][:, 0:512],
                                        lhsT=one1[:],
                                        rhs=bdn_r[:, ds * 512:(ds + 1) * 512],
                                        start=False, stop=True)
                    return last
                wl = [(cC, max(t_w, relu_done[(gi * GRP + GRP - 1, 1)], t_l1db))]
                if dn_psum_free[pb] is not None:
                    wl.append(dn_psum_free[pb])
                prev = y2_evac.get((tt, ds))
                if prev is not None:
                    wl.append(prev)
                p.task("tensor", f_mm, waits=wl, inc=cP)
                t_mm = cP.n

                if gi == 0:
                    p.task("scalar", lambda e, tt=tt, ds=ds, pb=pb: e.copy(
                        out=Y2[tt][:, ds * 512:(ds + 1) * 512], in_=p_dn[pb][:, 0:512]),
                        waits=[(cP, t_mm)], inc=cC)
                    y2_evac[(tt, ds)] = (cC, cC.n)
                    dn_psum_free[pb] = (cC, cC.n)
                else:
                    def f_add(e, tt=tt, ds=ds, pb=pb):
                        return e.tensor_tensor(
                            out=Y2[tt][:, ds * 512:(ds + 1) * 512],
                            in0=Y2[tt][:, ds * 512:(ds + 1) * 512],
                            in1=p_dn[pb][:, 0:512], op=ALU.add)
                    wl2 = [(cP, t_mm)]
                    prev2 = y2_evac.get((tt, ds))
                    if prev2 is not None and prev2[0] is cC:
                        wl2.append(prev2)
                    p.task("vector", f_add, waits=wl2, inc=cV)
                    y2_evac[(tt, ds)] = (cV, cV.n)
                    dn_psum_free[pb] = (cV, cV.n)

        def dbg_store(bufs, n):
            cOg = p.ctr("og")
            for i, bp in enumerate(bufs[:n]):
                sh = bp.ap().shape if hasattr(bp, 'ap') else bp.shape
                p.task("gpsimd", lambda e, i=i, bp=bp, sh=sh: e.dma_start(
                    out=x_out[i * 128:i * 128 + sh[0], 0:sh[1]], in_=bp[:]),
                    waits=[(cV, cV.n), (cC, cC.n), (cP, cP.n)], inc=cOg, inc_k=16)
            p.task("sync", lambda e: e.nop(), waits=[(cOg, cOg.n)])
            if own_prog:
                p.finish()

        if stage == 3:
            dbg_store([l1], 1); return
        t_l1db = 0
        for gi in range(NG):
            bg = gi % 2
            p.task("gpsimd", lambda e, gi=gi, bg=bg: e.dma_start(
                out=lbg[bg][:], in_=lora_ub[:, gi * GRP * 128:(gi + 1) * GRP * 128]),
                waits=[(cP, lbg_used[bg])], inc=cLB[bg], inc_k=16)
            lbg_loaded[gi] = cLB[bg].n
            for mj in range(GRP):
                m = gi * GRP + mj
                if os.environ.get("SKIP_UPW") == "1":
                    up_tiles_done[m] = 0
                else:
                    sched_up_weight(m)
                if stage == 35 and m == 0:
                    dbg_store([wub[0]], 1); return
                sched_up_compute(m, mj, gi)
                if stage == 36 and m == 0:
                    import os as _os
                    dbg_store([wub[0] if _os.environ.get("DBG_WUB") == "1" else y1[0]], 1)
                    return
            lbg_used[bg] = cP.n
            t_lora_acc = sched_l1d_mms(gi)
            if gi == NG - 1:
                p.task("scalar", lambda e: e.copy(out=l1d_b[:], in_=p_l1[:]),
                       waits=[(cP, t_lora_acc)], inc=cC)
                t_l1db = cC.n
            for ds in range(4):
                buf = (gi * 4 + ds) % 2
                t_w = sched_dn_weight(gi, ds, buf)
                sched_dn_compute(gi, ds, buf, t_w, t_l1db)
            y1_consumed[gi] = cP.n
            if stage == 4 and gi == 0:
                dbg_store(y1, GRP); return
            if stage == 5 and gi == 0:
                dbg_store(Y2, NT); return

        if stage == 6:
            dbg_store(Y2, NT); return

        t_y2_all_c = cC.n
        t_y2_all_v = cV.n
        prev_v2 = 0
        xr_store = {0: 0, 1: 0}
        for i in range(NT):
            b = i % 2
            p.task("sync", lambda e, i=i, b=b: e.dma_start(
                out=XR[b][:], in_=x_in[i * 128:(i + 1) * 128, :]),
                waits=[(cO[b], xr_store[b]), (cV, t_y2_all_v if i < 2 else 0)],
                inc=cXR[b], inc_k=16)
            t_ld = cXR[b].n
            tv = emit_ln_chunk(p, Y2[i], None, dump, cols,
                               waits=[(cXR[b], t_ld), (cV, t_y2_all_v),
                                      (cC, t_y2_all_c)],
                               prev_v=prev_v2, out_ap=XR[b], resid_ap=XR[b])
            prev_v2 = tv
            p.task("sync", lambda e, i=i, b=b: e.dma_start(
                out=x_out[i * 128:(i + 1) * 128, :], in_=XR[b][:]),
                waits=[(cV, tv)], inc=cO[b], inc_k=16)
            xr_store[b] = cO[b].n
        p.task("sync", lambda e: e.nop(), waits=[(cO[0], cO[0].n), (cO[1], cO[1].n)])
        if own_prog:
            p.finish()
    return nc



def build_l1f(nc, stage=99, p=None, io=None, gate=()):
    x_in = io["x_in"] if io is not None else nc.dram_tensor("x_in", [T, D], F32, kind="ExternalInput").ap()
    gw_q = io["gw_q"] if io is not None else nc.dram_tensor("gw_q", [H, D], U8, kind="ExternalInput").ap()
    uw_q = io["uw_q"] if io is not None else nc.dram_tensor("uw_q", [H, D], U8, kind="ExternalInput").ap()
    dw_q = io["dw_q"] if io is not None else nc.dram_tensor("dw_q", [D, H], U8, kind="ExternalInput").ap()
    gs_c = io["gs_c"] if io is not None else nc.dram_tensor("gs_c", [128, NH], F32, kind="ExternalInput").ap()
    us_c = io["us_c"] if io is not None else nc.dram_tensor("us_c", [128, NH], F32, kind="ExternalInput").ap()
    ds_c = io["ds_c"] if io is not None else nc.dram_tensor("ds_c", [128, ND], F32, kind="ExternalInput").ap()
    x_out = io["x_out"] if io is not None else nc.dram_tensor("x_out", [T, D], F32, kind="ExternalOutput").ap()

    es = contextlib.ExitStack()
    with es:
        U = _uid()
        sb = lambda name, shape, dt=BF16: es.enter_context(nc.sbuf_tensor(U + name, shape, dt))
        ps = lambda name, shape: es.enter_context(nc.psum_tensor(U + name, shape, F32))
        XR = [sb(f"XR{b}", [128, D], F32) for b in range(2)]
        xf = [sb(f"xf{c}", [128, T]) for c in range(ND)]
        y1 = [sb(f"y1{m}", [128, T]) for m in range(GRP)]
        wgb = [sb(f"wgb{b}", [128, ND * 128]) for b in range(2)]
        wub = [sb(f"wub{b}", [128, ND * 128]) for b in range(2)]
        wgraw = [sb(f"wgraw{b}", [128, D], U8) for b in range(2)]
        wuraw = [sb(f"wuraw{b}", [128, D], U8) for b in range(2)]
        wdraw = [sb(f"wdraw{b}", [128, GRP * 128], U8) for b in range(2)]
        wdb = [sb(f"wdb{b}", [128, GRP * 512]) for b in range(2)]
        gsc = sb("gsc", [128, NH], F32)
        gzc = sb("gzc", [128, NH], F32)
        usc = sb("usc", [128, NH], F32)
        uzc = sb("uzc", [128, NH], F32)
        dsc = sb("dsc", [128, ND], F32)
        dzc = sb("dzc", [128, ND], F32)
        g_sb = [sb(f"g_sb{b}", [128, 512]) for b in range(2)]
        ident = sb("ident", [128, 128])
        one1 = sb("one1", [1, 128])
        dump = sb("dump", [128, D])
        cols = sb("cols", [128, 8], F32)
        wq_nat = sb("wq_nat", [128, 4 * 1024])
        p_tr = ps("p_tr", [128, 512])
        p_tr2 = ps("p_tr2", [128, 512])
        p_g = [ps(f"p_g{b}", [128, 512]) for b in range(2)]
        p_u = [ps(f"p_u{b}", [128, 512]) for b in range(2)]
        p_dn = [ps(f"p_dn{b}", [128, 512]) for b in range(2)]
        xn_es = contextlib.ExitStack()
        xn = [xn_es.enter_context(nc.sbuf_tensor(U + f"xn{i}", [128, D], BF16))
              for i in range(NT)]

        own_prog = p is None
        if own_prog:
            p = Prog(nc)
        p.default_waits = list(gate)
        cV, cC, cP = (p.ctr(n) for n in ("v", "c", "pe"))
        cSc = p.ctr("sc")
        cXR = [p.ctr("xr0"), p.ctr("xr1")]
        cWG = [p.ctr("wg0"), p.ctr("wg1")]
        cWU = [p.ctr("wu0"), p.ctr("wu1")]
        cWD = [p.ctr("wd0"), p.ctr("wd1")]
        cO = [p.ctr("o0"), p.ctr("o1")]

        t_id = emit_identity(p, ident, one1)
        p.task("sync", lambda e: e.dma_start(out=gsc[:], in_=gs_c), inc=cSc, inc_k=16)
        p.task("sync", lambda e: e.dma_start(out=usc[:], in_=us_c), inc=cSc, inc_k=16)
        p.task("sync", lambda e: e.dma_start(out=dsc[:], in_=ds_c), inc=cSc, inc_k=16)
        t_sc_dma = cSc.n

        def f_z(e):
            e.tensor_scalar(out=gzc[:], in0=gsc[:], scalar1=128.0, scalar2=None,
                            op0=ALU.mult)
            e.tensor_scalar(out=uzc[:], in0=usc[:], scalar1=128.0, scalar2=None,
                            op0=ALU.mult)
            return e.tensor_scalar(out=dzc[:], in0=dsc[:], scalar1=128.0, scalar2=None,
                                   op0=ALU.mult)
        p.task("vector", f_z, waits=[(cSc, t_sc_dma)], inc=cV)
        t_scales = cV.n

        prev_v = t_scales
        xr_free = {0: 0, 1: 0}
        for i in range(NT):
            b = i % 2
            p.task("sync", lambda e, i=i, b=b: e.dma_start(
                out=XR[b][:], in_=x_in[i * 128:(i + 1) * 128, :]),
                waits=[(cV, xr_free[b])], inc=cXR[b], inc_k=16)
            t_ld = cXR[b].n
            prev_v = emit_ln_chunk(p, XR[b], xn[i], dump, cols,
                                   waits=[(cXR[b], t_ld)], prev_v=prev_v)
            xr_free[b] = prev_v
        tiles = [(xn[i][:, c * 128:(c + 1) * 128], xf[c][:, i * 128:(i + 1) * 128])
                 for c in range(ND) for i in range(NT)]
        t_xf = emit_tm_to_fm(p, tiles, ident, p_tr, waits=[(cV, prev_v)], t_ident=t_id, psum2=p_tr2)
        xn_es.close()
        Y2 = [sb(f"Y2{i}", [128, D], F32) for i in range(NT)]

        up_tiles_done = {}
        raw_free = {id(wgraw): {0: 0, 1: 0}, id(wuraw): {0: 0, 1: 0}}
        wdraw_free = {0: 0, 1: 0}
        wqnat_free = {0: 0, 1: 0, "dn": 0}
        h_done = {}
        y1_consumed = {}
        y2_evac = {}
        dn_psum_free = {0: None, 1: None}
        wb_used = {0: 0, 1: 0}
        wdb_used = {0: 0, 1: 0}

        def sched_gu_weight(m, raw, wb, q_ap, s_sb, z_sb, ctr, hb):
            b = m % 2
            base = hb * 2048
            p.task("sync", lambda e, m=m, b=b, raw=raw, q_ap=q_ap: e.dma_start(
                out=raw[b][:], in_=q_ap[m * 128:(m + 1) * 128, :]),
                waits=[(cV, raw_free[id(raw)][b])], inc=ctr[b], inc_k=16)
            t_dma = ctr[b].n
            for s4 in range(4):
                def f_dq(e, m=m, b=b, s4=s4, raw=raw, s_sb=s_sb, z_sb=z_sb, base=base):
                    return e.tensor_scalar(
                        out=wq_nat[:, base + s4 * 512:base + (s4 + 1) * 512],
                        in0=raw[b][:, s4 * 512:(s4 + 1) * 512],
                        scalar1=s_sb[:, m:m + 1], scalar2=z_sb[:, m:m + 1],
                        op0=ALU.mult, op1=ALU.subtract)
                p.task("vector", f_dq,
                       waits=[(ctr[b], t_dma), (cV, t_scales),
                              (cC, wqnat_free[hb])], inc=cV)
            raw_free[id(raw)][b] = cV.n
            t_dq = cV.n
            tiles = [(wq_nat[:, base + k * 128:base + (k + 1) * 128],
                      wb[b][:, k * 128:(k + 1) * 128]) for k in range(ND)]
            t = emit_tm_to_fm(p, tiles, ident, p_tr,
                              waits=[(cV, t_dq), (cP, wb_used[b])], t_ident=t_id, psum2=p_tr2)
            wqnat_free[hb] = t
            return t

        def sched_gu_compute(m, g_idx, gi):
            b = m % 2
            for tb in range(2):
                def f_g(e, m=m, b=b, tb=tb):
                    last = None
                    for k in range(ND):
                        last = e.matmul(out=p_g[tb][:, 0:512],
                                        lhsT=wgb[b][:, k * 128:(k + 1) * 128],
                                        rhs=xf[k][:, tb * 512:(tb + 1) * 512],
                                        start=(k == 0), stop=(k == ND - 1))
                    return last
                p.task("tensor", f_g,
                       waits=[(cC, up_tiles_done[m]),
                              (cV, h_done.get((m - 1, tb), 0))],
                       inc=cP)
                t_gmm = cP.n

                def f_silu(e, tb=tb, b=b):
                    return e.activation(out=g_sb[tb][:], in_=p_g[tb][:, 0:512],
                                        func=AF.Silu)
                p.task("scalar", f_silu, waits=[(cP, t_gmm)], inc=cC)
                t_silu = cC.n

                def f_u(e, m=m, b=b, tb=tb):
                    last = None
                    for k in range(ND):
                        last = e.matmul(out=p_u[tb][:, 0:512],
                                        lhsT=wub[b][:, k * 128:(k + 1) * 128],
                                        rhs=xf[k][:, tb * 512:(tb + 1) * 512],
                                        start=(k == 0), stop=(k == ND - 1))
                    return last
                p.task("tensor", f_u,
                       waits=[(cC, up_tiles_done[m]),
                              (cV, h_done.get((m - 1, tb), 0))],
                       inc=cP)
                t_umm = cP.n

                def f_h(e, m=m, tb=tb, g_idx=g_idx):
                    return e.tensor_tensor(out=y1[g_idx][:, tb * 512:(tb + 1) * 512],
                                           in0=g_sb[tb][:], in1=p_u[tb][:, 0:512],
                                           op=ALU.mult)
                p.task("vector", f_h, waits=[(cP, t_umm), (cC, t_silu),
                                             (cP, y1_consumed.get(gi - 1, 0))], inc=cV)
                h_done[(m, tb)] = cV.n
            wb_used[b] = cP.n

        def sched_dn_weight(gi, ds, buf):
            for dci in range(4):
                dc = ds * 4 + dci
                p.task("sync", lambda e, dc=dc, gi=gi, buf=buf: e.dma_start(
                    out=wdraw[buf][:],
                    in_=dw_q[dc * 128:(dc + 1) * 128,
                             gi * GRP * 128:(gi + 1) * GRP * 128]),
                    waits=[(cV, wdraw_free[buf])], inc=cWD[buf], inc_k=16)
                t_dma = cWD[buf].n

                def f_dq(e, dc=dc, buf=buf, dci=dci):
                    return e.tensor_scalar(
                        out=wq_nat[:, dci * 1024:(dci + 1) * 1024],
                        in0=wdraw[buf][:],
                        scalar1=dsc[:, dc:dc + 1], scalar2=dzc[:, dc:dc + 1],
                        op0=ALU.mult, op1=ALU.subtract)
                p.task("vector", f_dq, waits=[(cWD[buf], t_dma), (cV, t_scales),
                                              (cC, max(wqnat_free[0], wqnat_free[1],
                                                       wqnat_free["dn"]))], inc=cV)
                wdraw_free[buf] = cV.n
            t_dq = cV.n
            tiles = [(wq_nat[:, dci * 1024 + mj * 128: dci * 1024 + (mj + 1) * 128],
                      wdb[buf][:, mj * 512 + dci * 128: mj * 512 + (dci + 1) * 128])
                     for mj in range(GRP) for dci in range(4)]
            t_w = emit_tm_to_fm(p, tiles, ident, p_tr,
                                waits=[(cV, t_dq), (cP, wdb_used[buf])], t_ident=t_id, psum2=p_tr2)
            wqnat_free[0] = wqnat_free[1] = wqnat_free["dn"] = t_w
            return t_w

        def sched_dn_compute(gi, ds, buf, t_w):
            for tt in range(NT):
                pb = tt % 2

                def f_mm(e, gi=gi, ds=ds, buf=buf, tt=tt, pb=pb):
                    last = None
                    for mj in range(GRP):
                        last = e.matmul(out=p_dn[pb][:, 0:512],
                                        lhsT=y1[mj][:, tt * 128:(tt + 1) * 128],
                                        rhs=wdb[buf][:, mj * 512:(mj + 1) * 512],
                                        start=(mj == 0), stop=(mj == GRP - 1))
                    return last
                wl = [(cC, t_w), (cV, h_done[(gi * GRP + GRP - 1, 1)])]
                if dn_psum_free[pb] is not None:
                    wl.append(dn_psum_free[pb])
                prev = y2_evac.get((tt, ds))
                if prev is not None:
                    wl.append(prev)
                p.task("tensor", f_mm, waits=wl, inc=cP)
                t_mm = cP.n

                if gi == 0:
                    p.task("scalar", lambda e, tt=tt, ds=ds, pb=pb: e.copy(
                        out=Y2[tt][:, ds * 512:(ds + 1) * 512], in_=p_dn[pb][:, 0:512]),
                        waits=[(cP, t_mm)], inc=cC)
                    y2_evac[(tt, ds)] = (cC, cC.n)
                    dn_psum_free[pb] = (cC, cC.n)
                else:
                    def f_add(e, tt=tt, ds=ds, pb=pb):
                        return e.tensor_tensor(
                            out=Y2[tt][:, ds * 512:(ds + 1) * 512],
                            in0=Y2[tt][:, ds * 512:(ds + 1) * 512],
                            in1=p_dn[pb][:, 0:512], op=ALU.add)
                    wl2 = [(cP, t_mm)]
                    prev2 = y2_evac.get((tt, ds))
                    if prev2 is not None and prev2[0] is cC:
                        wl2.append(prev2)
                    p.task("vector", f_add, waits=wl2, inc=cV)
                    y2_evac[(tt, ds)] = (cV, cV.n)
                    dn_psum_free[pb] = (cV, cV.n)
            wdb_used[buf] = cP.n

        import os as _os

        def dbg_store(bufs, n):
            cOg = p.ctr("og")
            for i, bp in enumerate(bufs[:n]):
                sh = bp.ap().shape if hasattr(bp, 'ap') else bp.shape
                p.task("gpsimd", lambda e, i=i, bp=bp, sh=sh: e.dma_start(
                    out=x_out[i * 128:i * 128 + sh[0], 0:sh[1]], in_=bp[:]),
                    waits=[(cV, cV.n), (cC, cC.n), (cP, cP.n)], inc=cOg, inc_k=16)
            p.task("sync", lambda e: e.nop(), waits=[(cOg, cOg.n)])
            if own_prog:
                p.finish()

        def gu_w(m):
            tg = sched_gu_weight(m, wgraw, wgb, gw_q, gsc, gzc, cWG, 0)
            tu = sched_gu_weight(m, wuraw, wub, uw_q, usc, uzc, cWU, 1)
            up_tiles_done[m] = max(tg, tu)

        gu_w(0)
        dn_w = {}
        for gi in range(NG):
            for mj in range(GRP):
                m = gi * GRP + mj
                if m + 1 < NH:
                    gu_w(m + 1)
                sched_gu_compute(m, mj, gi)
            dn_w[(gi, 0)] = sched_dn_weight(gi, 0, (gi * 4) % 2)
            for ds in range(4):
                buf = (gi * 4 + ds) % 2
                if ds + 1 < 4:
                    dn_w[(gi, ds + 1)] = sched_dn_weight(gi, ds + 1, (gi * 4 + ds + 1) % 2)
                sched_dn_compute(gi, ds, buf, dn_w[(gi, ds)])
            y1_consumed[gi] = cP.n

        t_y2_all_c = cC.n
        t_y2_all_v = cV.n
        prev_v2 = 0
        xr_store = {0: 0, 1: 0}
        for i in range(NT):
            b = i % 2
            p.task("sync", lambda e, i=i, b=b: e.dma_start(
                out=XR[b][:], in_=x_in[i * 128:(i + 1) * 128, :]),
                waits=[(cO[b], xr_store[b]), (cV, t_y2_all_v if i < 2 else 0)],
                inc=cXR[b], inc_k=16)
            t_ld = cXR[b].n
            tv = emit_ln_chunk(p, Y2[i], None, dump, cols,
                               waits=[(cXR[b], t_ld), (cV, t_y2_all_v),
                                      (cC, t_y2_all_c)],
                               prev_v=prev_v2, out_ap=XR[b], resid_ap=XR[b])
            prev_v2 = tv
            p.task("sync", lambda e, i=i, b=b: e.dma_start(
                out=x_out[i * 128:(i + 1) * 128, :], in_=XR[b][:]),
                waits=[(cV, tv)], inc=cO[b], inc_k=16)
            xr_store[b] = cO[b].n
        p.task("sync", lambda e: e.nop(), waits=[(cO[0], cO[0].n), (cO[1], cO[1].n)])
        if own_prog:
            p.finish()
    return nc


NHEAD = 16
HSLAB = 4
NSLAB = NHEAD // HSLAB


def build_l0a(nc, stage=99, p=None, io=None, gate=()):
    x_own = io["x_own"] if io is not None else nc.dram_tensor("x_own", [T, D], F32, kind="ExternalInput").ap()
    x_pre = io["x_pre"] if io is not None else nc.dram_tensor("x_pre", [T, D], F32, kind="ExternalInput").ap()
    wq = io["wq"] if io is not None else nc.dram_tensor("wq", [D, D], F32, kind="ExternalInput").ap()
    wk = io["wk"] if io is not None else nc.dram_tensor("wk", [D, D], F32, kind="ExternalInput").ap()
    wv = io["wv"] if io is not None else nc.dram_tensor("wv", [D, D], F32, kind="ExternalInput").ap()
    wo = io["wo"] if io is not None else nc.dram_tensor("wo", [D, D], F32, kind="ExternalInput").ap()
    wg1 = io["wg1"] if io is not None else nc.dram_tensor("wg1", [D, 128], F32, kind="ExternalInput").ap()
    wg2 = io["wg2"] if io is not None else nc.dram_tensor("wg2", [128, D], F32, kind="ExternalInput").ap()
    nw_c = io["nw_c"] if io is not None else nc.dram_tensor("nw_c", [128, NHEAD], F32, kind="ExternalInput").ap()
    ld_r = io["ld_r"] if io is not None else nc.dram_tensor("ld_r", [1, NHEAD], F32, kind="ExternalInput").ap()
    x_out = io["x_out"] if io is not None else nc.dram_tensor("x_out", [T, D], F32, kind="ExternalOutput").ap()

    es = contextlib.ExitStack()
    with es:
        U = _uid()
        sb = lambda name, shape, dt=BF16: es.enter_context(nc.sbuf_tensor(U + name, shape, dt))
        ps = lambda name, shape, dt=F32: es.enter_context(nc.psum_tensor(U + name, shape, dt))
        XR = [sb(f"XR{b}", [128, D], F32) for b in range(2)]
        xf = [sb(f"xf{c}", [128, T]) for c in range(ND)]
        O_tm = [sb(f"O{c}", [128, D]) for c in range(NT)]
        S_sb = sb("S_sb", [128, NHEAD * 128], F32)
        S_bf = sb("S_bf", [128, NHEAD * 128])
        DTm = sb("DTm", [128, NHEAD * 128])
        ldc = sb("ldc", [128, NHEAD], F32)
        qdec = sb("qdec", [128, NHEAD], F32)
        kdec = sb("kdec", [128, NHEAD], F32)
        lam = sb("lam", [128, NHEAD], F32)
        idxc = sb("idxc", [128, 2], F32)
        diff = sb("diff", [128, 128], F32)
        nwc = sb("nwc", [128, NHEAD], F32)
        wg2_sb = sb("wg2_sb", [128, D])
        wg1_sb = sb("wg1_sb", [128, ND * 128])
        g1 = sb("g1", [128, T])
        ident = sb("ident", [128, 128])
        one1 = sb("one1", [1, 128])
        ld_row = sb("ld_row", [1, NHEAD], F32)
        one1f = sb("one1f", [1, 128], F32)
        dump = sb("dump", [128, D])
        cols = sb("cols", [128, 8], F32)
        gate_sb = sb("gate_sb", [128, T])
        oc_f32 = sb("oc_f32", [128, 128], F32)
        kfm = sb("kfm", [128, 512])
        ksc = sb("ksc", [128, 128])
        scb = sb("scb", [128, 128])
        xnc = sb("xnc", [128, D])
        means = sb("means", [128, NHEAD], F32)
        rst = sb("rst", [128, NHEAD], F32)
        rst2 = sb("rst2", [128, NHEAD], F32)
        p_tr = ps("p_tr", [128, 512])
        p_tr2 = ps("p_tr2", [128, 512])
        p_pj = [ps(f"p_pj{b}", [128, 512]) for b in range(2)]
        p_sc = ps("p_sc", [128, 512])
        p_oi = ps("p_oi", [128, 512])
        p_oe = ps("p_oe", [128, 512])
        p_st = ps("p_st", [128, 512])

        own_prog = p is None
        if own_prog:
            p = Prog(nc)
        p.default_waits = list(gate)
        cV, cC, cP = (p.ctr(n) for n in ("v", "c", "pe"))
        cQ = p.ctr("q")
        cG = p.ctr("g")
        cXR = [p.ctr("xr0"), p.ctr("xr1")]
        cW = [p.ctr("w0"), p.ctr("w1")]
        cO = [p.ctr("o0"), p.ctr("o1")]

        t_id = emit_identity(p, ident, one1)
        p.task("gpsimd", lambda e: e.memset(one1f[:], 1.0), inc=p.ctr("q"))
        t_id = (t_id[0], t_id[0].n)

        cSc = p.ctr("sc")
        cLd = p.ctr("ldr")
        p.task("sync", lambda e: e.dma_start(out=ld_row[:], in_=ld_r), inc=cLd, inc_k=16)
        t_ld = cLd.n

        def f_iota(e):
            e.iota(idxc[:, 0:1], [[1, 1]], base=0, channel_multiplier=1,
                   allow_small_or_imprecise_dtypes=True)
            return e.iota(diff[:], [[1, 128]], base=0, channel_multiplier=-1,
                          allow_small_or_imprecise_dtypes=True)
        p.task("gpsimd", f_iota, inc=cQ)
        t_iota0 = cQ.n
        p.task("gpsimd", lambda e: e.affine_select(
            out=diff[:], in_=diff[:], compare_op=ALU.is_ge, fill=0.0, base=0,
            pattern=[[1, 128]], channel_multiplier=-1),
            waits=[(cQ, t_iota0)], inc=cQ)
        t_iota = cQ.n

        def f_ldc(e):
            return e.matmul(out=p_tr[:, 0:NHEAD], lhsT=one1f[:], rhs=ld_row[:],
                            start=True, stop=True)
        p.task("tensor", f_ldc, waits=[(cLd, t_ld), (cQ, t_id[1])], inc=cP)
        p.task("scalar", lambda e: e.copy(out=ldc[:], in_=p_tr[:, 0:NHEAD]),
               waits=[(cP, cP.n)], inc=cC)
        t_ldc = cC.n

        def f_idx(e):
            return e.tensor_scalar(out=idxc[:, 1:2], in0=idxc[:, 0:1], scalar1=1.0,
                                   scalar2=None, op0=ALU.add)
        p.task("vector", f_idx, waits=[(cQ, t_iota)], inc=cV)

        def f_idx2(e):
            return e.tensor_scalar(out=idxc[:, 0:1], in0=idxc[:, 0:1], scalar1=127.0,
                                   scalar2=-1.0, op0=ALU.subtract, op1=ALU.mult)
        p.task("vector", f_idx2, waits=[(cV, cV.n)], inc=cV)
        t_idx = cV.n

        def f_dec(e):
            e.activation(out=qdec[:], in_=ldc[:], func=AF.Exp, scale=idxc[:, 1:2])
            e.activation(out=kdec[:], in_=ldc[:], func=AF.Exp, scale=idxc[:, 0:1])
            return e.activation(out=lam[:], in_=ldc[:], func=AF.Exp, scale=128.0)
        p.task("scalar", f_dec, waits=[(cC, t_ldc), (cV, t_idx)], inc=cC)
        t_dec = cC.n

        for h in range(NHEAD):
            p.task("scalar", lambda e, h=h: e.activation(
                out=DTm[:, h * 128:(h + 1) * 128], in_=diff[:], func=AF.Exp,
                scale=ldc[:, h:h + 1]), waits=[(cC, t_ldc), (cQ, t_iota)], inc=cC)
            t_e = cC.n
            p.task("gpsimd", lambda e, h=h: e.affine_select(
                out=DTm[:, h * 128:(h + 1) * 128], in_=DTm[:, h * 128:(h + 1) * 128],
                compare_op=ALU.is_ge, fill=0.0, base=0, pattern=[[1, 128]],
                channel_multiplier=-1), waits=[(cC, t_e)], inc=cQ)
        t_dtm = cQ.n

        p.task("gpsimd", lambda e: e.dma_start(out=wg2_sb[:], in_=wg2), inc=cG, inc_k=16)
        p.task("gpsimd", lambda e: e.dma_start(
            out=wg1_sb[:].rearrange("p (k r) -> p k r", k=ND),
            in_=wg1.rearrange("(k p) r -> p k r", p=128)), inc=cG, inc_k=16)
        t_wg = cG.n
        p.task("sync", lambda e: e.dma_start(out=nwc[:], in_=nw_c), inc=cSc, inc_k=16)
        t_nwb = 0
        t_nwc = cSc.n

        def f_s0(e):
            e.memset(S_sb[:], 0.0)
            return e.memset(S_bf[:], 0.0)
        p.task("gpsimd", lambda e: e.memset(S_sb[:], 0.0), inc=cQ)
        p.task("gpsimd", lambda e: e.memset(S_bf[:], 0.0), inc=cQ)
        t_s0 = cQ.n

        wbuf = [sb(f"wbuf{b}", [128, 16 * 512]) for b in range(2)]
        kq_sl = [sb(f"kq{b}", [128, HSLAB * 1024]) for b in range(2)]
        K_sl = [sb(f"Ksl{b}", [128, NT * 512]) for b in range(2)]
        V_sl = [sb(f"Vsl{b}", [128, NT * 512]) for b in range(2)]

        wbuf_used = {0: 0, 1: 0}
        kv_used = {0: (cP, 0), 1: (cP, 0)}
        s_ready = {}
        ln_prev = [0]

        def emit_half_ln(x_ap_dram, out_xf=True):
            prev_v = ln_prev[0]
            for i in range(NT):
                b = i % 2
                p.task("sync", lambda e, i=i, b=b: e.dma_start(
                    out=XR[b][:], in_=x_ap_dram[i * 128:(i + 1) * 128, :]),
                    waits=[(cV, xr_free_ln[b])], inc=cXR[b], inc_k=16)
                t_ldx = cXR[b].n
                prev_v = emit_ln_chunk(p, XR[b], xnc, dump, cols,
                                       waits=[(cXR[b], t_ldx), xnc_free[0]],
                                       prev_v=prev_v)
                xr_free_ln[b] = prev_v
                tiles = [(xnc[:, c * 128:(c + 1) * 128], xf[c][:, i * 128:(i + 1) * 128])
                         for c in range(ND)]
                t_tr = emit_tm_to_fm(p, tiles, ident, p_tr,
                                     waits=[(cV, prev_v)], t_ident=t_id, psum2=p_tr2)
                xnc_free[0] = (cP, p.ctrs["pe"].n)
                prev_v = prev_v
                ln_gate[0] = t_tr
            ln_prev[0] = prev_v
            return ln_gate[0]

        xr_free_ln = {0: 0, 1: 0}
        xnc_free = [(cP, 0)]
        ln_gate = [0]


        def sched_proj_slab(ms, xf_src, own):
            b = ms % 2
            col0 = ms * 512
            projs = [(wk, 0), (wv, 1)] + ([(wq, 2)] if own else [])
            for (w_ap, wi) in projs:
                p.task("gpsimd", lambda e, w_ap=w_ap, wi=wi, col0=col0, b=b: e.dma_start(
                    out=wbuf[b][:, wi * 512 * 0:16 * 512].rearrange(
                        "p (k f) -> p k f", k=16) if False else
                    wbuf[b][:].rearrange("p (k f) -> p k f", k=16),
                    in_=w_ap[:, col0:col0 + 512].rearrange("(k p) f -> p k f", p=128)),
                    waits=[(cP, wbuf_used[b])], inc=cW[b], inc_k=16)
                t_w = cW[b].n
                if wi == 0 or wi == 1:
                    dst = K_sl[b] if wi == 0 else V_sl[b]
                    for c in range(NT):
                        def f_mm(e, c=c, b=b, t_w=t_w):
                            last = None
                            for k in range(ND):
                                last = e.matmul(out=p_pj[c % 2][:, 0:512],
                                                lhsT=xf_src[k][:, c * 128:(c + 1) * 128],
                                                rhs=wbuf[b][:, k * 512:(k + 1) * 512],
                                                start=(k == 0), stop=(k == ND - 1))
                            return last
                        p.task("tensor", f_mm,
                               waits=[(cW[b], t_w),
                                      (cC, max(pj_evac[c % 2], ln_gate[0]))],
                               inc=cP)
                        t_mm = cP.n
                        p.task("scalar", lambda e, c=c, b=b, dst=dst: e.activation(
                            out=dst[:, c * 512:(c + 1) * 512], in_=p_pj[c % 2][:, 0:512],
                            func=AF.Silu), waits=[(cP, t_mm)], inc=cC)
                        pj_evac[c % 2] = cC.n
                else:
                    for hh in range(HSLAB):
                        for tb in range(2):
                            def f_mm(e, hh=hh, tb=tb, b=b, col0=col0, t_w=t_w):
                                last = None
                                for k in range(ND):
                                    last = e.matmul(
                                        out=p_pj[tb % 2][:, 0:512],
                                        lhsT=wbuf[b][:, k * 512 + hh * 128:
                                                     k * 512 + (hh + 1) * 128],
                                        rhs=xf_src[k][:, tb * 512:(tb + 1) * 512],
                                        start=(k == 0), stop=(k == ND - 1))
                                return last
                            p.task("tensor", f_mm,
                                   waits=[(cW[b], t_w),
                                          (cC, max(pj_evac[tb % 2], ln_gate[0]))],
                                   inc=cP)
                            t_mm = cP.n
                            p.task("scalar", lambda e, hh=hh, tb=tb, b=b: e.activation(
                                out=kq_sl[b][:, hh * 1024 + tb * 512:
                                             hh * 1024 + (tb + 1) * 512],
                                in_=p_pj[tb % 2][:, 0:512], func=AF.Silu),
                                waits=[(cP, t_mm)], inc=cC)
                            pj_evac[tb % 2] = cC.n
                wbuf_used[b] = cP.n
            return cC.n

        pj_evac = {0: 0, 1: 0}

        def sched_state_chunk(ms, b, c, t_kv, own):
            t_kfm = 0
            if own:
                def f_tr4(e, b=b, c=c):
                    last = None
                    for hh in range(HSLAB):
                        last = e.matmul(out=p_tr[:, hh * 128:(hh + 1) * 128],
                                        lhsT=K_sl[b][:, c * 512 + hh * 128:
                                                     c * 512 + (hh + 1) * 128],
                                        rhs=ident[:], start=True, stop=True)
                    return last
                p.task("tensor", f_tr4, waits=[(cC, max(t_kv, kfm_free[0]))], inc=cP)
                t_trm = cP.n
                p.task("scalar", lambda e: e.copy(out=kfm[:], in_=p_tr[:, 0:512]),
                       waits=[(cP, t_trm)], inc=cC)
                t_kfm = cC.n
                kfm_free[0] = t_kfm
            for hh in range(HSLAB):
                h = ms * HSLAB + hh
                if own:
                    def f_sc(e, b=b, c=c, hh=hh):
                        return e.matmul(out=p_sc[:, 0:128],
                                        lhsT=kfm[:, hh * 128:(hh + 1) * 128],
                                        rhs=kq_sl[b][:, hh * 1024 + c * 128:
                                                     hh * 1024 + (c + 1) * 128],
                                        start=True, stop=True)
                    p.task("tensor", f_sc, waits=[(cC, t_kfm), (cV, scb_free[0])],
                           inc=cP)
                    t_scm = cP.n
                    def f_msk(e, h=h):
                        return e.tensor_tensor(out=scb[:], in0=p_sc[:, 0:128],
                                               in1=DTm[:, h * 128:(h + 1) * 128],
                                               op=ALU.mult)
                    p.task("vector", f_msk, waits=[(cP, t_scm), (cQ, t_dtm)], inc=cV)
                    t_msk = cV.n
                    scb_free[0] = t_msk
                    def f_oi(e, b=b, c=c, hh=hh):
                        return e.matmul(out=p_oi[:, 0:128], lhsT=scb[:],
                                        rhs=V_sl[b][:, c * 512 + hh * 128:
                                                    c * 512 + (hh + 1) * 128],
                                        start=True, stop=True)
                    p.task("tensor", f_oi, waits=[(cV, t_msk), (cV, oi_free[0])], inc=cP)
                    t_oim = cP.n
                    def f_oe(e, b=b, c=c, hh=hh, h=h):
                        return e.matmul(out=p_oe[:, 0:128],
                                        lhsT=kq_sl[b][:, hh * 1024 + c * 128:
                                                      hh * 1024 + (c + 1) * 128],
                                        rhs=S_bf[:, h * 128:(h + 1) * 128],
                                        start=True, stop=True)
                    p.task("tensor", f_oe,
                           waits=[(cV, max(s_ready.get(h, 0), oi_free[0]))], inc=cP)
                    t_oem = cP.n
                    def f_cmb1(e, h=h):
                        return e.tensor_scalar(out=oc_f32[:], in0=p_oe[:, 0:128],
                                               scalar1=qdec[:, h:h + 1], scalar2=None,
                                               op0=ALU.mult)
                    p.task("vector", f_cmb1, waits=[(cP, max(t_oem, t_oim)),
                                                    (cC, t_dec)], inc=cV)

                    def f_cmb2(e, c=c, h=h):
                        return e.tensor_tensor(out=O_tm[c][:, h * 128:(h + 1) * 128],
                                               in0=oc_f32[:], in1=p_oi[:, 0:128],
                                               op=ALU.add)
                    p.task("vector", f_cmb2, waits=[(cV, cV.n)], inc=cV)
                    oi_free[0] = cV.n
                def f_ks(e, b=b, c=c, hh=hh, h=h):
                    return e.tensor_scalar(out=ksc[:],
                                           in0=K_sl[b][:, c * 512 + hh * 128:
                                                       c * 512 + (hh + 1) * 128],
                                           scalar1=kdec[:, h:h + 1], scalar2=None,
                                           op0=ALU.mult)
                p.task("vector", f_ks, waits=[(cC, max(t_kv, t_dec)),
                                              (cP, ksc_free[0])], inc=cV)
                t_ks = cV.n

                def f_st(e, b=b, c=c, hh=hh):
                    return e.matmul(out=p_st[:, 0:128], lhsT=ksc[:],
                                    rhs=V_sl[b][:, c * 512 + hh * 128:
                                                c * 512 + (hh + 1) * 128],
                                    start=True, stop=True)
                p.task("tensor", f_st, waits=[(cV, max(t_ks, st_free[0]))], inc=cP)
                t_stm = cP.n
                ksc_free[0] = cP.n

                def f_su(e, h=h):
                    e.tensor_scalar(out=S_sb[:, h * 128:(h + 1) * 128],
                                    in0=S_sb[:, h * 128:(h + 1) * 128],
                                    scalar1=lam[:, h:h + 1], scalar2=None,
                                    op0=ALU.mult)
                    e.tensor_tensor(out=S_sb[:, h * 128:(h + 1) * 128],
                                    in0=S_sb[:, h * 128:(h + 1) * 128],
                                    in1=p_st[:, 0:128], op=ALU.add)
                    return e.tensor_copy(out=S_bf[:, h * 128:(h + 1) * 128],
                                         in_=S_sb[:, h * 128:(h + 1) * 128])
                wl = [(cP, t_stm), (cC, t_dec)]
                p.task("vector", f_su, waits=wl, inc=cV)
                s_ready[h] = cV.n
                st_free[0] = cV.n

        kfm_free = [0]
        scb_free = [0]
        oi_free = [0]
        ksc_free = [0]
        st_free = [0]

        def dbg_store(bufs, n):
            cOg = p.ctr("og")
            for i, bp in enumerate(bufs[:n]):
                sh = bp.ap().shape if hasattr(bp, 'ap') else bp.shape
                p.task("gpsimd", lambda e, i=i, bp=bp, sh=sh: e.dma_start(
                    out=x_out[i * 128:i * 128 + sh[0], 0:sh[1]], in_=bp[:]),
                    waits=[(cV, cV.n), (cC, cC.n), (cP, cP.n), (cQ, cQ.n)],
                    inc=cOg, inc_k=16)
            p.task("sync", lambda e: e.nop(), waits=[(cOg, cOg.n)])
            if own_prog:
                p.finish()

        emit_half_ln(x_pre)
        for ms in range(NSLAB):
            b = ms % 2
            t_kv = sched_proj_slab(ms, xf, own=False)
            for c in range(NT):
                sched_state_chunk(ms, b, c, t_kv, own=False)
            kv_used[b] = (cP, p.ctrs["pe"].n)
            if stage == 1 and ms == 0:
                dbg_store([S_sb], 1); return
        if stage == 2:
            dbg_store([S_sb], 1); return

        emit_half_ln(x_own)
        def f_g1(e):
            last = None
            for tb in range(2):
                for k in range(ND):
                    last = e.matmul(out=p_pj[0][:, 0:512],
                                    lhsT=wg1_sb[:, k * 128:(k + 1) * 128],
                                    rhs=xf[k][:, tb * 512:(tb + 1) * 512],
                                    start=(k == 0), stop=(k == ND - 1))
            return last
        for tb in range(2):
            def f_g1t(e, tb=tb):
                last = None
                for k in range(ND):
                    last = e.matmul(out=p_pj[0][:, 0:512],
                                    lhsT=wg1_sb[:, k * 128:(k + 1) * 128],
                                    rhs=xf[k][:, tb * 512:(tb + 1) * 512],
                                    start=(k == 0), stop=(k == ND - 1))
                return last
            p.task("tensor", f_g1t, waits=[(cC, max(ln_gate[0], pj_evac[0])),
                                           (cG, t_wg)], inc=cP)
            t_g1 = cP.n
            p.task("scalar", lambda e, tb=tb: e.copy(
                out=g1[:, tb * 512:(tb + 1) * 512], in_=p_pj[0][:, 0:512]),
                waits=[(cP, t_g1)], inc=cC)
            pj_evac[0] = cC.n

        for ms in range(NSLAB):
            b = ms % 2
            t_kv = sched_proj_slab(ms, xf, own=True)
            for c in range(NT):
                sched_state_chunk(ms, b, c, t_kv, own=True)
            kv_used[b] = (cP, p.ctrs["pe"].n)
            if stage == 3 and ms == 0:
                dbg_store(O_tm, 1); return
        if stage == 4:
            dbg_store(O_tm, NT); return

        gn_done = []
        gsb_free = {}
        for c in range(NT):
            t_og_l = 0
            for s4 in range(4):
                def f_gt(e, c=c, s4=s4):
                    return e.matmul(out=p_pj[s4 % 2][:, 0:512],
                                    lhsT=g1[:, c * 128:(c + 1) * 128],
                                    rhs=wg2_sb[:, s4 * 512:(s4 + 1) * 512],
                                    start=True, stop=True)
                p.task("tensor", f_gt, waits=[(cC, max(pj_evac[s4 % 2], 0)),
                                              (cG, t_wg)], inc=cP)
                t_gt = cP.n
                p.task("scalar", lambda e, s4=s4: e.activation(
                    out=gate_sb[:, (s4 % 2) * 512:(s4 % 2) * 512 + 512],
                    in_=p_pj[s4 % 2][:, 0:512], func=AF.Sigmoid),
                    waits=[(cP, t_gt), (cV, gsb_free.get(s4 % 2, 0))], inc=cC)
                pj_evac[s4 % 2] = cC.n
                t_gate = cC.n

                def f_og(e, c=c, s4=s4):
                    return e.tensor_tensor(
                        out=O_tm[c][:, s4 * 512:(s4 + 1) * 512],
                        in0=O_tm[c][:, s4 * 512:(s4 + 1) * 512],
                        in1=gate_sb[:, (s4 % 2) * 512:(s4 % 2) * 512 + 512],
                        op=ALU.mult)
                p.task("vector", f_og, waits=[(cC, t_gate),
                                              (cV, gn_done[-1] if gn_done else 0)],
                       inc=cV)
                gsb_free[s4 % 2] = cV.n
                t_og_l = cV.n
            t_og = t_og_l
            for h in range(NHEAD):
                p.task("scalar", lambda e, c=c, h=h: e.activation(
                    out=dump[:, h * 128:(h + 1) * 128],
                    in_=O_tm[c][:, h * 128:(h + 1) * 128], func=AF.Square,
                    accum_out=means[:, h:h + 1]),
                    waits=[(cV, t_og)], inc=cC)
            t_mean = cC.n

            def f_rst(e):
                return e.tensor_scalar(out=rst2[:], in0=means[:], scalar1=1.0 / 128.0,
                                       scalar2=EPS, op0=ALU.mult, op1=ALU.add)
            p.task("vector", f_rst, waits=[(cC, t_mean)], inc=cV)
            t_v1 = cV.n
            p.task("scalar", lambda e: e.activation(out=means[:], in_=rst2[:],
                                                    func=AF.Sqrt),
                   waits=[(cV, t_v1)], inc=cC)
            t_sq = cC.n
            p.task("vector", lambda e: e.reciprocal(out=rst[:], in_=means[:]),
                   waits=[(cC, t_sq)], inc=cV)
            t_rc = cV.n
            for h in range(NHEAD):
                p.task("vector", lambda e, c=c, h=h: e.tensor_scalar(
                    out=O_tm[c][:, h * 128:(h + 1) * 128],
                    in0=O_tm[c][:, h * 128:(h + 1) * 128],
                    scalar1=rst[:, h:h + 1], scalar2=None, op0=ALU.mult),
                    waits=[(cV, t_rc)], inc=cV)
            gn_done.append(cV.n)

        if stage == 5:
            dbg_store(O_tm, NT); return
        tiles = [(O_tm[i][:, c * 128:(c + 1) * 128], xf[c][:, i * 128:(i + 1) * 128])
                 for c in range(ND) for i in range(NT)]
        t_o4f = emit_tm_to_fm(p, tiles, ident, p_tr,
                              waits=[(cV, gn_done[-1])], t_ident=t_id, psum2=p_tr2)

        delta = O_tm
        wo_evac = {0: 0, 1: 0}
        wo_scaled = {}
        for ms in range(4):
            b = ms % 2
            p.task("gpsimd", lambda e, ms=ms, b=b: e.dma_start(
                out=wbuf[b][:].rearrange("p (k f) -> p k f", k=16),
                in_=wo[:, ms * 512:(ms + 1) * 512].rearrange("(k p) f -> p k f", p=128)),
                waits=[(cP, wbuf_used[b]), (cV, wo_scaled.get(b, 0))],
                inc=cW[b], inc_k=16)
            t_wdma = cW[b].n

            def f_wsc(e, b=b):
                last = None
                for k in range(ND):
                    last = e.tensor_scalar(
                        out=wbuf[b][:, k * 512:(k + 1) * 512],
                        in0=wbuf[b][:, k * 512:(k + 1) * 512],
                        scalar1=nwc[:, k:k + 1], scalar2=None, op0=ALU.mult)
                return last
            p.task("vector", f_wsc, waits=[(cW[b], t_wdma), (cSc, t_nwc)], inc=cV)
            wo_scaled[b] = cV.n
            t_w = None
            for tt in range(NT):
                def f_mm(e, ms=ms, b=b, tt=tt):
                    last = None
                    for k in range(ND):
                        last = e.matmul(out=p_pj[tt % 2][:, 0:512],
                                        lhsT=xf[k][:, tt * 128:(tt + 1) * 128],
                                        rhs=wbuf[b][:, k * 512:(k + 1) * 512],
                                        start=(k == 0), stop=(k == ND - 1))
                    return last
                p.task("tensor", f_mm, waits=[(cV, wo_scaled[b]), (cC, t_o4f),
                                              (cC, wo_evac[tt % 2]),
                                              (cV, gn_done[-1])], inc=cP)
                t_mm = cP.n
                p.task("scalar", lambda e, ms=ms, tt=tt: e.copy(
                    out=delta[tt][:, ms * 512:(ms + 1) * 512],
                    in_=p_pj[tt % 2][:, 0:512]),
                    waits=[(cP, t_mm)], inc=cC)
                wo_evac[tt % 2] = cC.n
            wbuf_used[b] = cP.n
        t_delta = cC.n

        prev_v2 = 0
        xr_store = {0: 0, 1: 0}
        for i in range(NT):
            b = i % 2
            p.task("sync", lambda e, i=i, b=b: e.dma_start(
                out=XR[b][:], in_=x_own[i * 128:(i + 1) * 128, :]),
                waits=[(cO[b], xr_store[b]), (cC, t_delta if i < 2 else 0)],
                inc=cXR[b], inc_k=16)
            t_ldx = cXR[b].n
            tv = emit_ln_postadd_bf16(p, delta[i], XR[b], dump, cols,
                                      waits=[(cXR[b], t_ldx), (cC, t_delta)],
                                      prev_v=prev_v2)
            prev_v2 = tv
            p.task("sync", lambda e, i=i, b=b: e.dma_start(
                out=x_out[i * 128:(i + 1) * 128, :], in_=XR[b][:]),
                waits=[(cV, tv)], inc=cO[b], inc_k=16)
            xr_store[b] = cO[b].n
        p.task("sync", lambda e: e.nop(), waits=[(cO[0], cO[0].n), (cO[1], cO[1].n)])
        if own_prog:
            p.finish()
    return nc


def emit_ln_postadd_bf16(p, y_bf, xr, dump, cols, waits=(), prev_v=0, width=D):
    cV, cC = p.ctr("v"), p.ctr("c")
    hw = width // 2
    inv = 1.0 / width

    def f_sq(e):
        e.activation(out=dump[:, 0:hw], in_=y_bf[:, 0:hw], func=AF.Square,
                     accum_out=cols[:, 1:2])
        return e.activation(out=dump[:, hw:width], in_=y_bf[:, hw:width],
                            func=AF.Square, accum_out=cols[:, 7:8])
    p.task("scalar", f_sq, waits=list(waits) + [(cV, prev_v)], inc=cC)
    t_sq = cC.n

    def f_v(e):
        e.reduce_sum(out=cols[:, 0:1], in_=y_bf[:], axis=AX.X)
        e.tensor_tensor(out=cols[:, 1:2], in0=cols[:, 1:2], in1=cols[:, 7:8],
                        op=ALU.add)
        e.tensor_scalar(out=cols[:, 2:3], in0=cols[:, 0:1], scalar1=inv,
                        scalar2=None, op0=ALU.mult)
        e.tensor_scalar(out=cols[:, 3:4], in0=cols[:, 1:2], scalar1=inv,
                        scalar2=None, op0=ALU.mult)
        e.tensor_tensor(out=cols[:, 5:6], in0=cols[:, 2:3], in1=cols[:, 2:3],
                        op=ALU.mult)
        e.tensor_tensor(out=cols[:, 3:4], in0=cols[:, 3:4], in1=cols[:, 5:6],
                        op=ALU.subtract)
        return e.tensor_scalar(out=cols[:, 3:4], in0=cols[:, 3:4], scalar1=EPS,
                               scalar2=None, op0=ALU.add)
    p.task("vector", f_v, waits=list(waits) + [(cC, t_sq)], inc=cV)
    t_var = cV.n
    p.task("scalar", lambda e: e.activation(out=cols[:, 6:7], in_=cols[:, 3:4],
                                            func=AF.Sqrt),
           waits=[(cV, t_var)], inc=cC)
    t_std = cC.n

    def f_n(e):
        e.reciprocal(out=cols[:, 4:5], in_=cols[:, 6:7])
        e.tensor_scalar(out=dump[:], in0=y_bf[:],
                        scalar1=cols[:, 2:3], scalar2=cols[:, 4:5],
                        op0=ALU.subtract, op1=ALU.mult)
        return e.tensor_tensor(out=xr[:], in0=xr[:], in1=dump[:], op=ALU.add)
    p.task("vector", f_n, waits=[(cC, t_std)], inc=cV)
    return cV.n


NQG = 4
NHD = 8
DH = 64
DKV = 512
SEQ = 2048
NTT = SEQ // 128
HGRP = 4


def build_l1a(nc, stage=99, p=None, io=None, gate=()):
    x_own = io["x_own"] if io is not None else nc.dram_tensor("x_own", [T, D], F32, kind="ExternalInput").ap()
    x_oth = io["x_oth"] if io is not None else nc.dram_tensor("x_oth", [T, D], F32, kind="ExternalInput").ap()
    wqt = io["wqt"] if io is not None else nc.dram_tensor("wqt", [D, D], F32, kind="ExternalInput").ap()
    wkt = io["wkt"] if io is not None else nc.dram_tensor("wkt", [D, DKV], F32, kind="ExternalInput").ap()
    wvt = io["wvt"] if io is not None else nc.dram_tensor("wvt", [D, DKV], F32, kind="ExternalInput").ap()
    mem = io["mem"] if io is not None else nc.dram_tensor("mem", [DKV, DKV], F32, kind="ExternalInput").ap()
    mn_c = io["mn_c"] if io is not None else nc.dram_tensor("mn_c", [128, 4], F32, kind="ExternalInput").ap()
    mw = io["mw"] if io is not None else nc.dram_tensor("mw", [1, 1], F32, kind="ExternalInput").ap()
    x_out = io["x_out"] if io is not None else nc.dram_tensor("x_out", [T, D], F32, kind="ExternalOutput").ap()

    es = contextlib.ExitStack()
    with es:
        U = _uid()
        sb = lambda name, shape, dt=BF16: es.enter_context(nc.sbuf_tensor(U + name, shape, dt))
        ps = lambda name, shape: es.enter_context(nc.psum_tensor(U + name, shape, F32))
        XR = [sb(f"XR{b}", [128, D], F32) for b in range(2)]
        q_fm = [sb(f"qfm{c}", [128, T]) for c in range(ND)]
        K_tm = sb("K_tm", [128, NTT * 512])
        V_bf = sb("V_bf", [128, NTT * 520])
        q_s = [sb(f"q_s{b}", [64, T]) for b in range(2)]
        expT = [sb(f"expT{b}", [128, 512]) for b in range(2)]
        amem_sb = [sb(f"amem{c}", [128, T]) for c in range(2)]
        atmp = sb("atmp", [128, 128], F32)
        mem_sb = sb("mem_sb", [128, 4 * 512])
        mnc_sb = sb("mnc_sb", [128, 4], F32)
        mnc_bf = sb("mnc_bf", [128, 4])
        wcols = sb("wcols", [128, 4], F32)
        mwrow = sb("mwrow", [1, 2], F32)
        dnr = sb("dnr", [1, T], F32)
        rcol = sb("rcol", [128, 4], F32)
        ident = sb("ident", [128, 128])
        one1 = sb("one1", [1, 128])
        one1f = sb("one1f", [1, 128], F32)
        onef = sb("onef", [1, 1], F32)
        dump = sb("dump", [128, D])
        dump2 = sb("dump2", [128, D])
        cols = sb("cols", [128, 8], F32)
        wbuf = [sb(f"wbuf{b}", [128, 16 * 512]) for b in range(2)]
        p_tr = ps("p_tr", [128, 512])
        p_pj = [ps(f"p_pj{b}", [128, 512]) for b in range(2)]
        p_sc = [ps(f"p_sc{b}", [128, 512]) for b in range(2)]
        p_dn = ps("p_dn", [1, T])
        p_pv = ps("p_pv", [128, 260])
        xn_es = contextlib.ExitStack()
        xf = [xn_es.enter_context(nc.sbuf_tensor(U + f"xf{c}", [128, 2 * T], BF16))
              for c in range(ND)]

        own_prog = p is None
        if own_prog:
            p = Prog(nc)
        p.default_waits = list(gate)
        cV, cC, cP = (p.ctr(n) for n in ("v", "c", "pe"))
        cQ = p.ctr("q")
        cG = p.ctr("g")
        cSc = p.ctr("sc")
        cXR = [p.ctr("xr0"), p.ctr("xr1")]
        cW = [p.ctr("w0"), p.ctr("w1")]
        cO = [p.ctr("o0"), p.ctr("o1")]

        t_id = emit_identity(p, ident, one1)
        p.task("gpsimd", lambda e: e.memset(one1f[:], 1.0), inc=cQ)
        p.task("gpsimd", lambda e: e.memset(onef[:], 1.0), inc=cQ)
        p.task("gpsimd", lambda e: e.memset(
            V_bf[:].rearrange("p (tt h s) -> p (tt h) s", tt=NTT, h=8)[:, :, 64:65],
            1.0), inc=cQ)
        t_setup = (cQ, cQ.n)

        p.task("gpsimd", lambda e: e.dma_start(
            out=mem_sb[:].rearrange("p (k f) -> p k f", k=4),
            in_=mem.rearrange("(k p) f -> p k f", p=128)), inc=cG, inc_k=16)
        t_mem = cG.n
        p.task("sync", lambda e: e.dma_start(out=mnc_sb[:], in_=mn_c), inc=cSc, inc_k=16)
        p.task("sync", lambda e: e.dma_start(out=mwrow[:, 0:1], in_=mw), inc=cSc, inc_k=16)
        t_mw = cSc.n
        p.task("vector", lambda e: e.tensor_copy(out=mnc_bf[:], in_=mnc_sb[:]),
               waits=[(cSc, t_mw)], inc=cV)
        t_mnb = cV.n
        p.task("scalar", lambda e: e.activation(out=mwrow[:, 1:2], in_=mwrow[:, 0:1],
                                                func=AF.Sigmoid),
               waits=[(cSc, t_mw)], inc=cC)
        t_sig = cC.n
        p.task("tensor", lambda e: e.matmul(out=p_tr[:, 0:1], lhsT=one1f[:],
                                            rhs=mwrow[:, 1:2], start=True, stop=True),
               waits=[(cC, t_sig), t_setup], inc=cP)
        p.task("scalar", lambda e: e.copy(out=wcols[:, 0:1], in_=p_tr[:, 0:1]),
               waits=[(cP, cP.n)], inc=cC)
        t_w1 = cC.n
        p._tr_evac = t_w1
        p.task("vector", lambda e: e.tensor_scalar(
            out=wcols[:, 1:2], in0=wcols[:, 0:1], scalar1=1.0, scalar2=-1.0,
            op0=ALU.subtract, op1=ALU.mult), waits=[(cC, t_w1)], inc=cV)
        t_wc = cV.n

        prev_v = [0]
        xr_free = {0: 0, 1: 0}
        t_lastpe = [0]

        def half_ln(x_dram, half):
            for i in range(NT):
                b = i % 2
                p.task("sync", lambda e, i=i, b=b, x_dram=x_dram: e.dma_start(
                    out=XR[b][:], in_=x_dram[i * 128:(i + 1) * 128, :]),
                    waits=[(cV, xr_free[b])], inc=cXR[b], inc_k=16)
                t_ld = cXR[b].n
                pv = emit_ln_chunk(p, XR[b], dump, dump2, cols,
                                   waits=[(cXR[b], t_ld), (cP, t_lastpe[0])],
                                   prev_v=prev_v[0])
                prev_v[0] = pv
                xr_free[b] = pv
                tiles = [(dump[:, c * 128:(c + 1) * 128],
                          xf[c][:, half * T + i * 128: half * T + (i + 1) * 128])
                         for c in range(ND)]
                emit_tm_to_fm(p, tiles, ident, p_tr, waits=[(cV, pv)], t_ident=t_id)
                t_lastpe[0] = p.ctrs["pe"].n

        half_ln(x_own, 0)
        half_ln(x_oth, 1)
        t_xf = p.ctrs["c"].n

        wbuf_used = SharedCell()
        pj_evac = {0: 0, 1: 0}

        def stream_w(w_ap, col0, width, b):
            p.task("gpsimd", lambda e, w_ap=w_ap, col0=col0, width=width, b=b:
                   e.dma_start(
                       out=wbuf[b][:, 0:16 * width].rearrange("p (k f) -> p k f", k=16),
                       in_=w_ap[:, col0:col0 + width].rearrange("(k p) f -> p k f", p=128)),
                   waits=[(cP, wbuf_used[b])], inc=cW[b], inc_k=16)
            return cW[b].n

        t_wk = stream_w(wkt, 0, 512, 0)
        for tt in range(NTT):
            def f_mm(e, tt=tt, t_wk=t_wk):
                last = None
                for k in range(ND):
                    last = e.matmul(out=p_pj[tt % 2][:, 0:512],
                                    lhsT=xf[k][:, tt * 128:(tt + 1) * 128],
                                    rhs=wbuf[0][:, k * 512:(k + 1) * 512],
                                    start=(k == 0), stop=(k == ND - 1))
                return last
            p.task("tensor", f_mm, waits=[(cW[0], t_wk),
                                          (cC, max(pj_evac[tt % 2], t_xf))], inc=cP)
            t_mm = cP.n
            p.task("scalar", lambda e, tt=tt: e.copy(
                out=K_tm[:, tt * 512:(tt + 1) * 512], in_=p_pj[tt % 2][:, 0:512]),
                waits=[(cP, t_mm)], inc=cC)
            pj_evac[tt % 2] = cC.n
        wbuf_used[0] = cP.n

        t_wv = stream_w(wvt, 0, 512, 1)
        for tt in range(NTT):
            def f_mm(e, tt=tt, t_wv=t_wv):
                last = None
                for k in range(ND):
                    last = e.matmul(out=p_pj[tt % 2][:, 0:512],
                                    lhsT=xf[k][:, tt * 128:(tt + 1) * 128],
                                    rhs=wbuf[1][:, k * 512:(k + 1) * 512],
                                    start=(k == 0), stop=(k == ND - 1))
                return last
            p.task("tensor", f_mm, waits=[(cW[1], t_wv),
                                          (cC, max(pj_evac[tt % 2], t_xf))], inc=cP)
            t_mm = cP.n

            def f_ev(e, tt=tt):
                return e.copy(
                    out=V_bf[:, tt * 520:(tt + 1) * 520].rearrange(
                        "p (h s) -> p h s", h=8)[:, :, 0:64],
                    in_=p_pj[tt % 2][:, 0:512].rearrange("p (h s) -> p h s", h=8))
            p.task("scalar", f_ev, waits=[(cP, t_mm)], inc=cC)
            pj_evac[tt % 2] = cC.n
        wbuf_used[1] = cP.n

        for ms in range(4):
            b = ms % 2
            t_wq = stream_w(wqt, ms * 512, 512, b)
            for ec in range(4):
                c = ms * 4 + ec
                for tb in range(2):
                    def f_mm(e, c=c, ec=ec, b=b, tb=tb, t_wq=t_wq):
                        last = None
                        for k in range(ND):
                            last = e.matmul(
                                out=p_pj[tb % 2][:, 0:512],
                                lhsT=wbuf[b][:, k * 512 + ec * 128:
                                             k * 512 + (ec + 1) * 128],
                                rhs=xf[k][:, tb * 512:(tb + 1) * 512],
                                start=(k == 0), stop=(k == ND - 1))
                        return last
                    p.task("tensor", f_mm, waits=[(cW[b], t_wq),
                                                  (cC, max(pj_evac[tb % 2], t_xf))],
                           inc=cP)
                    t_mm = cP.n
                    p.task("scalar", lambda e, c=c, tb=tb: e.copy(
                        out=q_fm[c][:, tb * 512:(tb + 1) * 512],
                        in_=p_pj[tb % 2][:, 0:512]),
                        waits=[(cP, t_mm)], inc=cC)
                    pj_evac[tb % 2] = cC.n
            wbuf_used[b] = cP.n
        t_qkv = cC.n

        def dbg_store(aps):
            cOg = p.ctr("og")
            for i, ap2 in enumerate(aps):
                sh = ap2.shape
                p.task("gpsimd", lambda e, i=i, ap2=ap2, sh=sh: e.dma_start(
                    out=x_out[i * 128:i * 128 + sh[0], 0:sh[1]], in_=ap2),
                    waits=[(cV, cV.n), (cC, cC.n), (cP, cP.n)], inc=cOg, inc_k=16)
            p.task("sync", lambda e: e.nop(), waits=[(cOg, cOg.n)])
            if own_prog:
                p.finish()

        if stage in (1, 2, 3):
            xn_es.close()
            if stage == 1:
                dbg_store([K_tm[:, j * 2048:(j + 1) * 2048] for j in range(4)])
            elif stage == 2:
                dbg_store([V_bf[:, j * 520:j * 520 + 520] for j in range(4)])
            else:
                dbg_store([q_fm[c][:] for c in range(8)])
            return

        xn_es.close()
        ADO = [sb(f"ADO{i}", [128, D]) for i in range(NT)]
        K_F = [sb(f"KF{j}", [64, SEQ]) for j in range(HGRP)]

        cQS = [p.ctr("qsd0"), p.ctr("qsd1")]
        ktr_evac = [0]
        exp_done = {0: 0, 1: 0}
        expT_free = {0: 0, 1: 0}
        pv_free = [0]

        for hg in range(2):
            for j in range(HGRP):
                h = hg * HGRP + j
                for tt in range(NTT):
                    def f_tr(e, h=h, tt=tt):
                        return e.matmul(out=p_tr[0:64, 0:128],
                                        lhsT=K_tm[:, tt * 512 + (h % 8) * 64:
                                                  tt * 512 + (h % 8) * 64 + 64],
                                        rhs=ident[:], start=True, stop=True)
                    p.task("tensor", f_tr, waits=[(cC, max(t_qkv, ktr_evac[0])),
                                                  (cV, pv_free[0])],
                           inc=cP)
                    t_trm = cP.n
                    p.task("scalar", lambda e, j=j, tt=tt: e.copy(
                        out=K_F[j][:, tt * 128:(tt + 1) * 128],
                        in_=p_tr[0:64, 0:128]),
                        waits=[(cP, t_trm)], inc=cC)
                    ktr_evac[0] = cC.n
            t_kf = cC.n
            combos = [(qg, hg * HGRP + j) for qg in range(NQG) for j in range(HGRP)]

            def sched_qs(ci):
                qg, h = combos[ci]
                ec = (qg * 512 + h * 64) // 128
                off = (qg * 512 + h * 64) % 128
                qb = ci % 2
                p.task("gpsimd", lambda e, ec=ec, off=off, qb=qb: e.dma_start(
                    out=q_s[qb][:], in_=q_fm[ec][off:off + 64, :]),
                    waits=[(cC, t_qkv), (cP, qs_used[qb])], inc=cQS[qb], inc_k=16)
                qs_t[ci] = cQS[qb].n

            qs_t = {}
            qs_used = {0: cP.n, 1: cP.n}
            sched_qs(0)
            for ci in range(len(combos)):
                qg, h = combos[ci]
                j = h - hg * HGRP
                qb = ci % 2
                if ci + 1 < len(combos):
                    sched_qs(ci + 1)
                t_qs = qs_t[ci]
                if True:
                    for sb_i in range(2):
                        t_scm = {}
                        t_exp_d = {}

                        def sched_sc(tt):
                            pb = tt % 2

                            def f_sc(e, j=j, sb_i=sb_i, tt=tt, pb=pb, qb=qb):
                                return e.matmul(
                                    out=p_sc[pb][:, 0:512],
                                    lhsT=K_F[j][:, tt * 128:(tt + 1) * 128],
                                    rhs=q_s[qb][:, sb_i * 512:(sb_i + 1) * 512],
                                    start=True, stop=True)
                            p.task("tensor", f_sc,
                                   waits=[(cQS[qb], t_qs),
                                          (cC, max(t_kf, exp_done[pb])),
                                          (cV, pv_free[0] if tt == 0 else 0)],
                                   inc=cP)
                            t_scm[tt] = cP.n

                        def sched_exp(tt):
                            pb = tt % 2

                            def f_exp(e, pb=pb):
                                return e.activation(out=expT[pb][:],
                                                    in_=p_sc[pb][:, 0:512],
                                                    func=AF.Exp, scale=0.125)
                            p.task("scalar", f_exp,
                                   waits=[(cP, max(t_scm[tt], expT_free[pb]))],
                                   inc=cC)
                            exp_done[pb] = cC.n
                            t_exp_d[tt] = cC.n

                        sched_sc(0)
                        sched_exp(0)
                        for tt in range(NTT):
                            pb = tt % 2
                            if tt + 1 < NTT:
                                sched_sc(tt + 1)
                                sched_exp(tt + 1)

                            def f_pv(e, tt=tt, pb=pb, h=h):
                                last = None
                                pv_banks = [p_pv, p_pj[0], p_pj[1], p_tr]
                                for sj in range(4):
                                    last = e.matmul(
                                        out=pv_banks[sj][:, 0:65],
                                        lhsT=expT[pb][:, sj * 128:(sj + 1) * 128],
                                        rhs=V_bf[:, tt * 520 + (h % 8) * 65:
                                                 tt * 520 + (h % 8) * 65 + 65],
                                        start=(tt == 0), stop=(tt == NTT - 1),
                                        skip_group_check=True)
                                return last
                            p.task("tensor", f_pv, waits=[(cC, t_exp_d[tt]), t_setup],
                                   inc=cP)
                            expT_free[pb] = cP.n
                        t_pvm = cP.n

                        def f_ev(e, qg=qg, h=h, sb_i=sb_i):
                            last = None
                            pv_banks = [p_pv, p_pj[0], p_pj[1], p_tr]
                            for sj in range(4):
                                e.reciprocal(out=rcol[:, sj:sj + 1],
                                             in_=pv_banks[sj][:, 64:65])
                            for sj in range(4):
                                st = sb_i * 4 + sj
                                last = e.tensor_scalar(
                                    out=ADO[st][:, qg * 512 + h * 64:
                                                qg * 512 + (h + 1) * 64],
                                    in0=pv_banks[sj][:, 0:64],
                                    scalar1=rcol[:, sj:sj + 1], scalar2=None,
                                    op0=ALU.mult)
                            return last
                        p.task("vector", f_ev, waits=[(cP, t_pvm)], inc=cV)
                        pv_free[0] = cV.n
                qs_used[qb] = cP.n
        t_attn_v = cV.n
        if stage == 4:
            dbg_store([ADO[i][:] for i in range(NT)]); return

        for i in range(NT):
            p.task("vector", lambda e, i=i: e.tensor_scalar(
                out=ADO[i][:], in0=ADO[i][:], scalar1=wcols[:, 1:2], scalar2=None,
                op0=ALU.mult), waits=[(cV, max(t_attn_v, t_wc))], inc=cV)
        t_ado_w = cV.n

        elu_done = 0
        for c2 in range(ND):
            def f_e1(e, c2=c2):
                return e.tensor_scalar(out=dump[:, 0:T], in0=q_fm[c2][:],
                                       scalar1=0.0, scalar2=None, op0=ALU.min)
            p.task("vector", f_e1, waits=[(cV, t_ado_w), (cC, elu_done)], inc=cV)
            tv1 = cV.n
            p.task("scalar", lambda e: e.activation(out=dump[:, 0:T], in_=dump[:, 0:T],
                                                    func=AF.Exp),
                   waits=[(cV, tv1)], inc=cC)
            tc1 = cC.n

            def f_e2(e, c2=c2):
                return e.scalar_tensor_tensor(out=q_fm[c2][:], in0=dump[:, 0:T],
                                              scalar=1.0, in1=q_fm[c2][:],
                                              op0=ALU.subtract, op1=ALU.max)
            p.task("vector", f_e2, waits=[(cC, tc1)], inc=cV)
            elu_done = tc1
        t_elu = cV.n

        amem_evac = {0: 0, 1: 0}
        dn_evac = [0]
        cmb_done = [0]
        for qg in range(NQG):
            def f_dn(e, qg=qg):
                last = None
                for half in range(2):
                    for k in range(4):
                        last = e.matmul(
                            out=p_dn[0:1, half * 512:(half + 1) * 512],
                            lhsT=mnc_bf[:, k:k + 1],
                            rhs=q_fm[qg * 4 + k][:, half * 512:(half + 1) * 512],
                            start=(k == 0), stop=(k == 3))
                return last
            p.task("tensor", f_dn, waits=[(cV, max(t_elu, t_mnb)),
                                          (cV, dn_evac[0])], inc=cP)
            t_dnm = cP.n
            p.task("vector", lambda e: e.tensor_copy(out=dnr[:], in_=p_dn[:]),
                   waits=[(cP, t_dnm)], inc=cV)
            dn_evac[0] = cV.n
            t_dnr = cV.n

            for e2g in range(2):
                for e2i in range(2):
                    e2c = e2g * 2 + e2i
                    pb = e2i
                    for half in range(2):
                        def f_amh(e, qg=qg, e2c=e2c, pb=pb, half=half):
                            last = None
                            for k in range(4):
                                last = e.matmul(
                                    out=p_pj[pb][:, 0:512],
                                    lhsT=mem_sb[:, k * 512 + e2c * 128:
                                                k * 512 + (e2c + 1) * 128],
                                    rhs=q_fm[qg * 4 + k][:, half * 512:(half + 1) * 512],
                                    start=(k == 0), stop=(k == 3))
                            return last
                        p.task("tensor", f_amh, waits=[(cV, t_elu), (cG, t_mem),
                                                       (cC, amem_evac[pb]),
                                                       (cV, cmb_done[0])], inc=cP)
                        t_amm = cP.n
                        p.task("scalar", lambda e, e2i=e2i, half=half, pb=pb: e.copy(
                            out=amem_sb[e2i][:, half * 512:(half + 1) * 512],
                            in_=p_pj[pb][:, 0:512]),
                            waits=[(cP, t_amm)], inc=cC)
                        amem_evac[pb] = cC.n
                t_amem = cC.n

                for st in range(NT):
                    def f_dc(e, st=st):
                        return e.matmul(out=p_tr[:, 256:257],
                                        lhsT=dnr[0:1, st * 128:(st + 1) * 128],
                                        rhs=onef[:], start=True, stop=True)
                    p.task("tensor", f_dc, waits=[(cV, t_dnr), t_setup,
                                                  (cV, cmb_done[0])], inc=cP)
                    t_dcm = cP.n

                    p.task("vector", lambda e: e.reciprocal(
                        out=rcol[:, 1:2], in_=p_tr[:, 256:257]),
                        waits=[(cP, t_dcm)], inc=cV)
                    t_rc0 = cV.n
                    p.task("vector", lambda e: e.tensor_tensor(
                        out=rcol[:, 2:3], in0=rcol[:, 1:2], in1=wcols[:, 0:1],
                        op=ALU.mult), waits=[(cV, max(t_rc0, t_wc))], inc=cV)
                    t_rc = cV.n
                    for e2i in range(2):
                        e2c = e2g * 2 + e2i

                        def f_tr2(e, e2i=e2i, st=st):
                            return e.matmul(out=p_tr[:, 0:128],
                                            lhsT=amem_sb[e2i][:, st * 128:(st + 1) * 128],
                                            rhs=ident[:], start=True, stop=True)
                        p.task("tensor", f_tr2, waits=[(cC, t_amem),
                                                       (cV, max(cmb_done[0], t_rc))],
                               inc=cP)
                        t_trm = cP.n

                        def f_cmb(e, qg=qg, e2c=e2c, st=st):
                            e.tensor_scalar(out=atmp[:], in0=p_tr[:, 0:128],
                                            scalar1=rcol[:, 2:3], scalar2=None,
                                            op0=ALU.mult)
                            dst = ADO[st][:, qg * 512 + e2c * 128:
                                          qg * 512 + (e2c + 1) * 128]
                            return e.tensor_tensor(out=dst, in0=dst, in1=atmp[:],
                                                   op=ALU.add)
                        p.task("vector", f_cmb, waits=[(cP, t_trm), (cV, t_rc),
                                                       (cV, t_ado_w)], inc=cV)
                        cmb_done[0] = cV.n

        t_all_v = cV.n
        if stage == 5:
            dbg_store([ADO[i][:] for i in range(NT)]); return
        prev_v2 = 0
        xr_store = {0: 0, 1: 0}
        for i in range(NT):
            b = i % 2
            p.task("sync", lambda e, i=i, b=b: e.dma_start(
                out=XR[b][:], in_=x_own[i * 128:(i + 1) * 128, :]),
                waits=[(cO[b], xr_store[b]), (cV, t_all_v if i < 2 else 0)],
                inc=cXR[b], inc_k=16)
            t_ld = cXR[b].n
            tv = emit_ln_postadd_bf16(p, ADO[i], XR[b], dump, cols,
                                      waits=[(cXR[b], t_ld), (cV, t_all_v)],
                                      prev_v=prev_v2)
            prev_v2 = tv
            p.task("sync", lambda e, i=i, b=b: e.dma_start(
                out=x_out[i * 128:(i + 1) * 128, :], in_=XR[b][:]),
                waits=[(cV, tv)], inc=cO[b], inc_k=16)
            xr_store[b] = cO[b].n
        p.task("sync", lambda e: e.nop(), waits=[(cO[0], cO[0].n), (cO[1], cO[1].n)])
        if own_prog:
            p.finish()
    return nc



def _shards(x):
    v = np.asarray(x).reshape(4, 2, 1024, 2048)
    return [v[c // 2, c % 2] for c in range(8)]


def kernel_multi(**inp):
    x = f32c(inp["L0_attn_x"])
    xs = _shards(x)
    zeros = np.zeros((1024, 2048), np.float32)

    nc = bass.Bass("TRN2", target_bir_lowering=False)
    build_l0a(nc)
    com = dict(
        wq=f32c(inp["L0_attn_Wq"]), wk=f32c(inp["L0_attn_Wk"]),
        wv=f32c(inp["L0_attn_Wv"]), wo=f32c(inp["L0_attn_Wo"]),
        wg1=f32c(inp["L0_attn_Wg1"]), wg2=f32c(inp["L0_attn_Wg2"]),
        nw_c=f32c(np.asarray(inp["L0_attn_norm_w"]).reshape(16, 128).T),
        ld_r=f32c(np.asarray(inp["L0_attn_log_decay"]).reshape(1, -1)),
    )
    in_maps = []
    for c in range(8):
        pre = zeros if c % 2 == 0 else xs[c - 1]
        in_maps.append(dict(x_own=xs[c], x_pre=f32c(pre), **com))
    res = run_bass_kernel_spmd(nc, in_maps, core_ids=list(range(8)))
    ys = [res.results[c]["x_out"] for c in range(8)]

    nc = bass.Bass("TRN2", target_bir_lowering=False)
    build_l0f(nc)
    com = dict(
        wup_q=u8c(inp["L0_ffn_w_up_q"]),
        wup_s=f32c(np.asarray(inp["L0_ffn_w_up_scale"]).reshape(64, 128, 32)
                   .transpose(1, 0, 2).reshape(128, -1)),
        bup_c=f32c(np.asarray(inp["L0_ffn_b_up"]).reshape(64, 128).T),
        la_t=f32c(np.asarray(inp["L0_ffn_w_up_lora_a"]).reshape(16, 128, 16)
                  .transpose(1, 0, 2).reshape(128, -1)),
        lora_ub=f32c(inp["L0_ffn_w_up_lora_b"]),
        wdn_q=u8c(inp["L0_ffn_w_down_q"]),
        wdn_s=f32c(np.asarray(inp["L0_ffn_w_down_scale"]).reshape(16, 128, 128)
                   .transpose(1, 0, 2).reshape(128, -1)),
        b_dn=f32c(np.asarray(inp["L0_ffn_b_down"]).reshape(1, -1)),
        da_t=f32c(np.asarray(inp["L0_ffn_w_down_lora_a"]).reshape(64, 128, 16)
                  .transpose(1, 0, 2).reshape(128, -1)),
        lora_db=f32c(inp["L0_ffn_w_down_lora_b"]),
    )
    res = run_bass_kernel_spmd(nc, [dict(x_in=f32c(ys[c]), **com) for c in range(8)],
                               core_ids=list(range(8)))
    zs = [res.results[c]["x_out"] for c in range(8)]

    nc = bass.Bass("TRN2", target_bir_lowering=False)
    build_l1a(nc)
    com = dict(
        wqt=f32c(np.asarray(inp["L1_attn_Wq"]).T),
        wkt=f32c(np.asarray(inp["L1_attn_Wk"]).T),
        wvt=f32c(np.asarray(inp["L1_attn_Wv"]).T),
        mem=f32c(inp["L1_attn_memory"]),
        mn_c=f32c(np.asarray(inp["L1_attn_memory_norm"]).reshape(4, 128).T),
        mw=f32c(np.asarray(inp["L1_attn_memory_weight"]).reshape(1, 1)),
    )
    in_maps = []
    for c in range(8):
        oth = zs[c + 1] if c % 2 == 0 else zs[c - 1]
        in_maps.append(dict(x_own=f32c(zs[c]), x_oth=f32c(oth), **com))
    res = run_bass_kernel_spmd(nc, in_maps, core_ids=list(range(8)))
    us = [res.results[c]["x_out"] for c in range(8)]

    nc = bass.Bass("TRN2", target_bir_lowering=False)
    build_l1f(nc)
    com = dict(
        gw_q=u8c(inp["L1_ffn_gate_w"]),
        uw_q=u8c(inp["L1_ffn_up_w"]),
        dw_q=u8c(inp["L1_ffn_down_w"]),
        gs_c=f32c(np.asarray(inp["L1_ffn_gate_scale"]).reshape(64, 128).T),
        us_c=f32c(np.asarray(inp["L1_ffn_up_scale"]).reshape(64, 128).T),
        ds_c=f32c(np.asarray(inp["L1_ffn_down_scale"]).reshape(16, 128).T),
    )
    res = run_bass_kernel_spmd(nc, [dict(x_in=f32c(us[c]), **com) for c in range(8)],
                               core_ids=list(range(8)))
    outs = [res.results[c]["x_out"] for c in range(8)]

    out = np.stack(outs, axis=0).reshape(4, 2, 1024, 2048).reshape(4, 2048, 2048)
    return np.ascontiguousarray(out, dtype=np.float32)



def build_fused(nc):
    ext = lambda n, sh, dt=F32: nc.dram_tensor(n, sh, dt, kind="ExternalInput").ap()
    x_own = ext("x_own", [T, D])
    x_par = ext("x_par", [T, D])
    xpre_own = ext("xpre_own", [T, D])
    xpre_par = ext("xpre_par", [T, D])
    l0a_w = dict(wq=ext("wq", [D, D]), wk=ext("wk", [D, D]), wv=ext("wv", [D, D]),
                 wo=ext("wo", [D, D]), wg1=ext("wg1", [D, 128]),
                 wg2=ext("wg2", [128, D]), nw_c=ext("nw_c", [128, NHEAD]),
                 ld_r=ext("ld_r", [1, NHEAD]))
    l0f_w = dict(wup_q=ext("wup_q", [H, D], U8), wup_s=ext("wup_s", [128, NH * 32]),
                 bup_c=ext("bup_c", [128, NH]), la_t=ext("la_t", [128, ND * 16]),
                 lora_ub=ext("lora_ub", [16, H]), wdn_q=ext("wdn_q", [D, H], U8),
                 wdn_s=ext("wdn_s", [128, ND * 128]), b_dn=ext("b_dn", [1, D]),
                 da_t=ext("da_t", [128, NH * 16]), lora_db=ext("lora_db", [16, D]))
    l1a_w = dict(wqt=ext("wqt", [D, D]), wkt=ext("wkt", [D, DKV]),
                 wvt=ext("wvt", [D, DKV]), mem=ext("mem", [DKV, DKV]),
                 mn_c=ext("mn_c", [128, 4]), mw=ext("mw", [1, 1]))
    l1f_w = dict(gw_q=ext("gw_q", [H, D], U8), uw_q=ext("uw_q", [H, D], U8),
                 dw_q=ext("dw_q", [D, H], U8), gs_c=ext("gs_c", [128, NH]),
                 us_c=ext("us_c", [128, NH]), ds_c=ext("ds_c", [128, ND]))
    x_out = nc.dram_tensor("x_out", [T, D], F32, kind="ExternalOutput").ap()
    scr = {n: nc.dram_tensor(n, [T, D], F32).ap()
           for n in ("y_own", "y_par", "z_own", "z_par", "u_own")}

    p = Prog(nc)

    def gate():
        return [(c, c.n) for c in p.ctrs.values()]

    build_l0a(nc, p=p, io=dict(x_own=x_own, x_pre=xpre_own,
                               x_out=scr["y_own"], **l0a_w), gate=[])
    build_l0a(nc, p=p, io=dict(x_own=x_par, x_pre=xpre_par,
                               x_out=scr["y_par"], **l0a_w), gate=gate())
    build_l0f(nc, p=p, io=dict(x_in=scr["y_own"], x_out=scr["z_own"], **l0f_w),
              gate=gate())
    build_l0f(nc, p=p, io=dict(x_in=scr["y_par"], x_out=scr["z_par"], **l0f_w),
              gate=gate())
    build_l1a(nc, p=p, io=dict(x_own=scr["z_own"], x_oth=scr["z_par"],
                               x_out=scr["u_own"], **l1a_w), gate=gate())
    build_l1f(nc, p=p, io=dict(x_in=scr["u_own"], x_out=x_out, **l1f_w),
              gate=gate())
    p.finish()
    return nc


def _fused_inmaps(inp):
    x = f32c(inp["L0_attn_x"])
    xs = _shards(x)
    zeros = np.zeros((1024, 2048), np.float32)
    com = dict(
        wq=f32c(inp["L0_attn_Wq"]), wk=f32c(inp["L0_attn_Wk"]),
        wv=f32c(inp["L0_attn_Wv"]), wo=f32c(inp["L0_attn_Wo"]),
        wg1=f32c(inp["L0_attn_Wg1"]), wg2=f32c(inp["L0_attn_Wg2"]),
        nw_c=f32c(np.asarray(inp["L0_attn_norm_w"]).reshape(16, 128).T),
        ld_r=f32c(np.asarray(inp["L0_attn_log_decay"]).reshape(1, -1)),
        wup_q=u8c(inp["L0_ffn_w_up_q"]),
        wup_s=f32c(np.asarray(inp["L0_ffn_w_up_scale"]).reshape(64, 128, 32)
                   .transpose(1, 0, 2).reshape(128, -1)),
        bup_c=f32c(np.asarray(inp["L0_ffn_b_up"]).reshape(64, 128).T),
        la_t=f32c(np.asarray(inp["L0_ffn_w_up_lora_a"]).reshape(16, 128, 16)
                  .transpose(1, 0, 2).reshape(128, -1)),
        lora_ub=f32c(inp["L0_ffn_w_up_lora_b"]),
        wdn_q=u8c(inp["L0_ffn_w_down_q"]),
        wdn_s=f32c(np.asarray(inp["L0_ffn_w_down_scale"]).reshape(16, 128, 128)
                   .transpose(1, 0, 2).reshape(128, -1)),
        b_dn=f32c(np.asarray(inp["L0_ffn_b_down"]).reshape(1, -1)),
        da_t=f32c(np.asarray(inp["L0_ffn_w_down_lora_a"]).reshape(64, 128, 16)
                  .transpose(1, 0, 2).reshape(128, -1)),
        lora_db=f32c(inp["L0_ffn_w_down_lora_b"]),
        wqt=f32c(np.asarray(inp["L1_attn_Wq"]).T),
        wkt=f32c(np.asarray(inp["L1_attn_Wk"]).T),
        wvt=f32c(np.asarray(inp["L1_attn_Wv"]).T),
        mem=f32c(inp["L1_attn_memory"]),
        mn_c=f32c(np.asarray(inp["L1_attn_memory_norm"]).reshape(4, 128).T),
        mw=f32c(np.asarray(inp["L1_attn_memory_weight"]).reshape(1, 1)),
        gw_q=u8c(inp["L1_ffn_gate_w"]),
        uw_q=u8c(inp["L1_ffn_up_w"]),
        dw_q=u8c(inp["L1_ffn_down_w"]),
        gs_c=f32c(np.asarray(inp["L1_ffn_gate_scale"]).reshape(64, 128).T),
        us_c=f32c(np.asarray(inp["L1_ffn_up_scale"]).reshape(64, 128).T),
        ds_c=f32c(np.asarray(inp["L1_ffn_down_scale"]).reshape(16, 128).T),
    )
    in_maps = []
    for c in range(8):
        h = c % 2
        par = c + 1 if h == 0 else c - 1
        in_maps.append(dict(
            x_own=xs[c], x_par=xs[par],
            xpre_own=f32c(zeros if h == 0 else xs[par]),
            xpre_par=f32c(xs[c] if h == 0 else zeros),
            **com))
    return in_maps


def kernel(**inp):
    nc = bass.Bass("TRN2", target_bir_lowering=False)
    build_fused(nc)
    res = run_bass_kernel_spmd(nc, _fused_inmaps(inp), core_ids=list(range(8)))
    outs = [res.results[c]["x_out"] for c in range(8)]
    out = np.stack(outs, axis=0).reshape(4, 2, 1024, 2048).reshape(4, 2048, 2048)
    return np.ascontiguousarray(out, dtype=np.float32)
